# version 14
# baseline (speedup 1.0000x reference)
import numpy as np
import ml_dtypes
import concourse.bass as bass
import concourse.mybir as mybir
from concourse.bass_utils import run_bass_kernel_spmd

F32 = mybir.dt.float32
BF16 = mybir.dt.bfloat16
F16 = mybir.dt.float16
AF = mybir.ActivationFunctionType
OP = mybir.AluOpType
AX = mybir.AxisListType
bf16 = ml_dtypes.bfloat16

ENGINES = ("sync", "scalar", "vector", "tensor", "gpsimd")


class Ph:

    def __init__(self, nc, name=None, pr=None):
        self.nc = nc
        self.name = name
        self.pr = pr
        self.ops = []
        if pr is not None:
            self.count = pr.scount
            self.chan_engine = pr.chan_engine
        else:
            self.count = {e: 0 for e in ENGINES}
            self.chan_engine = {}
        for e in ENGINES:
            self.count.setdefault(e, 0)
        self.last_writer = {}
        self.readers = {}
        self.my_chans = set()

    @staticmethod
    def _key(x):
        if isinstance(x, tuple):
            h, region = x
        else:
            h, region = x, None
        name = getattr(h, "name", None)
        if name is None:
            t = getattr(h, "tensor", None)
            name = getattr(t, "name", repr(h))
        return (name, region)

    def op(self, engine, fn, reads=(), writes=(), dma=False, inc=None,
           deps=(), chan=None):
        rkeys = [self._key(r) for r in reads]
        wkeys = [self._key(w) for w in writes]
        dep_toks = set(t for t in deps if t is not None)
        for k in rkeys:
            t = self.last_writer.get(k)
            if t is not None:
                dep_toks.add(t)
        for k in wkeys:
            t = self.last_writer.get(k)
            if t is not None:
                dep_toks.add(t)
            for t in self.readers.get(k, ()):
                dep_toks.add(t)
        if dma:
            assert wkeys, "dma op needs a write target"
            chan = ("dma", chan if chan is not None else wkeys[0][0])
            if chan not in self.count:
                self.count[chan] = 0
                self.chan_engine[chan] = engine
            assert self.chan_engine[chan] == engine, (chan, engine)
            self.my_chans.add(chan)
            n_inc = 16 if inc is None else inc
            self.count[chan] += n_inc
            tok = (chan, self.count[chan])
        else:
            n_inc = 1 if inc is None else inc
            self.count[engine] += n_inc
            tok = (engine, self.count[engine])
        rounded = set()
        for de, dv in dep_toks:
            if isinstance(de, tuple) and de[0] == "dma":
                rounded.add((de, self.count[de]))
            else:
                rounded.add((de, dv))
        dep_toks = {t for t in rounded if t != tok}
        self.ops.append((engine, fn, dep_toks, n_inc, tok))
        for k in rkeys:
            self.readers.setdefault(k, []).append(tok)
        for k in wkeys:
            self.last_writer[k] = tok
            self.readers[k] = []
        return tok

    def barrier_wait(self, engine, toks):
        self.ops.append((engine, None, set(t for t in toks if t), 0, None))

    def emit(self, stack):
        nc = self.nc
        if self.pr is not None:
            sems = {key: self.pr.sem(key) for key in self.count}
        else:
            sems = {}
            for key in self.count:
                nm = key if isinstance(key, str) else f"c_{key[1]}"
                sems[key] = stack.enter_context(
                    nc.semaphore(f"{self.name}_{nm}"[:60]))
        per_engine = {e: [] for e in ENGINES}
        for rec in self.ops:
            per_engine[rec[0]].append(rec)
        import contextlib
        scope = nc.named_scope(self.name) if self.name else contextlib.nullcontext()
        with scope, nc.Block(self.name) as blk:
            for e in ENGINES:
                recs = per_engine[e]
                drains = [(c, self.count[c]) for c, ce in self.chan_engine.items()
                          if ce == e and c in self.my_chans]
                if not recs and not drains:
                    continue

                def body(eng, recs=recs, e=e, drains=drains):
                    watermark = {}
                    for _, fn, deps, n_inc, tok in recs:
                        need = {}
                        for de, dv in deps:
                            need[de] = max(need.get(de, 0), dv)
                        for de in sorted(need, key=repr):
                            dv = need[de]
                            if watermark.get(de, 0) >= dv:
                                continue
                            eng.wait_ge(sems[de], dv)
                            watermark[de] = dv
                        if fn is not None:
                            ins = fn(eng)
                            if n_inc:
                                sem_key = tok[0] if tok else e
                                ins.then_inc(sems[sem_key], n_inc)
                    if self.count.get(e) and watermark.get(e, 0) < self.count[e]:
                        eng.wait_ge(sems[e], self.count[e])
                    for c, cv in drains:
                        if watermark.get(c, 0) < cv:
                            eng.wait_ge(sems[c], cv)
                getattr(blk, e)(body)
        return sems



D = 2048
DT = D // 128
NB = 2
T = 2048
NT = NB * T
RT = 512
NR = NT // RT
I0 = 5504
I1 = 5632
GU = 2 * I1
H = 16
DH = 128
HKV = 4
CH = 64
NL = 64
LNEPS = 1e-5
RMSEPS = 1e-6
NEG = -1.0e9
SCALE1 = 1.0 / float(np.sqrt(DH))


def host_prep(inputs, n_cores=8):
    f32 = np.float32

    shared = {}

    def tile_lhsT(A):
        K, M = A.shape
        return np.ascontiguousarray(
            A.reshape(K // 128, 128, M // 128, 128).transpose(2, 1, 0, 3)
            .reshape(M // 128, 128, K)).astype(np.float32)

    shared["wqT"] = tile_lhsT(inputs["L0_attn_wq"].T)
    shared["wkT"] = tile_lhsT(inputs["L0_attn_wk"].T)
    shared["wvT"] = tile_lhsT(inputs["L0_attn_wv"].T)
    shared["woT"] = tile_lhsT(inputs["L0_attn_wo"].T)

    shared["g_idxT"] = np.ascontiguousarray(inputs["L0_ffn_gate_idx"].T).astype(bf16)
    shared["u_idxT"] = np.ascontiguousarray(inputs["L0_ffn_up_idx"].T).astype(bf16)
    shared["d_idxT"] = np.ascontiguousarray(inputs["L0_ffn_down_idx"].T).astype(bf16)
    shared["lut"] = np.ascontiguousarray(inputs["L0_ffn_lut"].reshape(1, 16)).astype(f32)

    shared["Wk1"] = tile_lhsT(inputs["L1_attn_Wk"])
    shared["Wv1"] = tile_lhsT(inputs["L1_attn_Wv"])
    shared["Wg1"] = tile_lhsT(inputs["L1_attn_Wg"])
    shared["Wp1"] = tile_lhsT(inputs["L1_attn_Wp"])
    lq = np.asarray(inputs["L1_attn_latent_queries"])[0]
    shared["latq"] = np.ascontiguousarray(lq.transpose(1, 2, 0)).reshape(H, DH, NL).astype(f32)

    def unpack_T(packed):
        w = packed.astype(np.uint32)
        nib = np.zeros((w.shape[0], w.shape[1] * 8), np.uint8)
        for i in range(8):
            nib[:, i::8] = ((w >> np.uint32(4 * i)) & np.uint32(15)).astype(np.uint8)
        return np.ascontiguousarray(nib.T).astype(np.float32).astype(bf16)

    shared["gu_codesT"] = unpack_T(inputs["L1_ffn_gate_up_packed"])
    shared["dn_codesT"] = unpack_T(inputs["L1_ffn_down_packed"])
    shared["gu_scalesT"] = np.ascontiguousarray(
        np.repeat(np.asarray(inputs["L1_ffn_gate_up_scales"]), 128, axis=0))
    shared["dn_scalesT"] = np.ascontiguousarray(
        np.repeat(np.asarray(inputs["L1_ffn_down_scales"]), 128, axis=0))

    inv = 1.0 / 10000.0 ** (np.arange(0, DH, 2, dtype=np.float64) / DH)
    fr = np.arange(T, dtype=np.float64)[None, :] * inv[:, None]
    shared["cos0"] = np.concatenate([np.cos(fr), np.cos(fr)], 0).astype(f32)
    shared["sin0"] = np.concatenate([np.sin(fr), np.sin(fr)], 0).astype(f32)
    P0 = np.zeros((DH, DH), np.float32)
    for i in range(64):
        P0[i, i + 64] = -1.0
        P0[i + 64, i] = 1.0
    shared["permT0"] = np.ascontiguousarray(P0.T).astype(bf16)
    inv1 = 1.0 / 10000.0 ** (np.arange(0, 64, dtype=np.float64) * 2.0 / DH)
    ang = np.arange(128, dtype=np.float64)[None, :] * inv1[:, None]
    c1 = np.zeros((DH, 128), np.float64); s1_ = np.zeros((DH, 128), np.float64)
    c1[0::2] = np.cos(ang); c1[1::2] = np.cos(ang)
    s1_[0::2] = np.sin(ang); s1_[1::2] = np.sin(ang)
    shared["cos1"] = c1.astype(f32)
    shared["sin1"] = s1_.astype(f32)
    P1 = np.zeros((DH, DH), np.float32)
    for i in range(64):
        P1[2 * i, 2 * i + 1] = -1.0
        P1[2 * i + 1, 2 * i] = 1.0
    shared["permT1"] = np.ascontiguousarray(P1.T).astype(bf16)
    lam = np.exp2(-8.0 * np.arange(1, H + 1, dtype=np.float64) / H)
    r = np.arange(CH, dtype=np.float64)
    diagT = np.zeros((CH, H * CH), np.float32)
    qdec = np.zeros((DH, H * CH), np.float32)
    kdec = np.zeros((DH, H * CH), np.float32)
    for h in range(H):
        dd = r[None, :] - r[:, None]
        diagT[:, h * CH:(h + 1) * CH] = np.where(dd >= 0, np.exp(-lam[h] * dd), 0.0)
        qdec[:, h * CH:(h + 1) * CH] = np.exp(-lam[h] * (r + 1.0))[None, :]
        kdec[:, h * CH:(h + 1) * CH] = np.exp(-lam[h] * (CH - 1.0 - r))[None, :]
    shared["diagT"] = diagT
    shared["qdec"] = qdec
    shared["kdec"] = kdec
    inv0 = 1.0 / 10000.0 ** (np.arange(0, DH, 2, dtype=np.float64) / DH)
    cc, jj, ii = np.meshgrid(np.arange(T // CH), np.arange(CH), inv0,
                             indexing="ij")
    ang0 = (cc * CH + jj) * ii
    shared["cos0tm"] = np.ascontiguousarray(
        np.cos(ang0).transpose(1, 0, 2).reshape(CH, -1)).astype(f32)
    shared["sin0tm"] = np.ascontiguousarray(
        np.sin(ang0).transpose(1, 0, 2).reshape(CH, -1)).astype(f32)
    kdecT = np.zeros((CH, H), np.float64)
    for h in range(H):
        kdecT[:, h] = np.exp(-lam[h] * (CH - 1.0 - r))
    shared["kdecT"] = kdecT.astype(f32)
    shared["kdecTM"] = np.ascontiguousarray(
        np.repeat(kdecT.T[:, :, None], DH, axis=2).transpose(1, 0, 2)
        .reshape(CH, H * DH)).astype(f32)
    shared["blk_dec"] = np.exp(-lam * CH).astype(np.float64)
    lmask = np.where(np.arange(128)[None, :] <= np.arange(NL)[:, None], 0.0, NEG)
    shared["latmask"] = lmask.astype(f32)

    x = np.asarray(inputs["L0_attn_x"])
    def padrows(a, rows):
        out = np.zeros((rows, a.shape[1]), a.dtype)
        out[: a.shape[0]] = a
        return out
    d_pad = padrows(shared["d_idxT"], 6144)
    dn_pad = padrows(shared["dn_codesT"], 6144)
    dns_pad = padrows(shared["dn_scalesT"], 6144)
    per_core = []
    bpc = x.shape[0] // n_cores
    for c in range(n_cores):
        xc = x[c * bpc:(c + 1) * bpc]
        xfm = np.ascontiguousarray(xc.transpose(2, 0, 1).reshape(D, -1))
        pc = {"x0": xfm}
        pc["g_idxT_s"] = np.ascontiguousarray(shared["g_idxT"][c * 256:(c + 1) * 256])
        pc["u_idxT_s"] = np.ascontiguousarray(shared["u_idxT"][c * 256:(c + 1) * 256])
        pc["d_idxT_s"] = np.ascontiguousarray(d_pad[c * 768:(c + 1) * 768])
        pc["gu_codesT_s"] = np.ascontiguousarray(shared["gu_codesT"][c * 256:(c + 1) * 256])
        pc["gu_scalesT_s"] = np.ascontiguousarray(shared["gu_scalesT"][c * 256:(c + 1) * 256])
        pc["dn_codesT_s"] = np.ascontiguousarray(dn_pad[c * 768:(c + 1) * 768])
        pc["dn_scalesT_s"] = np.ascontiguousarray(dns_pad[c * 768:(c + 1) * 768])
        per_core.append(pc)
    return shared, per_core


def gather_out(results):
    outs = []
    for r in results:
        xfm = np.asarray(r["xout"])
        outs.append(xfm.reshape(D, NB, T).transpose(1, 2, 0))
    return np.concatenate(outs, 0).astype(np.float32)


from contextlib import ExitStack, contextmanager


class Prog:

    def __init__(self, nc, stack):
        self.nc = nc
        self.es = stack
        self.stack = stack
        self.t = {}
        self.scount = {}
        self.chan_engine = {}
        self._sems = {}

    def sem(self, key):
        if key not in self._sems:
            nm = key if isinstance(key, str) else f"c_{key[1]}"
            self._sems[key] = self.es.enter_context(
                self.nc.semaphore(str(nm)[:60]))
        return self._sems[key]

    @contextmanager
    def scope(self):
        old = self.stack
        with ExitStack() as s:
            self.stack = s
            try:
                yield
            finally:
                self.stack = old

    input_names = None

    def dram_in(self, name, shape, dt):
        self.t[name] = self.nc.dram_tensor(name, shape, dt, kind="ExternalInput")
        if Prog.input_names is not None:
            Prog.input_names.append(name)
        return self.t[name]

    def dram_out(self, name, shape, dt):
        self.t[name] = self.nc.dram_tensor(name, shape, dt, kind="ExternalOutput")
        return self.t[name]

    def dram_tmp(self, name, shape, dt, shared=False):
        kw = {"addr_space": "Shared"} if shared else {}
        self.t[name] = self.nc.dram_tensor(name, shape, dt, **kw)
        return self.t[name]

    def sbuf(self, name, shape, dt, persistent=False):
        st = self.es if persistent else self.stack
        h = st.enter_context(self.nc.sbuf_tensor(name, shape, dt))
        self.t[name] = h
        return h

    def psum(self, name, shape, dt=F32):
        h = self.stack.enter_context(self.nc.psum_tensor(name, shape, dt))
        self.t[name] = h
        return h


def emit_dequant_ghost(pr, ph, src, dst, nrows, ncols, lutB, cdiff, tag):
    nc = pr.nc
    CC = 1376 if ncols % 1376 == 0 else 1024
    idx = [pr.sbuf(f"dq_{tag}_idx{i}", [128, CC], BF16) for i in range(2)]
    tmp = pr.sbuf(f"dq_{tag}_tmp", [128, CC], BF16)
    p = [pr.sbuf(f"dq_{tag}_p{i}", [128, CC], BF16) for i in range(4)]
    s01 = pr.sbuf(f"dq_{tag}_s01", [128, CC], F32)
    s23 = pr.sbuf(f"dq_{tag}_s23", [128, CC], F32)
    w = [pr.sbuf(f"dq_{tag}_w{i}", [128, CC], BF16) for i in range(2)]

    n_rt = nrows // 128
    chunks = []
    c0 = 0
    while c0 < ncols:
        cw = min(CC, ncols - c0)
        chunks.append((c0, cw))
        c0 += cw

    for rt in range(n_rt):
        for ci, (c0, cw) in enumerate(chunks):
            ib = idx[(rt * len(chunks) + ci) % 2]
            wb = w[(rt * len(chunks) + ci) % 2]
            ph.op("sync", lambda e, ib=ib, rt=rt, c0=c0, cw=cw: e.dma_start(
                out=ib[:, 0:cw], in_=src[rt * 128:(rt + 1) * 128, c0:c0 + cw]),
                reads=[], writes=[ib], dma=True)
            for g in range(4):
                j0 = 4 * g
                ph.op("vector", lambda e, pg=p[g], ib=ib, j0=j0, cw=cw: e.tensor_scalar(
                    pg[:, 0:cw], ib[:, 0:cw], float(j0), cdiff[:, j0:j0 + 1],
                    OP.is_ge, OP.mult), reads=[ib, cdiff], writes=[p[g]])
                for j in (j0 + 1, j0 + 2, j0 + 3):
                    ph.op("vector", lambda e, ib=ib, j=j, cw=cw: e.tensor_scalar(
                        tmp[:, 0:cw], ib[:, 0:cw], float(j), cdiff[:, j:j + 1],
                        OP.is_ge, OP.mult), reads=[ib, cdiff], writes=[tmp])
                    ph.op("vector", lambda e, pg=p[g], cw=cw: e.tensor_add(
                        pg[:, 0:cw], pg[:, 0:cw], tmp[:, 0:cw]),
                        reads=[tmp, p[g]], writes=[p[g]])
            ph.op("vector", lambda e, cw=cw: e.tensor_add(
                s01[:, 0:cw], p[0][:, 0:cw], p[1][:, 0:cw]),
                reads=[p[0], p[1]], writes=[s01])
            ph.op("vector", lambda e, cw=cw: e.tensor_add(
                s23[:, 0:cw], p[2][:, 0:cw], p[3][:, 0:cw]),
                reads=[p[2], p[3]], writes=[s23])
            ph.op("vector", lambda e, wb=wb, cw=cw: e.tensor_add(
                wb[:, 0:cw], s01[:, 0:cw], s23[:, 0:cw]),
                reads=[s01, s23], writes=[wb])
            ph.op("gpsimd", lambda e, wb=wb, rt=rt, c0=c0, cw=cw: e.dma_start(
                out=dst[rt * 128:(rt + 1) * 128, c0:c0 + cw], in_=wb[:, 0:cw]),
                reads=[wb], writes=[(dst, (rt, c0))], dma=True,
                chan=f"dqs_{tag}_{(rt * len(chunks) + ci) % 2}")


_FP4_VALS = [0.0, 0.5, 1.0, 1.5, 2.0, 3.0, 4.0, 6.0,
             -0.0, -0.5, -1.0, -1.5, -2.0, -3.0, -4.0, -6.0]


def emit_dequant_fp4(pr, ph, src, scales, dst, nrows, ncols, tag):
    nc = pr.nc
    CC = 1024
    idx = [pr.sbuf(f"dq_{tag}_idx{i}", [128, CC], BF16) for i in range(2)]
    sc = [pr.sbuf(f"dq_{tag}_sc{i}", [128, CC], F16) for i in range(2)]
    tmp = pr.sbuf(f"dq_{tag}_tmp", [128, CC], BF16)
    acc = pr.sbuf(f"dq_{tag}_acc", [128, CC], BF16)
    w = [pr.sbuf(f"dq_{tag}_w{i}", [128, CC], BF16) for i in range(2)]
    diffs = [_FP4_VALS[j] - _FP4_VALS[j - 1] for j in range(1, 16)]

    n_rt = nrows // 128
    chunks = []
    c0 = 0
    while c0 < ncols:
        cw = min(CC, ncols - c0)
        chunks.append((c0, cw))
        c0 += cw

    for rt in range(n_rt):
        for ci, (c0, cw) in enumerate(chunks):
            par = (rt * len(chunks) + ci) % 2
            ib, sb, wb = idx[par], sc[par], w[par]
            ph.op("sync", lambda e, ib=ib, rt=rt, c0=c0, cw=cw: e.dma_start(
                out=ib[:, 0:cw], in_=src[rt * 128:(rt + 1) * 128, c0:c0 + cw]),
                writes=[ib], dma=True)
            ph.op("sync", lambda e, sb=sb, rt=rt, c0=c0, cw=cw: e.dma_start(
                out=sb[:, 0:cw], in_=scales[rt * 128:(rt + 1) * 128, c0:c0 + cw]),
                writes=[sb], dma=True)
            ph.op("vector", lambda e, ib=ib, cw=cw: e.tensor_scalar(
                acc[:, 0:cw], ib[:, 0:cw], 1.0, float(diffs[0]), OP.is_ge, OP.mult),
                reads=[ib], writes=[acc])
            for j in range(2, 16):
                dj = float(diffs[j - 1])
                ph.op("vector", lambda e, ib=ib, j=j, dj=dj, cw=cw: e.tensor_scalar(
                    tmp[:, 0:cw], ib[:, 0:cw], float(j), dj, OP.is_ge, OP.mult),
                    reads=[ib], writes=[tmp])
                ph.op("vector", lambda e, cw=cw: e.tensor_add(
                    acc[:, 0:cw], acc[:, 0:cw], tmp[:, 0:cw]),
                    reads=[tmp, acc], writes=[acc])
            ph.op("vector", lambda e, wb=wb, sb=sb, cw=cw: e.tensor_mul(
                wb[:, 0:cw], acc[:, 0:cw], sb[:, 0:cw]),
                reads=[acc, sb], writes=[wb])
            ph.op("gpsimd", lambda e, wb=wb, rt=rt, c0=c0, cw=cw: e.dma_start(
                out=dst[rt * 128:(rt + 1) * 128, c0:c0 + cw], in_=wb[:, 0:cw]),
                reads=[wb], writes=[(dst, (rt, c0))], dma=True,
                chan=f"dqs_{tag}_{par}")

def emit_ffn(pr, ph, xin, xout, g_scr, g_off, u_scr, u_off, dn_scr, inter,
             rms, tag, out_toks=None):
    nc = pr.nc
    n_oi = inter // 128
    ones_bf = pr.t["ones_bf"]
    ones_f32 = pr.t["ones_f32"]

    xb2 = [[pr.sbuf(f"ffn_{tag}_xb{p}_{i}", [128, RT], BF16) for i in range(DT)]
           for p in range(2)]
    harr = [pr.sbuf(f"ffn_{tag}_h{i}", [128, RT], BF16) for i in range(n_oi)]
    z = [pr.sbuf(f"ffn_{tag}_z{i}", [128, RT], BF16) for i in range(DT)]
    nq = 2 if rms else DT
    zsq = [pr.sbuf(f"ffn_{tag}_q{i}", [128, RT], BF16) for i in range(nq)]
    sg = [pr.sbuf(f"ffn_{tag}_sg{i}", [128, RT], BF16) for i in range(4)]
    wg = [pr.sbuf(f"ffn_{tag}_wg{i}", [128, DT * RT], BF16) for i in range(2)]
    wu = [pr.sbuf(f"ffn_{tag}_wu{i}", [128, DT * RT], BF16) for i in range(2)]
    wdn = [pr.sbuf(f"ffn_{tag}_wd{i}", [128, 4 * RT], BF16) for i in range(2)]
    if rms:
        yarr = [pr.sbuf(f"ffn_{tag}_y{i}", [128, RT], BF16) for i in range(DT)]
    utmp = [pr.sbuf(f"ffn_{tag}_u{i}", [128, RT], F32) for i in range(3)]
    ost = [pr.sbuf(f"ffn_{tag}_o{i}", [128, RT], F32) for i in range(3)]
    rows = pr.sbuf(f"ffn_{tag}_rows", [1, 3 * RT], F32)

    work = [pr.psum(f"ffn_{tag}_pw{i}", [128, RT]) for i in range(4)]
    bc1 = pr.psum(f"ffn_{tag}_bc1", [128, RT])
    bc2 = pr.psum(f"ffn_{tag}_bc2", [128, RT])
    st1 = pr.psum(f"ffn_{tag}_st1", [1, RT])
    st2 = pr.psum(f"ffn_{tag}_st2", [1, RT])

    groups = [list(range(g, min(g + 4, n_oi))) for g in range(0, n_oi, 4)]

    def stat_mm(bank, rhs, first, last):
        def fn(e, bank=bank, rhs=rhs, first=first, last=last):
            return e.matmul(bank[:], ones_bf[:, 0:1], rhs[:],
                            start=first, stop=last)
        return fn

    def load_x(r):
        if r >= NR:
            return
        t0 = r * RT
        xb = xb2[r % 2]
        for ot in range(DT):
            ph.op("gpsimd", lambda e, ot=ot, t0=t0, xb=xb: e.dma_start(
                out=xb[ot][:], in_=xin[ot * 128:(ot + 1) * 128, t0:t0 + RT]),
                writes=[xb[ot]], dma=True, chan=f"ffn_{tag}_xbg{r % 2}")

    load_x(0)
    for r in range(NR):
        t0 = r * RT
        xb = xb2[r % 2]

        def wload(dst, scr, off, g0, ncols):
            rs = scr[:].ap[0][0]
            src_ap = bass.AP(scr, off + g0 * 128,
                             [[rs, 128], [128 * rs, DT], [1, ncols]])
            dst_ap = bass.AP(dst, 0, [[dst[:].ap[0][0], 128], [RT, DT], [1, ncols]])
            return lambda e: e.dma_start(out=dst_ap, in_=src_ap)

        for gi, G in enumerate(groups):
            g0 = G[0]
            ncols = len(G) * 128
            wgb = wg[gi % 2]
            wub = wu[gi % 2]
            ph.op("sync", wload(wgb, g_scr, g_off, g0, ncols),
                  writes=[wgb], dma=True)
            ph.op("sync", wload(wub, u_scr, u_off, g0, ncols),
                  writes=[wub], dma=True)
            for j, oi in enumerate(G):
                def gmm(e, j=j, wgb=wgb, xb=xb):
                    ins = None
                    for di in range(DT):
                        ins = e.matmul(work[j][:],
                                       wgb[:, di * RT + j * 128:di * RT + (j + 1) * 128],
                                       xb[di][:], start=(di == 0), stop=(di == DT - 1))
                    return ins
                ph.op("tensor", gmm,
                      reads=[wgb] + [xb[di] for di in range(DT)],
                      writes=[work[j]])
                ph.op("scalar", lambda e, j=j: e.activation(
                    sg[j][:], work[j][:], AF.Silu), reads=[work[j]], writes=[sg[j]])
            for j, oi in enumerate(G):
                def umm(e, j=j, wub=wub, xb=xb):
                    ins = None
                    for di in range(DT):
                        ins = e.matmul(work[j][:],
                                       wub[:, di * RT + j * 128:di * RT + (j + 1) * 128],
                                       xb[di][:], start=(di == 0), stop=(di == DT - 1))
                    return ins
                ph.op("tensor", umm,
                      reads=[wub] + [xb[di] for di in range(DT)],
                      writes=[work[j]])
                ph.op("vector", lambda e, j=j, oi=oi: e.tensor_mul(
                    harr[oi][:], sg[j][:], work[j][:]),
                    reads=[sg[j], work[j]], writes=[harr[oi]])

        kquads = [list(range(k, min(k + 4, n_oi))) for k in range(0, n_oi, 4)]
        for go in range(4):
            ots = [4 * go + i for i in range(4)]
            for qi, KQ in enumerate(kquads):
                wdb = wdn[qi % 2]
                nk = len(KQ)
                k0 = KQ[0]
                def dload(e, wdb=wdb, k0=k0, nk=nk, go=go):
                    rs = dn_scr[:].ap[0][0]
                    src_ap = bass.AP(dn_scr, k0 * 128 * rs + go * RT,
                                     [[rs, 128], [128 * rs, nk], [1, RT]])
                    dst_ap = bass.AP(wdb, 0, [[wdb[:].ap[0][0], 128], [RT, nk], [1, RT]])
                    return e.dma_start(out=dst_ap, in_=src_ap)
                ph.op("sync", dload, writes=[wdb], dma=True)
                def dmm(e, wdb=wdb, KQ=KQ, ots=ots):
                    ins = None
                    for kk, ki in enumerate(KQ):
                        for j, ot in enumerate(ots):
                            ins = e.matmul(work[j][:],
                                           wdb[:, kk * RT + j * 128:kk * RT + (j + 1) * 128],
                                           harr[ki][:], start=(ki == 0),
                                           stop=(ki == n_oi - 1))
                    return ins
                ph.op("tensor", dmm,
                      reads=[wdb] + [harr[ki] for ki in KQ],
                      writes=[work[j] for j in range(4)])
            for j, ot in enumerate(ots):
                q = zsq[ot % nq]
                if rms:
                    ph.op("vector", lambda e, j=j, ot=ot: e.tensor_copy(
                        yarr[ot][:], work[j][:]), reads=[work[j]], writes=[yarr[ot]])
                    ph.op("scalar", lambda e, ot=ot, q=q: e.activation(
                        q[:], yarr[ot][:], AF.Square),
                        reads=[yarr[ot]], writes=[q])
                    ph.op("tensor", stat_mm(st2, q, ot == 0, ot == DT - 1),
                          reads=[q, ones_bf], writes=[(st2, "acc")])
                else:
                    ph.op("vector", lambda e, j=j, ot=ot, xb=xb: e.scalar_tensor_tensor(
                        z[ot][:], work[j][:], 1.0, xb[ot][:], OP.mult, OP.add),
                        reads=[work[j], xb[ot]], writes=[z[ot]])
                    ph.op("scalar", lambda e, ot=ot, q=q: e.activation(
                        q[:], z[ot][:], AF.Square),
                        reads=[z[ot]], writes=[q])
                    ph.op("tensor", stat_mm(st1, z[ot], ot == 0, ot == DT - 1),
                          reads=[z[ot], ones_bf], writes=[(st1, "acc")])
                    ph.op("tensor", stat_mm(st2, q, ot == 0, ot == DT - 1),
                          reads=[q, ones_bf], writes=[(st2, "acc")])

        R = rows
        if rms:
            ph.op("vector", lambda e: e.tensor_scalar(
                R[:, 0:RT], st2[:], 1.0 / D, RMSEPS, OP.mult, OP.add),
                reads=[(st2, "acc")], writes=[(rows, "A")])
            ph.op("vector", lambda e: e.reciprocal(R[:, RT:2 * RT], R[:, 0:RT]),
                reads=[(rows, "A")], writes=[(rows, "B")])
            ph.op("scalar", lambda e: e.activation(
                R[:, 2 * RT:3 * RT], R[:, RT:2 * RT], AF.Sqrt),
                reads=[(rows, "B")], writes=[(rows, "C")])
            ph.op("tensor", lambda e: e.matmul(
                bc1[:], ones_f32[0:1, :], R[:, 2 * RT:3 * RT],
                start=True, stop=True), reads=[(rows, "C"), ones_f32], writes=[bc1])
            for ot in range(DT):
                q = zsq[ot % nq]
                ph.op("vector", lambda e, ot=ot: e.tensor_mul(
                    utmp[ot % 2][:], yarr[ot][:], bc1[:]),
                    reads=[yarr[ot], bc1], writes=[utmp[ot % 2]])
                ph.op("vector", lambda e, ot=ot, xb=xb: e.tensor_add(
                    z[ot][:], utmp[ot % 2][:], xb[ot][:]),
                    reads=[utmp[ot % 2], xb[ot]], writes=[z[ot]])
                ph.op("scalar", lambda e, ot=ot, q=q: e.activation(
                    q[:], z[ot][:], AF.Square),
                    reads=[z[ot]], writes=[q])
                ph.op("tensor", stat_mm(st1, z[ot], ot == 0, ot == DT - 1),
                      reads=[z[ot], ones_bf], writes=[(st1, "acc")])
                ph.op("tensor", stat_mm(st2, q, ot == 0, ot == DT - 1),
                      reads=[q, ones_bf], writes=[(st2, "acc")])

        load_x(r + 1)
        emit_ln_tail(pr, ph, tag, z, zsq, st1, st2, bc1, bc2, rows,
                     utmp, ost, xout, r, out_toks)


def emit_lut_prep(pr, ph):
    nc = pr.nc
    lut_s = pr.sbuf("lut_s", [1, 16], F32, persistent=True)
    lutB = pr.sbuf("lutB", [128, 16], F32, persistent=True)
    cdiff = pr.sbuf("cdiff", [128, 16], F32, persistent=True)
    pb = pr.psum("lut_pb", [128, 16])
    ph.op("sync", lambda e: e.dma_start(out=lut_s[:], in_=pr.t["lut"][:, :]),
          writes=[lut_s], dma=True)
    ph.op("tensor", lambda e: e.matmul(
        pb[:], pr.t["ones_f32"][0:1, :], lut_s[:], start=True, stop=True),
        reads=[lut_s, pr.t["ones_f32"]], writes=[pb])
    ph.op("vector", lambda e: e.tensor_copy(lutB[:], pb[:]), reads=[pb], writes=[lutB])
    ph.op("vector", lambda e: e.tensor_copy(cdiff[:, 0:1], lutB[:, 0:1]),
          reads=[lutB], writes=[(cdiff, 0)])
    ph.op("vector", lambda e: e.tensor_sub(cdiff[:, 1:16], lutB[:, 1:16], lutB[:, 0:15]),
          reads=[lutB], writes=[(cdiff, 1)])
    return lutB, cdiff


def emit_ones(pr, ph):
    ones_bf = pr.sbuf("ones_bf", [128, 1], BF16, persistent=True)
    ones_f32 = pr.sbuf("ones_f32", [1, 128], F32, persistent=True)
    ph.op("gpsimd", lambda e: e.memset(ones_bf[:], 1.0), writes=[ones_bf])
    ph.op("gpsimd", lambda e: e.memset(ones_f32[:], 1.0), writes=[ones_f32])


def build_l0_ffn_program():
    nc = bass.Bass("TRN2", target_bir_lowering=False)
    with ExitStack() as es:
        pr = Prog(nc, es)
        xin = pr.dram_in("xin", [D, NT], F32)
        xout = pr.dram_out("xout", [D, NT], F32)
        g_idxT = pr.dram_in("g_idxT", [D, I0], BF16)
        u_idxT = pr.dram_in("u_idxT", [D, I0], BF16)
        d_idxT = pr.dram_in("d_idxT", [I0, D], BF16)
        pr.dram_in("lut", [1, 16], F32)
        gw = pr.dram_tmp("gw", [D, I0], BF16, shared=True)
        uw = pr.dram_tmp("uw", [D, I0], BF16, shared=True)
        dw = pr.dram_tmp("dw", [I0, D], BF16)

        with pr.scope():
            ph0 = Ph(nc, pr=pr, name="prep")
            emit_ones(pr, ph0)
            emit_lut_prep(pr, ph0)
            ph0.emit(es)
        lutB, cdiff = pr.t["lutB"], pr.t["cdiff"]
        for tag, src, dst, nr_, nc_ in (
            ("g", g_idxT, gw, D, I0),
            ("u", u_idxT, uw, D, I0),
            ("d", d_idxT, dw, I0, D),
        ):
            with pr.scope():
                ph = Ph(nc, pr=pr, name="dq_" + tag)
                emit_dequant_ghost(pr, ph, src, dst, nr_, nc_, lutB, cdiff, tag)
                ph.emit(es)
        with pr.scope():
            ph = Ph(nc, pr=pr, name="ffn0")
            emit_ffn(pr, ph, xin, xout, gw, 0, uw, 0, dw, I0, False, "f0")
            ph.emit(es)
    return nc


def build_l1_ffn_program(debug_w=False):
    nc = bass.Bass("TRN2", target_bir_lowering=False)
    with ExitStack() as es:
        pr = Prog(nc, es)
        xin = pr.dram_in("xin", [D, NT], F32)
        xout = pr.dram_out("xout", [D, NT], F32)
        gu_codesT = pr.dram_in("gu_codesT", [D, GU], BF16)
        dn_codesT = pr.dram_in("dn_codesT", [I1, D], BF16)
        gu_scalesT = pr.dram_in("gu_scalesT", [D, GU], F16)
        dn_scalesT = pr.dram_in("dn_scalesT", [I1, D], F16)
        mk_w = pr.dram_out if debug_w else pr.dram_tmp
        guw = mk_w("guw", [D, GU], BF16)
        dnw = mk_w("dnw", [I1, D], BF16)

        with pr.scope():
            ph0 = Ph(nc, pr=pr, name="prep")
            emit_ones(pr, ph0)
            ph0.emit(es)
        with pr.scope():
            ph = Ph(nc, pr=pr, name="dq_gu")
            emit_dequant_fp4(pr, ph, gu_codesT, gu_scalesT, guw, D, GU, "gu")
            ph.emit(es)
        with pr.scope():
            ph = Ph(nc, pr=pr, name="dq_dn")
            emit_dequant_fp4(pr, ph, dn_codesT, dn_scalesT, dnw, I1, D, "dn")
            ph.emit(es)
        with pr.scope():
            ph = Ph(nc, pr=pr, name="ffn1")
            emit_ffn(pr, ph, xin, xout, guw, 0, guw, I1, dnw, I1, True, "f1")
            ph.emit(es)
    return nc

def emit_identity(pr, ph):
    ident = pr.sbuf("ident", [128, 128], BF16, persistent=True)
    ph.op("gpsimd", lambda e: e.iota(ident[:], [[1, 128]], base=0,
          channel_multiplier=-1, allow_small_or_imprecise_dtypes=True),
          writes=[ident])
    ph.op("vector", lambda e: e.tensor_scalar(ident[:], ident[:], 0.0, 1.0,
          OP.is_equal, OP.mult), reads=[ident], writes=[ident])
    return ident


def emit_ln_tail(pr, ph, tag, z, zsq_pair, st1, st2, bc1, bc2, rows, utmp, ost,
                 xout, r, out_toks):
    ones_f32 = pr.t["ones_f32"]
    A = rows[:, 0:RT]; B = rows[:, RT:2 * RT]; C = rows[:, 2 * RT:3 * RT]
    kA, kB, kC = (rows, "A"), (rows, "B"), (rows, "C")
    ph.op("vector", lambda e: e.tensor_scalar_mul(A, st1[:], 1.0 / D),
          reads=[(st1, "acc")], writes=[kA])
    ph.op("vector", lambda e: e.tensor_scalar_mul(B, st2[:], 1.0 / D),
          reads=[(st2, "acc")], writes=[kB])
    ph.op("vector", lambda e: e.tensor_mul(C, A, A), reads=[kA], writes=[kC])
    ph.op("vector", lambda e: e.tensor_sub(B, B, C), reads=[kB, kC], writes=[kB])
    ph.op("vector", lambda e: e.tensor_scalar_add(B, B, LNEPS), reads=[kB], writes=[kB])
    ph.op("vector", lambda e: e.reciprocal(C, B), reads=[kB], writes=[kC])
    ph.op("scalar", lambda e: e.activation(B, C, AF.Sqrt), reads=[kC], writes=[kB])
    ph.op("tensor", lambda e: e.matmul(
        bc1[:], ones_f32[0:1, :], A, start=True, stop=True),
        reads=[kA, ones_f32], writes=[bc1])
    ph.op("tensor", lambda e: e.matmul(
        bc2[:], ones_f32[0:1, :], B, start=True, stop=True),
        reads=[kB, ones_f32], writes=[bc2])
    t0 = r * RT
    nu, no = len(utmp), len(ost)
    for ot in range(DT):
        u = utmp[ot % nu]
        o = ost[ot % no]
        ph.op("vector", lambda e, ot=ot, u=u: e.scalar_tensor_tensor(
            u[:], z[ot][:], 1.0, bc1[:], OP.mult, OP.subtract),
            reads=[z[ot], bc1], writes=[u])
        ph.op("vector", lambda e, ot=ot, u=u, o=o: e.tensor_mul(
            o[:], u[:], bc2[:]),
            reads=[u, bc2], writes=[o])
        tok = ph.op("gpsimd", lambda e, ot=ot, t0=t0, o=o: e.dma_start(
            out=xout[ot * 128:(ot + 1) * 128, t0:t0 + RT], in_=o[:]),
            reads=[o], writes=[(xout, (ot, r))], dma=True)
        if out_toks is not None:
            out_toks.append(tok)


def emit_l1_attn(pr, es, xin, xout, out_toks=None):
    nc = pr.nc
    Wk1, Wv1, Wg1, Wp1 = pr.t["Wk1"], pr.t["Wv1"], pr.t["Wg1"], pr.t["Wp1"]
    wpb1 = pr.dram_tmp("wpb1", [DT, 128, D], BF16)
    latq, cos1d, sin1d, permT1d, latmaskd = (
        pr.t["latq"], pr.t["cos1"], pr.t["sin1"], pr.t["permT1"], pr.t["latmask"])

    attnb = pr.sbuf("a1_attnb", [64, NB * H * 128], BF16)
    latqb = pr.sbuf("a1_latq", [128, H * NL], BF16)
    cos1 = pr.sbuf("a1_cos1", [128, 128], F32)
    sin1 = pr.sbuf("a1_sin1", [128, 128], F32)
    permT1 = pr.sbuf("a1_permT1", [128, 128], BF16)
    latmask = pr.sbuf("a1_mask", [64, 128], F32)

    with pr.scope():
        ph = Ph(nc, pr=pr, name="a1kv")
        ident = pr.t["ident"]
        karr = [pr.sbuf(f"a1_k{b}", [128, H * 128], BF16) for b in range(NB)]
        varr = [pr.sbuf(f"a1_v{b}", [128, H * 128], BF16) for b in range(NB)]
        stage = [pr.sbuf(f"a1kv_st{i}", [128, 2048], F32) for i in range(2)]
        wkv = [pr.sbuf(f"a1kv_w{i}", [128, 2048], BF16) for i in range(2)]
        xk = [pr.sbuf(f"a1kv_x{b}", [128, DT * 128], BF16) for b in range(NB)]
        xst = [pr.sbuf(f"a1kv_xs{i}", [128, 128], F32) for i in range(2)]
        kev = pr.sbuf("a1kv_kev", [128, 128], BF16)
        tc = pr.sbuf("a1kv_tc", [128, 128], BF16)
        ts = pr.sbuf("a1kv_ts", [128, 128], BF16)
        sm = pr.sbuf("a1kv_sm", [64, 128], F32)
        ngm = pr.sbuf("a1kv_ngm", [64, 1], F32)
        wexp = pr.sbuf("a1kv_wexp", [64, 128], BF16)
        ssum = pr.sbuf("a1kv_ssum", [64, 1], F32)
        rs = pr.sbuf("a1kv_rs", [64, 1], F32)
        wn = pr.sbuf("a1kv_wn", [64, 128], BF16)
        wT = pr.sbuf("a1kv_wT", [128, 64], BF16)
        vT = pr.sbuf("a1kv_vT", [128, 128], BF16)
        pk = pr.psum("a1kv_pk", [128, 128])
        pr_ = pr.psum("a1kv_pr", [128, 128])
        psc = pr.psum("a1kv_psc", [64, 128])
        ptw = pr.psum("a1kv_ptw", [128, 64], BF16)
        ptv = pr.psum("a1kv_ptv", [128, 128], BF16)
        pat = pr.psum("a1kv_pat", [64, 128])

        ph.op("sync", lambda e: e.dma_start(out=cos1[:], in_=cos1d[:, :]),
              writes=[cos1], dma=True)
        ph.op("sync", lambda e: e.dma_start(out=sin1[:], in_=sin1d[:, :]),
              writes=[sin1], dma=True)
        ph.op("sync", lambda e: e.dma_start(out=permT1[:], in_=permT1d[:, :]),
              writes=[permT1], dma=True)
        ph.op("sync", lambda e: e.dma_start(out=latmask[:], in_=latmaskd[:, :]),
              writes=[latmask], dma=True)
        lqs = stage[0]
        ph.op("sync", lambda e: e.dma_start(
            out=lqs[:, 0:H * NL],
            in_=bass.AP(latq, 0, [[NL, 128], [128 * NL, H], [1, NL]])),
            writes=[stage[0]], dma=True)
        ph.op("vector", lambda e: e.tensor_copy(latqb[:], lqs[:, 0:H * NL]),
              reads=[stage[0]], writes=[latqb])

        for b in range(NB):
            for di in range(DT):
                xs = xst[di % 2]
                ph.op("gpsimd", lambda e, xs=xs, b=b, di=di: e.dma_start(
                    out=xs[:], in_=xin[di * 128:(di + 1) * 128,
                                       b * T:b * T + 128]),
                    writes=[xs], dma=True)
                ph.op("vector", lambda e, xs=xs, b=b, di=di: e.tensor_copy(
                    xk[b][:, di * 128:(di + 1) * 128], xs[:]),
                    reads=[xs], writes=[(xk[b], di)])

        wpc = pr.sbuf("a1kv_wpc", [128, D], BF16)
        for mt in range(DT):
            st = stage[mt % 2]
            ph.op("sync", lambda e, st=st, mt=mt: e.dma_start(
                out=st[:], in_=Wp1[mt, :, :]), writes=[st], dma=True)
            ph.op("vector", lambda e, st=st: e.tensor_copy(wpc[:], st[:]),
                  reads=[st], writes=[wpc])
            ph.op("gpsimd", lambda e, mt=mt: e.dma_start(
                out=wpb1[mt, :, :], in_=wpc[:]), reads=[wpc],
                writes=[(wpb1, mt)], dma=True)
        for mat, dest, dorope in ((Wk1, karr, True), (Wv1, varr, False)):
            for et in range(H):
                st = stage[et % 2]
                wb = wkv[et % 2]
                ph.op("sync", lambda e, st=st, et=et, mat=mat: e.dma_start(
                    out=st[:], in_=mat[et, :, :]), writes=[st], dma=True)
                ph.op("vector", lambda e, st=st, wb=wb: e.tensor_copy(wb[:], st[:]),
                      reads=[st], writes=[wb])
                for b in range(NB):
                    def kmm(e, wb=wb, b=b):
                        ins = None
                        for di in range(DT):
                            ins = e.matmul(pk[:], wb[:, di * 128:(di + 1) * 128],
                                           xk[b][:, di * 128:(di + 1) * 128],
                                           start=(di == 0), stop=(di == DT - 1))
                        return ins
                    ph.op("tensor", kmm,
                          reads=[wb] + [(xk[b], di) for di in range(DT)],
                          writes=[pk])
                    if dorope:
                        ph.op("vector", lambda e: e.tensor_copy(kev[:], pk[:]),
                              reads=[pk], writes=[kev])
                        ph.op("tensor", lambda e: e.matmul(
                            pr_[:], permT1[:], kev[:], start=True, stop=True),
                            reads=[kev, permT1], writes=[pr_])
                        ph.op("vector", lambda e: e.tensor_mul(tc[:], kev[:], cos1[:]),
                              reads=[kev, cos1], writes=[tc])
                        ph.op("vector", lambda e: e.tensor_mul(ts[:], pr_[:], sin1[:]),
                              reads=[pr_, sin1], writes=[ts])
                        ph.op("vector", lambda e, dest=dest, b=b, et=et: e.tensor_add(
                            dest[b][:, et * 128:(et + 1) * 128], tc[:], ts[:]),
                            reads=[tc, ts], writes=[(dest[b], et)])
                    else:
                        ph.op("vector", lambda e, dest=dest, b=b, et=et: e.tensor_copy(
                            dest[b][:, et * 128:(et + 1) * 128], pk[:]),
                            reads=[pk], writes=[(dest[b], et)])

        for b in range(NB):
            for h in range(H):
                ph.op("tensor", lambda e, b=b, h=h: e.matmul(
                    psc[:], latqb[:, h * NL:(h + 1) * NL],
                    karr[b][:, h * 128:(h + 1) * 128], start=True, stop=True),
                    reads=[latqb, (karr[b], h)], writes=[psc])
                ph.op("vector", lambda e: e.scalar_tensor_tensor(
                    sm[:], psc[:], SCALE1, latmask[:], OP.mult, OP.add),
                    reads=[psc, latmask], writes=[sm])
                ph.op("vector", lambda e: e.reduce_max(
                    ngm[:], sm[:], axis=AX.X, negate=True),
                    reads=[sm], writes=[ngm])
                ph.op("scalar", lambda e: e.activation(
                    wexp[:], sm[:], AF.Exp, bias=ngm[:, 0:1]),
                    reads=[sm, ngm], writes=[wexp])
                ph.op("vector", lambda e: e.reduce_sum(ssum[:], wexp[:], axis=AX.X),
                      reads=[wexp], writes=[ssum])
                ph.op("vector", lambda e: e.reciprocal(rs[:], ssum[:]),
                      reads=[ssum], writes=[rs])
                ph.op("vector", lambda e: e.tensor_scalar_mul(
                    wn[:], wexp[:], rs[:, 0:1]), reads=[wexp, rs], writes=[wn])
                ph.op("tensor", lambda e: e.transpose(ptw[:], wn[:], ident[0:64, 0:64]),
                      reads=[wn, ident], writes=[ptw])
                ph.op("vector", lambda e: e.tensor_copy(wT[:], ptw[:]),
                      reads=[ptw], writes=[wT])
                ph.op("tensor", lambda e, b=b, h=h: e.transpose(
                    ptv[:], varr[b][:, h * 128:(h + 1) * 128], ident[:]),
                    reads=[(varr[b], h), ident], writes=[ptv])
                ph.op("vector", lambda e: e.tensor_copy(vT[:], ptv[:]),
                      reads=[ptv], writes=[vT])
                ph.op("tensor", lambda e: e.matmul(
                    pat[:], wT[:], vT[:], start=True, stop=True),
                    reads=[wT, vT], writes=[pat])
                ph.op("vector", lambda e, b=b, h=h: e.tensor_copy(
                    attnb[:, (b * H + h) * 128:(b * H + h + 1) * 128], pat[:]),
                    reads=[pat], writes=[(attnb, (b, h))])
        ph.emit(es)

    with pr.scope():
        ph = Ph(nc, pr=pr, name="a1main")
        NG = 1024 // 128
        wgres = pr.sbuf("a1m_wg", [128, NG * 2048], BF16)
        wpst = [pr.sbuf(f"a1m_wp{i}", [128, D], BF16) for i in range(2)]
        xb2 = [[pr.sbuf(f"a1m_xb{p}_{i}", [128, RT], BF16) for i in range(DT)]
               for p in range(2)]
        gexp = [pr.sbuf(f"a1m_ge{i}", [64, RT], BF16) for i in range(2)] * 8
        gr = [pr.sbuf(f"a1m_gr{i}", [1, RT], F32) for i in range(2)]
        warr = [pr.sbuf(f"a1m_wa{i}", [128, RT], BF16) for i in range(H)]
        z = [pr.sbuf(f"a1m_z{i}", [128, RT], BF16) for i in range(DT)]
        zsq = [pr.sbuf(f"a1m_q{i}", [128, RT], BF16) for i in range(DT)]
        utmp = [pr.sbuf(f"a1m_u{i}", [128, RT], F32) for i in range(4)]
        ost = [pr.sbuf(f"a1m_o{i}", [128, RT], F32) for i in range(4)]
        rows = pr.sbuf("a1m_rows", [1, 3 * RT], F32)
        work = [pr.psum(f"a1m_pw{i}", [128, RT]) for i in range(3)]
        gs = pr.psum("a1m_gs", [1, RT])
        st1 = pr.psum("a1m_st1", [1, RT])
        st2 = pr.psum("a1m_st2", [1, RT])
        bc1 = pr.psum("a1m_bc1", [128, RT])
        bc2 = pr.psum("a1m_bc2", [128, RT])
        ones_bf = pr.t["ones_bf"]

        for mt in range(NG):
            ph.op("gpsimd", lambda e, mt=mt: e.dma_start(
                out=wgres[:, mt * 2048:(mt + 1) * 2048], in_=Wg1[mt, :, :]),
                writes=[(wgres, "all")], dma=True)


        def load_x(r):
            if r >= NR:
                return
            t0 = r * RT
            xb = xb2[r % 2]
            for ot in range(DT):
                ph.op("gpsimd", lambda e, ot=ot, t0=t0, xb=xb: e.dma_start(
                    out=xb[ot][:], in_=xin[ot * 128:(ot + 1) * 128, t0:t0 + RT]),
                    writes=[xb[ot]], dma=True, chan=f"a1m_xbg{r % 2}")

        load_x(0)
        for r in range(NR):
            t0 = r * RT
            b = r // (T // RT)
            xb = xb2[r % 2]
            ones_f32 = pr.t["ones_f32"]
            gsb = [gs[:], st1[:]]
            gskey = ["a1m_gs", "a1m_st1"]
            bcb = [bc1, bc2]
            bckey = ["a1m_bc1", "a1m_bc2"]
            for h in range(H):
                og, half = h // 2, (h % 2) * 64
                hp = h % 2
                w = work[0]
                def gmm(e, og=og, half=half, w=w, xb=xb):
                    ins = None
                    for di in range(DT):
                        c0 = og * 2048 + di * 128 + half
                        ins = e.matmul(w[0:64, :], wgres[:, c0:c0 + 64],
                                       xb[di][:], start=(di == 0), stop=(di == DT - 1))
                    return ins
                ph.op("tensor", gmm,
                      reads=[(wgres, "all")] + [xb[di] for di in range(DT)],
                      writes=[w])
                ph.op("scalar", lambda e, h=h, w=w: e.activation(
                    gexp[h][:], w[0:64, :], AF.Exp, scale=SCALE1),
                    reads=[w], writes=[gexp[h]])
                ph.op("tensor", lambda e, h=h, hp=hp: e.matmul(
                    gsb[hp], ones_bf[0:64, 0:1], gexp[h][:], start=True, stop=True),
                    reads=[gexp[h], ones_bf], writes=[(pr.t[gskey[hp]], "g")])
                g = gr[h % 2]
                ph.op("vector", lambda e, g=g, hp=hp: e.reciprocal(g[:], gsb[hp]),
                      reads=[(pr.t[gskey[hp]], "g")], writes=[g])
                ph.op("tensor", lambda e, g=g, hp=hp: e.matmul(
                    bcb[hp][0:64, :], ones_f32[0:1, 0:64], g[:], start=True, stop=True),
                    reads=[g, ones_f32], writes=[(pr.t[bckey[hp]], "g")])
                ph.op("vector", lambda e, h=h, hp=hp: e.tensor_mul(
                    gexp[h][:], gexp[h][:], bcb[hp][0:64, :]),
                    reads=[gexp[h], (pr.t[bckey[hp]], "g")], writes=[gexp[h]])
                wv = work[1 + h % 2]
                ph.op("tensor", lambda e, b=b, h=h, wv=wv: e.matmul(
                    wv[:], attnb[:, (b * H + h) * 128:(b * H + h + 1) * 128],
                    gexp[h][:], start=True, stop=True),
                    reads=[(attnb, (b, h)), gexp[h]], writes=[wv])
                ph.op("scalar", lambda e, h=h, wv=wv: e.copy(
                    warr[h][:], wv[:]), reads=[wv], writes=[warr[h]])
            for ot in range(DT):
                w = work[1 + ot % 2]
                wp = wpst[ot % 2]
                ph.op("sync", lambda e, ot=ot, wp=wp: e.dma_start(
                    out=wp[:], in_=wpb1[ot, :, :]), writes=[wp], dma=True)
                def pmm(e, ot=ot, w=w, wp=wp):
                    ins = None
                    for et in range(H):
                        ins = e.matmul(w[:],
                                       wp[:, et * 128:(et + 1) * 128],
                                       warr[et][:], start=(et == 0), stop=(et == H - 1))
                    return ins
                ph.op("tensor", pmm,
                      reads=[wp] + [warr[et] for et in range(H)],
                      writes=[w])
                q = zsq[ot]
                ph.op("vector", lambda e, ot=ot, w=w, xb=xb: e.scalar_tensor_tensor(
                    z[ot][:], w[:], 1.0, xb[ot][:], OP.mult, OP.add),
                    reads=[w, xb[ot]], writes=[z[ot]])
                ph.op("scalar", lambda e, ot=ot, q=q: e.activation(
                    q[:], z[ot][:], AF.Square), reads=[z[ot]], writes=[q])
                ph.op("tensor", lambda e, ot=ot: e.matmul(
                    st1[:], ones_bf[:, 0:1], z[ot][:],
                    start=(ot == 0), stop=(ot == DT - 1)),
                    reads=[z[ot], ones_bf], writes=[(st1, "acc"), (st1, "g")])
                ph.op("tensor", lambda e, ot=ot, q=q: e.matmul(
                    st2[:], ones_bf[:, 0:1], q[:],
                    start=(ot == 0), stop=(ot == DT - 1)),
                    reads=[q, ones_bf], writes=[(st2, "acc")])
            load_x(r + 1)
            emit_ln_tail(pr, ph, "a1", z, zsq, st1, st2, bc1, bc2, rows,
                         utmp, ost, xout, r, out_toks)
        ph.emit(es)


def build_l1_attn_program():
    nc = bass.Bass("TRN2", target_bir_lowering=False)
    with ExitStack() as es:
        pr = Prog(nc, es)
        xin = pr.dram_in("xin", [D, NT], F32)
        xout = pr.dram_out("xout", [D, NT], F32)
        pr.dram_in("Wk1", [H, 128, D], F32)
        pr.dram_in("Wv1", [H, 128, D], F32)
        pr.dram_in("Wg1", [8, 128, D], F32)
        pr.dram_in("Wp1", [DT, 128, D], F32)
        pr.dram_in("latq", [H, 128, NL], F32)
        pr.dram_in("cos1", [128, 128], F32)
        pr.dram_in("sin1", [128, 128], F32)
        pr.dram_in("permT1", [128, 128], BF16)
        pr.dram_in("latmask", [64, 128], F32)
        with pr.scope():
            ph0 = Ph(nc, pr=pr, name="prep")
            emit_ones(pr, ph0)
            emit_identity(pr, ph0)
            ph0.emit(es)
        with pr.scope():
            emit_l1_attn(pr, es, xin, xout)
    return nc

A0_DEBUG = False
A0_DEBUG2 = False


def emit_l0_attn(pr, es, xin, xout, out_toks=None, pre_ops=None):
    nc = pr.nc
    wqT, wkT, wvT, woT = pr.t["wqT"], pr.t["wkT"], pr.t["wvT"], pr.t["woT"]
    SC = 512
    NSC = T // SC
    NCH = SC // CH
    lam = np.exp2(-8.0 * np.arange(1, H + 1, dtype=np.float64) / H)
    blk_dec = [float(np.exp(-lam[h] * CH)) for h in range(H)]

    wqb = pr.dram_tmp("wqb", [H, 128, D], BF16)
    wkb = pr.dram_tmp("wkb", [HKV, 128, D], BF16)
    wob = pr.dram_tmp("wob", [DT, 128, D], BF16)
    wk_rhs = pr.dram_tmp("wk_rhs", [DT, 128, HKV * 128], BF16)
    wv_rhs = pr.dram_tmp("wv_rhs", [DT, 128, HKV * 128], BF16)

    cos0 = pr.sbuf("a0_cos0", [128, T], BF16)
    sin0 = pr.sbuf("a0_sin0", [128, T], BF16)
    cos0tm = pr.sbuf("a0_cos0tm", [CH, (T // CH) * 64], BF16)
    sin0tm = pr.sbuf("a0_sin0tm", [CH, (T // CH) * 64], BF16)
    permT0 = pr.sbuf("a0_permT0", [128, 128], BF16)
    diagT = pr.sbuf("a0_diagT", [64, H * CH], BF16)
    qdec = pr.sbuf("a0_qdec", [128, H * CH], BF16)
    kdecT = pr.sbuf("a0_kdecT", [CH, H], F32)
    kdecTM = pr.sbuf("a0_kdecTM", [CH, H * DH], BF16)

    with pr.scope():
        ph = pr.t.get("_shared_prep_ph") or Ph(nc, pr=pr, name="a0prep")
        for dst, srcn in ((permT0, "permT0"), (kdecT, "kdecT")):
            ph.op("sync", lambda e, dst=dst, srcn=srcn: e.dma_start(
                out=dst[:], in_=pr.t[srcn][:, :]), writes=[dst], dma=True)
        for dst, srcn in ((cos0, "cos0"), (sin0, "sin0"),
                          (cos0tm, "cos0tm"), (sin0tm, "sin0tm"),
                          (diagT, "diagT"), (qdec, "qdec"),
                          (kdecTM, "kdecTM")):
            ph.op("gpsimd", lambda e, dst=dst, srcn=srcn: e.dma_start(
                out=dst[:], in_=pr.t[srcn][:, :]), writes=[dst], dma=True)
        stage = [pr.sbuf(f"a0p_st{i}", [128, D], F32) for i in range(2)]
        wcast = [pr.sbuf(f"a0p_w{i}", [128, D], BF16) for i in range(2)]
        jobs = []
        for src_, dst, n, rhs in ((wqT, wqb, H, None), (wkT, wkb, HKV, wk_rhs),
                                  (wvT, None, HKV, wv_rhs), (woT, wob, DT, None)):
            for mt in range(n):
                jobs.append((src_, dst, mt, rhs))
        for i, (src_, dst, mt, rhs) in enumerate(jobs):
            st, wc = stage[i % 2], wcast[i % 2]
            ph.op("sync", lambda e, st=st, src_=src_, mt=mt: e.dma_start(
                out=st[:], in_=src_[mt, :, :]), writes=[st], dma=True)
            ph.op("vector", lambda e, st=st, wc=wc: e.tensor_copy(wc[:], st[:]),
                  reads=[st], writes=[wc])
            if dst is not None:
                ph.op("gpsimd", lambda e, wc=wc, dst=dst, mt=mt: e.dma_start(
                    out=dst[mt, :, :], in_=wc[:]), reads=[wc],
                    writes=[(dst, mt)], dma=True)
            if rhs is not None:
                for kt in range(DT):
                    ph.op("gpsimd", lambda e, wc=wc, rhs=rhs, mt=mt, kt=kt: e.dma_start(
                        out=rhs[kt, :, mt * 128:(mt + 1) * 128],
                        in_=wc[:, kt * 128:(kt + 1) * 128]), reads=[wc],
                        writes=[(rhs, (mt, kt))], dma=True,
                        chan=f"rhs_{kt % 4}")
        if pr.t.get("_shared_prep_ph") is None:
            ph.emit(es)
        else:
            ph.emit(es)
            pr.t["_shared_prep_ph"] = None

    with pr.scope():
        ph = Ph(nc, pr=pr, name="a0main")
        if pre_ops is not None:
            pre_ops(ph)
        ones_bf = pr.t["ones_bf"]
        ones_f32 = pr.t["ones_f32"]
        xb2 = [[pr.sbuf(f"a0m_xb{p}_{i}", [128, SC], BF16) for i in range(DT)]
               for p in range(2)]
        wbuf = [pr.sbuf(f"a0m_w{i}", [128, D], BF16) for i in range(2)]
        wobuf = [pr.sbuf(f"a0m_wo{i}", [128, D], BF16) for i in range(2)]
        kvres = pr.sbuf("a0m_kvres", [128, DT * 512], BF16)
        qarr = pr.sbuf("a0m_q", [128, H * SC], BF16)
        karr = pr.sbuf("a0m_k", [128, HKV * SC], BF16)
        ktm_raw = pr.sbuf("a0m_ktr", [CH, NCH * 512], BF16)
        ktm = ktm_raw
        vtm = pr.sbuf("a0m_vtm", [CH, NCH * 512], BF16)
        obuf = pr.sbuf("a0m_o", [128, H * SC], BF16)
        kv_sb = [[pr.sbuf(f"a0m_kv{h}_{p}", [128, 128], BF16) for p in range(2)]
                 for h in range(H)]
        qev = pr.sbuf("a0m_qev", [128, SC], BF16)
        tc = pr.sbuf("a0m_tc", [128, SC], BF16)
        ts2 = pr.sbuf("a0m_ts", [128, SC], BF16)
        qd = pr.sbuf("a0m_qd", [128, H * CH], BF16)
        kdT = [pr.sbuf(f"a0m_kdT{i}", [64, 128], BF16) for i in range(2)]
        z = [pr.sbuf(f"a0m_z{i}", [128, SC], BF16) for i in range(DT)]
        zsq = [pr.sbuf(f"a0m_q{i}", [128, SC], BF16) for i in range(DT)]
        utmp = [pr.sbuf(f"a0m_u{i}", [128, SC], F32) for i in range(2)]
        ost = [pr.sbuf(f"a0m_os{i}", [128, SC], F32) for i in range(2)]
        rows = pr.sbuf("a0m_rows", [1, 3 * SC], F32)

        qp = pr.psum("a0m_qp", [128, SC])
        qr = pr.psum("a0m_qr", [128, SC])
        pj = pr.psum("a0m_pj", [128, SC])
        bc1 = pr.psum("a0m_bc1", [128, SC])
        bc2 = pr.psum("a0m_bc2", [128, SC])
        pkv0 = pr.psum("a0m_pkv0", [128, 128])
        pkv1 = pr.psum("a0m_pkv1", [128, 128])
        stx = pr.psum("a0m_st", [1, SC])
        po = [qr[:, 0:CH], bc2[:, 0:CH]]
        pkv = [pkv0[:, :], pkv1[:, :]]
        pokey = ["qr", "bc2"]
        pkvkey = ["pkv0", "pkv1"]
        psh = [pj, bc1]
        pshkey = ["pj", "bc1"]
        smaskb = pr.sbuf("a0m_smb", [64, H * CH], BF16)


        def load_x(b, S):
            if S >= NSC:
                b, S = b + 1, 0
            if b >= NB:
                return
            t0 = b * T + S * SC
            xb = xb2[(b * NSC + S) % 2]
            for di in range(DT):
                ph.op("gpsimd", lambda e, di=di, t0=t0, xb=xb: e.dma_start(
                    out=xb[di][:], in_=xin[di * 128:(di + 1) * 128, t0:t0 + SC]),
                    writes=[xb[di]], dma=True, chan=f"a0m_xbg{(b * NSC + S) % 2}")

        load_x(0, 0)
        for b in range(NB):
            for h in range(H):
                ph.op("gpsimd", lambda e, h=h, b=b: e.memset(
                    kv_sb[h][0][:], 0.0), writes=[(kv_sb[h][0], "kv")])
            for S in range(NSC):
                t0 = b * T + S * SC
                p0 = S * SC
                r = (b * T + S * SC) // RT
                xb = xb2[(b * NSC + S) % 2]
                for (wsrc, n_et, dest) in ((wqb, H, qarr), (wkb, HKV, karr)):
                    for et in range(n_et):
                        wb = wbuf[et % 2]
                        ph.op("sync", lambda e, wb=wb, wsrc=wsrc, et=et: e.dma_start(
                            out=wb[:], in_=wsrc[et, :, :]), writes=[wb], dma=True)
                        def qmm(e, wb=wb, xb=xb):
                            ins = None
                            for di in range(DT):
                                ins = e.matmul(qp[:], wb[:, di * 128:(di + 1) * 128],
                                               xb[di][:], start=(di == 0),
                                               stop=(di == DT - 1))
                            return ins
                        ph.op("tensor", qmm,
                              reads=[wb] + [xb[di] for di in range(DT)], writes=[qp])
                        ph.op("vector", lambda e: e.tensor_copy(qev[:], qp[:]),
                              reads=[qp], writes=[qev])
                        ph.op("tensor", lambda e: e.matmul(
                            qr[:], permT0[:], qev[:], start=True, stop=True),
                            reads=[qev, permT0], writes=[(qr, "qr")])
                        ph.op("vector", lambda e, p0=p0: e.tensor_mul(
                            tc[:], qev[:], cos0[:, p0:p0 + SC]),
                            reads=[qev, cos0], writes=[tc])
                        ph.op("vector", lambda e, p0=p0: e.tensor_mul(
                            ts2[:], qr[:], sin0[:, p0:p0 + SC]),
                            reads=[(qr, "qr"), sin0], writes=[ts2])
                        ph.op("vector", lambda e, dest=dest, et=et: e.tensor_add(
                            dest[:, et * SC:(et + 1) * SC], tc[:], ts2[:]),
                            reads=[tc, ts2], writes=[(dest, et)])
                for (wsc, dest, dkey) in ((wk_rhs, ktm_raw, "ktr"), (wv_rhs, vtm, "vtm")):
                    for kt in range(DT):
                        ph.op("sync", lambda e, kt=kt, wsc=wsc: e.dma_start(
                            out=kvres[:, kt * 512:(kt + 1) * 512], in_=wsc[kt, :, :]),
                            writes=[kvres], dma=True, chan="kvres")
                    for c in range(NCH):
                        def tmm(e, c=c, xb=xb):
                            ins = None
                            for di in range(DT):
                                ins = e.matmul(
                                    qp[0:64, :],
                                    xb[di][:, c * CH:(c + 1) * CH],
                                    kvres[:, di * 512:(di + 1) * 512],
                                    start=(di == 0), stop=(di == DT - 1))
                            return ins
                        ph.op("tensor", tmm,
                              reads=[kvres] + [xb[di] for di in range(DT)],
                              writes=[qp])
                        ph.op("vector", lambda e, dest=dest, c=c: e.tensor_copy(
                            dest[:, c * 512:(c + 1) * 512], qp[0:64, :]),
                            reads=[qp], writes=[(dest, c)])
                cA = S * NCH * 64
                kraw_reads = [(ktm_raw, cc2) for cc2 in range(NCH)]
                pairs = [(c, g) for c in range(NCH) for g in range(HKV)]
                if A0_DEBUG2 and b == 0 and S == 0:
                    pairs = pairs[:1]
                for c, g in pairs:
                    o0 = c * 512 + g * 128
                    tco = cA + c * 64
                    ph.op("vector", lambda e, o0=o0, tco=tco: e.tensor_mul(
                        tc[0:CH, 0:64], ktm_raw[:, o0:o0 + 64],
                        cos0tm[:, tco:tco + 64]),
                        reads=kraw_reads + [cos0tm], writes=[(tc, "r")])
                    ph.op("vector", lambda e, o0=o0, tco=tco: e.tensor_mul(
                        ts2[0:CH, 0:64], ktm_raw[:, o0 + 64:o0 + 128],
                        sin0tm[:, tco:tco + 64]),
                        reads=kraw_reads + [sin0tm], writes=[(ts2, "r")])
                    ph.op("vector", lambda e, o0=o0, tco=tco: e.tensor_mul(
                        tc[0:CH, 64:128], ktm_raw[:, o0 + 64:o0 + 128],
                        cos0tm[:, tco:tco + 64]),
                        reads=kraw_reads + [cos0tm], writes=[(tc, "r")])
                    ph.op("vector", lambda e, o0=o0, tco=tco: e.tensor_mul(
                        ts2[0:CH, 64:128], ktm_raw[:, o0:o0 + 64],
                        sin0tm[:, tco:tco + 64]),
                        reads=kraw_reads + [sin0tm], writes=[(ts2, "r")])
                    ph.op("vector", lambda e, o0=o0: e.tensor_sub(
                        ktm_raw[:, o0:o0 + 64],
                        tc[0:CH, 0:64], ts2[0:CH, 0:64]),
                        reads=[(tc, "r"), (ts2, "r")] + kraw_reads,
                        writes=[(ktm_raw, c)])
                    ph.op("vector", lambda e, o0=o0: e.tensor_add(
                        ktm_raw[:, o0 + 64:o0 + 128],
                        tc[0:CH, 64:128], ts2[0:CH, 64:128]),
                        reads=[(tc, "r"), (ts2, "r")] + kraw_reads,
                        writes=[(ktm_raw, c)])
                for c in range(NCH):
                    cg = S * NCH + c
                    par = cg % 2
                    j0 = c * CH
                    ph.op("vector", lambda e, j0=j0: e.tensor_mul(
                        qd[:], bass.AP(qarr, j0, [[qarr[:].ap[0][0], 128], [SC, H], [1, CH]]),
                        qdec[:]), reads=[(qarr, et) for et in range(H)] + [qdec],
                        writes=[qd])
                    for hb in range(2):
                        bank = psh[hb]
                        bkey = (pr.t[f"a0m_{pshkey[hb]}"], pshkey[hb])
                        def stmm(e, hb=hb, j0=j0, bank=bank):
                            ins = None
                            for hh in range(8):
                                h = hb * 8 + hh
                                g = h // 4
                                ins = e.matmul(
                                    bank[0:64, hh * CH:(hh + 1) * CH],
                                    karr[:, g * SC + j0:g * SC + j0 + CH],
                                    qarr[:, h * SC + j0:h * SC + j0 + CH],
                                    start=True, stop=True)
                            return ins
                        ph.op("tensor", stmm,
                              reads=[(karr, g2) for g2 in range(HKV)]
                                    + [(qarr, h2) for h2 in range(hb * 8, hb * 8 + 8)],
                              writes=[bkey])
                        ph.op("vector", lambda e, hb=hb, bank=bank: e.tensor_mul(
                            smaskb[:, hb * 8 * CH:(hb + 1) * 8 * CH],
                            bank[0:64, 0:8 * CH],
                            diagT[:, hb * 8 * CH:(hb + 1) * 8 * CH]),
                            reads=[bkey, diagT], writes=[(smaskb, hb)])
                    for h in range(H):
                        g = h // 4
                        hp = h % 2
                        kt_ = kdT[hp]
                        ph.op("vector", lambda e, h=h, g=g, c=c, kt_=kt_: e.tensor_mul(
                            kt_[:], ktm[:, c * 512 + g * 128:c * 512 + (g + 1) * 128],
                            kdecTM[:, h * DH:(h + 1) * DH]),
                            reads=[(ktm, c), kdecTM], writes=[kt_])
                        def omm(e, h=h, g=g, c=c, par=par, hp=hp):
                            e.matmul(po[hp], vtm[:, c * 512 + g * 128:c * 512 + (g + 1) * 128],
                                     smaskb[0:64, h * CH:(h + 1) * CH],
                                     start=True, stop=False)
                            return e.matmul(po[hp], kv_sb[h][par][:],
                                            qd[:, h * CH:(h + 1) * CH],
                                            start=False, stop=True)
                        ph.op("tensor", omm,
                              reads=[(vtm, c), (smaskb, h // 8), (kv_sb[h][par], "kv"), qd],
                              writes=[(pr.t[f"a0m_{pokey[hp]}"], pokey[hp])])
                        ph.op("scalar", lambda e, h=h, j0=j0, hp=hp: e.copy(
                            obuf[:, h * SC + j0:h * SC + j0 + CH], po[hp]),
                            reads=[(pr.t[f"a0m_{pokey[hp]}"], pokey[hp])],
                            writes=[(obuf, h)])
                        ph.op("tensor", lambda e, g=g, c=c, hp=hp, kt_=kt_: e.matmul(
                            pkv[hp], kt_[:], vtm[:, c * 512 + g * 128:c * 512 + (g + 1) * 128],
                            start=True, stop=True),
                            reads=[kt_, (vtm, c)],
                            writes=[(pr.t[f"a0m_{pkvkey[hp]}"], pkvkey[hp])])
                        ph.op("vector", lambda e, h=h, par=par, hp=hp, bd=blk_dec[h]: e.scalar_tensor_tensor(
                            kv_sb[h][1 - par][:], kv_sb[h][par][:], bd, pkv[hp],
                            OP.mult, OP.add),
                            reads=[(kv_sb[h][par], "kv"),
                                   (pr.t[f"a0m_{pkvkey[hp]}"], pkvkey[hp])],
                            writes=[(kv_sb[h][1 - par], "kv")])
                for ot in range(DT):
                    wb = wobuf[ot % 2]
                    ph.op("sync", lambda e, wb=wb, ot=ot: e.dma_start(
                        out=wb[:], in_=wob[ot, :, :]), writes=[wb], dma=True)
                    def pmm(e, wb=wb):
                        ins = None
                        for et in range(H):
                            ins = e.matmul(pj[:], wb[:, et * 128:(et + 1) * 128],
                                           obuf[:, et * SC:(et + 1) * SC],
                                           start=(et == 0), stop=(et == H - 1))
                        return ins
                    ph.op("tensor", pmm,
                          reads=[wb] + [(obuf, et) for et in range(H)],
                          writes=[(pr.t["a0m_pj"], "pj")])
                    ph.op("vector", lambda e, ot=ot, xb=xb: e.scalar_tensor_tensor(
                        z[ot][:], pj[:], 1.0, xb[ot][:], OP.mult, OP.add),
                        reads=[(pr.t["a0m_pj"], "pj"), xb[ot]], writes=[z[ot]])
                    ph.op("scalar", lambda e, ot=ot: e.activation(
                        zsq[ot][:], z[ot][:], AF.Square),
                        reads=[z[ot]], writes=[zsq[ot]])
                load_x(b, S + 1)
                R = rows
                A = R[:, 0:SC]; B2 = R[:, SC:2 * SC]; C2 = R[:, 2 * SC:3 * SC]
                kA, kB, kC = (rows, "A"), (rows, "B"), (rows, "C")
                def s1mm(e):
                    ins = None
                    for ot in range(DT):
                        ins = e.matmul(stx[:], ones_bf[:, 0:1], z[ot][:],
                                       start=(ot == 0), stop=(ot == DT - 1))
                    return ins
                ph.op("tensor", s1mm, reads=[z[ot] for ot in range(DT)] + [ones_bf],
                      writes=[(stx, "acc")])
                ph.op("vector", lambda e: e.tensor_scalar_mul(A, stx[:], 1.0 / D),
                      reads=[(stx, "acc")], writes=[kA])
                def s2mm(e):
                    ins = None
                    for ot in range(DT):
                        ins = e.matmul(stx[:], ones_bf[:, 0:1], zsq[ot][:],
                                       start=(ot == 0), stop=(ot == DT - 1))
                    return ins
                ph.op("tensor", s2mm, reads=[zsq[ot] for ot in range(DT)] + [ones_bf],
                      writes=[(stx, "acc")])
                ph.op("vector", lambda e: e.tensor_scalar_mul(B2, stx[:], 1.0 / D),
                      reads=[(stx, "acc")], writes=[kB])
                ph.op("vector", lambda e: e.tensor_mul(C2, A, A), reads=[kA], writes=[kC])
                ph.op("vector", lambda e: e.tensor_sub(B2, B2, C2), reads=[kB, kC], writes=[kB])
                ph.op("vector", lambda e: e.tensor_scalar_add(B2, B2, LNEPS),
                      reads=[kB], writes=[kB])
                ph.op("vector", lambda e: e.reciprocal(C2, B2), reads=[kB], writes=[kC])
                ph.op("scalar", lambda e: e.activation(B2, C2, AF.Sqrt),
                      reads=[kC], writes=[kB])
                ph.op("tensor", lambda e: e.matmul(
                    bc1[:], ones_f32[0:1, :], A, start=True, stop=True),
                    reads=[kA, ones_f32], writes=[(pr.t["a0m_bc1"], "bc1")])
                ph.op("tensor", lambda e: e.matmul(
                    bc2[:], ones_f32[0:1, :], B2, start=True, stop=True),
                    reads=[kB, ones_f32], writes=[(pr.t["a0m_bc2"], "bc2")])
                for ot in range(DT):
                    u = utmp[ot % 2]
                    o = ost[ot % 2]
                    ph.op("vector", lambda e, ot=ot, u=u: e.scalar_tensor_tensor(
                        u[:], z[ot][:], 1.0, bc1[:], OP.mult, OP.subtract),
                        reads=[z[ot], (pr.t["a0m_bc1"], "bc1")], writes=[u])
                    ph.op("vector", lambda e, u=u, o=o: e.tensor_mul(
                        o[:], u[:], bc2[:]),
                        reads=[u, (pr.t["a0m_bc2"], "bc2")], writes=[o])
                    tok = ph.op("gpsimd", lambda e, ot=ot, t0=t0, o=o: e.dma_start(
                        out=xout[ot * 128:(ot + 1) * 128, t0:t0 + SC], in_=o[:]),
                        reads=[o], writes=[(xout, (ot, r))], dma=True)
                    if out_toks is not None:
                        out_toks.append(tok)
        ph.emit(es)


def build_l0_attn_program():
    nc = bass.Bass("TRN2", target_bir_lowering=False)
    with ExitStack() as es:
        pr = Prog(nc, es)
        xin = pr.dram_in("xin", [D, NT], F32)
        xout = pr.dram_out("xout", [D, NT], F32)
        pr.dram_in("wqT", [H, 128, D], F32)
        pr.dram_in("wkT", [HKV, 128, D], F32)
        pr.dram_in("wvT", [HKV, 128, D], F32)
        pr.dram_in("woT", [DT, 128, D], F32)
        pr.dram_in("cos0", [128, T], F32)
        pr.dram_in("sin0", [128, T], F32)
        pr.dram_in("permT0", [128, 128], BF16)
        pr.dram_in("diagT", [64, H * CH], F32)
        pr.dram_in("qdec", [128, H * CH], F32)
        pr.dram_in("cos0tm", [CH, (T // CH) * 64], F32)
        pr.dram_in("sin0tm", [CH, (T // CH) * 64], F32)
        pr.dram_in("kdecT", [CH, H], F32)
        pr.dram_in("kdecTM", [CH, H * DH], F32)
        if A0_DEBUG:
            pr.dram_out("dbg_q", [128, H * 512], BF16)
            pr.dram_out("dbg_k", [128, HKV * 512], BF16)
            pr.dram_out("dbg_ktm", [CH, 8 * 512], BF16)
            pr.dram_out("dbg_vtm", [CH, 8 * 512], BF16)
            pr.dram_out("dbg_ktr", [CH, 8 * 512], BF16)
            pr.dram_out("dbg_cos", [CH, (T // CH) * 64], BF16)
            pr.dram_out("dbg_sin", [CH, (T // CH) * 64], BF16)
        if A0_DEBUG2:
            pr.dram_out("dbg_tc", [64, 512], BF16)
            pr.dram_out("dbg_ts", [64, 512], BF16)
        with pr.scope():
            ph0 = Ph(nc, pr=pr, name="prep")
            emit_ones(pr, ph0)
            emit_identity(pr, ph0)
            ph0.emit(es)
        with pr.scope():
            emit_l0_attn(pr, es, xin, xout)
    return nc


def kernel(**inputs):
    return kernel_fused(**inputs)


def kernel_multilaunch(**inputs):
    n_cores = 8
    shared, per_core = host_prep(inputs, n_cores)
    cores = list(range(n_cores))

    def launch(nc, names, xin_list):
        in_maps = []
        for c in range(n_cores):
            im = {k: shared[k] for k in names}
            im["xin"] = xin_list[c]
            in_maps.append(im)
        res = run_bass_kernel_spmd(nc, in_maps, core_ids=cores)
        return [np.asarray(r["xout"]) for r in res.results]

    x = [pc["x0"] for pc in per_core]
    nc = build_l0_attn_program()
    x = launch(nc, ("wqT", "wkT", "wvT", "woT", "cos0", "sin0", "permT0",
                    "diagT", "qdec", "cos0tm", "sin0tm", "kdecT", "kdecTM"), x)
    nc = build_l0_ffn_program_x()
    x = launch(nc, ("g_idxT", "u_idxT", "d_idxT", "lut"), x)
    nc = build_l1_attn_program()
    x = launch(nc, ("Wk1", "Wv1", "Wg1", "Wp1", "latq", "cos1", "sin1",
                    "permT1", "latmask"), x)
    nc = build_l1_ffn_program_x()
    x = launch(nc, ("gu_codesT", "dn_codesT", "gu_scalesT", "dn_scalesT"), x)

    out = np.empty((16, T, D), np.float32)
    bpc = 16 // n_cores
    for c in range(n_cores):
        out[c * bpc:(c + 1) * bpc] = x[c].reshape(D, NB, T).transpose(1, 2, 0)
    return out


def build_l0_ffn_program_x():
    return build_l0_ffn_program()


def build_l1_ffn_program_x():
    return build_l1_ffn_program()


_FUSED_INPUTS = None


def build_full_program():
    global _FUSED_INPUTS
    nc = bass.Bass("TRN2", target_bir_lowering=False)
    Prog.input_names = []
    with ExitStack() as es:
        pr = Prog(nc, es)
        x0 = pr.dram_in("xin", [D, NT], F32)
        xout = pr.dram_out("xout", [D, NT], F32)
        xA = pr.dram_tmp("xA", [D, NT], F32)
        xB = pr.dram_tmp("xB", [D, NT], F32)
        pr.dram_in("wqT", [H, 128, D], F32)
        pr.dram_in("wkT", [HKV, 128, D], F32)
        pr.dram_in("wvT", [HKV, 128, D], F32)
        pr.dram_in("woT", [DT, 128, D], F32)
        pr.dram_in("cos0", [128, T], F32)
        pr.dram_in("sin0", [128, T], F32)
        pr.dram_in("permT0", [128, 128], BF16)
        pr.dram_in("diagT", [64, H * CH], F32)
        pr.dram_in("qdec", [128, H * CH], F32)
        pr.dram_in("cos0tm", [CH, (T // CH) * 64], F32)
        pr.dram_in("sin0tm", [CH, (T // CH) * 64], F32)
        pr.dram_in("kdecT", [CH, H], F32)
        pr.dram_in("kdecTM", [CH, H * DH], F32)
        g_idxT = pr.dram_in("g_idxT_s", [256, I0], BF16)
        u_idxT = pr.dram_in("u_idxT_s", [256, I0], BF16)
        d_idxT = pr.dram_in("d_idxT_s", [768, D], BF16)
        pr.dram_in("lut", [1, 16], F32)
        gw = pr.dram_tmp("gw", [D, I0], BF16, shared=True)
        uw = pr.dram_tmp("uw", [D, I0], BF16, shared=True)
        dw = pr.dram_tmp("dw", [6144, D], BF16, shared=True)
        gw_s = pr.dram_tmp("gw_s", [256, I0], BF16)
        uw_s = pr.dram_tmp("uw_s", [256, I0], BF16)
        dw_s = pr.dram_tmp("dw_s", [768, D], BF16)
        pr.dram_in("Wk1", [H, 128, D], F32)
        pr.dram_in("Wv1", [H, 128, D], F32)
        pr.dram_in("Wg1", [8, 128, D], F32)
        pr.dram_in("Wp1", [DT, 128, D], F32)
        pr.dram_in("latq", [H, 128, NL], F32)
        pr.dram_in("cos1", [128, 128], F32)
        pr.dram_in("sin1", [128, 128], F32)
        pr.dram_in("permT1", [128, 128], BF16)
        pr.dram_in("latmask", [64, 128], F32)
        gu_codesT = pr.dram_in("gu_codesT_s", [256, GU], BF16)
        dn_codesT = pr.dram_in("dn_codesT_s", [768, D], BF16)
        gu_scalesT = pr.dram_in("gu_scalesT_s", [256, GU], F16)
        dn_scalesT = pr.dram_in("dn_scalesT_s", [768, D], F16)
        guw = pr.dram_tmp("guw", [D, GU], BF16, shared=True)
        dnw = pr.dram_tmp("dnw", [6144, D], BF16, shared=True)
        guw_s = pr.dram_tmp("guw_s", [256, GU], BF16)
        dnw_s = pr.dram_tmp("dnw_s", [768, D], BF16)

        with pr.scope():
            ph0 = Ph(nc, pr=pr, name="prep")
            emit_ones(pr, ph0)
            emit_identity(pr, ph0)
            emit_lut_prep(pr, ph0)
            ph0.emit(es)
        lutB, cdiff = pr.t["lutB"], pr.t["cdiff"]
        with pr.scope():
            ph = Ph(nc, pr=pr, name="dq")
            pr.t["_shared_prep_ph"] = ph
            for tag, src, dst, nr_, nc_ in (
                    ("g", g_idxT, gw_s, 256, I0), ("u", u_idxT, uw_s, 256, I0),
                    ("d", d_idxT, dw_s, 768, D)):
                emit_dequant_ghost(pr, ph, src, dst, nr_, nc_, lutB, cdiff, tag)
            emit_dequant_fp4(pr, ph, gu_codesT, gu_scalesT, guw_s, 256, GU, "gu")
            emit_dequant_fp4(pr, ph, dn_codesT, dn_scalesT, dnw_s, 768, D, "dn")

        def gathers(ph):
            for shard, full in ((gw_s, gw), (uw_s, uw), (dw_s, dw),
                                (guw_s, guw), (dnw_s, dnw)):
                ph.op("gpsimd", lambda e, shard=shard, full=full: e.collective_compute(
                    "AllGather", OP.bypass, replica_groups=[list(range(8))],
                    ins=[shard.ap().opt()], outs=[full.ap().opt()]),
                    reads=[shard], writes=[full], inc=1)

        with pr.scope():
            emit_l0_attn(pr, es, x0, xA, pre_ops=gathers)
        pr.t["_shared_prep_ph"] = None
        with pr.scope():
            ph = Ph(nc, pr=pr, name="ffn0")
            emit_ffn(pr, ph, xA, xB, gw, 0, uw, 0, dw, I0, False, "f0")
            ph.emit(es)
        with pr.scope():
            emit_l1_attn(pr, es, xB, xA)
        with pr.scope():
            ph = Ph(nc, pr=pr, name="ffn1")
            emit_ffn(pr, ph, xA, xout, guw, 0, guw, I1, dnw, I1, True, "f1")
            ph.emit(es)
    _FUSED_INPUTS = set(Prog.input_names)
    Prog.input_names = None
    return nc


def kernel_fused(**inputs):
    n_cores = 8
    shared, per_core = host_prep(inputs, n_cores)
    nc = build_full_program()

    in_maps = []
    for c in range(n_cores):
        im = dict(shared)
        for k, v in per_core[c].items():
            im[k] = v
        im["xin"] = per_core[c]["x0"]
        im = {k: v for k, v in im.items() if _FUSED_INPUTS is None or k in _FUSED_INPUTS}
        in_maps.append(im)
    res = run_bass_kernel_spmd(nc, in_maps, core_ids=list(range(n_cores)))
    out = np.empty((16, T, D), np.float32)
    bpc = 16 // n_cores
    for c in range(n_cores):
        xf = np.asarray(res.results[c]["xout"])
        out[c * bpc:(c + 1) * bpc] = xf.reshape(D, NB, T).transpose(1, 2, 0)
    return out


# version 15
# speedup vs baseline: 1.0582x; 1.0582x over previous
import numpy as np
import ml_dtypes
import concourse.bass as bass
import concourse.mybir as mybir
from concourse.bass_utils import run_bass_kernel_spmd

F32 = mybir.dt.float32
BF16 = mybir.dt.bfloat16
F16 = mybir.dt.float16
AF = mybir.ActivationFunctionType
OP = mybir.AluOpType
AX = mybir.AxisListType
bf16 = ml_dtypes.bfloat16

ENGINES = ("sync", "scalar", "vector", "tensor", "gpsimd")


class Ph:

    def __init__(self, nc, name=None, pr=None):
        self.nc = nc
        self.name = name
        self.pr = pr
        self.ops = []
        if pr is not None:
            self.count = pr.scount
            self.chan_engine = pr.chan_engine
        else:
            self.count = {e: 0 for e in ENGINES}
            self.chan_engine = {}
        for e in ENGINES:
            self.count.setdefault(e, 0)
        self.last_writer = {}
        self.readers = {}
        self.my_chans = set()

    @staticmethod
    def _key(x):
        if isinstance(x, tuple):
            h, region = x
        else:
            h, region = x, None
        name = getattr(h, "name", None)
        if name is None:
            t = getattr(h, "tensor", None)
            name = getattr(t, "name", repr(h))
        return (name, region)

    def op(self, engine, fn, reads=(), writes=(), dma=False, inc=None,
           deps=(), chan=None):
        rkeys = [self._key(r) for r in reads]
        wkeys = [self._key(w) for w in writes]
        dep_toks = set(t for t in deps if t is not None)
        for k in rkeys:
            t = self.last_writer.get(k)
            if t is not None:
                dep_toks.add(t)
        for k in wkeys:
            t = self.last_writer.get(k)
            if t is not None:
                dep_toks.add(t)
            for t in self.readers.get(k, ()):
                dep_toks.add(t)
        if dma:
            assert wkeys, "dma op needs a write target"
            chan = ("dma", chan if chan is not None else wkeys[0][0])
            if chan not in self.count:
                self.count[chan] = 0
                self.chan_engine[chan] = engine
            assert self.chan_engine[chan] == engine, (chan, engine)
            self.my_chans.add(chan)
            n_inc = 16 if inc is None else inc
            self.count[chan] += n_inc
            tok = (chan, self.count[chan])
        else:
            n_inc = 1 if inc is None else inc
            self.count[engine] += n_inc
            tok = (engine, self.count[engine])
        rounded = set()
        for de, dv in dep_toks:
            if isinstance(de, tuple) and de[0] == "dma":
                rounded.add((de, self.count[de]))
            else:
                rounded.add((de, dv))
        dep_toks = {t for t in rounded if t != tok}
        self.ops.append((engine, fn, dep_toks, n_inc, tok))
        for k in rkeys:
            self.readers.setdefault(k, []).append(tok)
        for k in wkeys:
            self.last_writer[k] = tok
            self.readers[k] = []
        return tok

    def barrier_wait(self, engine, toks):
        self.ops.append((engine, None, set(t for t in toks if t), 0, None))

    def emit(self, stack):
        nc = self.nc
        if self.pr is not None:
            sems = {key: self.pr.sem(key) for key in self.count}
        else:
            sems = {}
            for key in self.count:
                nm = key if isinstance(key, str) else f"c_{key[1]}"
                sems[key] = stack.enter_context(
                    nc.semaphore(f"{self.name}_{nm}"[:60]))
        per_engine = {e: [] for e in ENGINES}
        for rec in self.ops:
            per_engine[rec[0]].append(rec)
        import contextlib
        scope = nc.named_scope(self.name) if self.name else contextlib.nullcontext()
        with scope, nc.Block(self.name) as blk:
            for e in ENGINES:
                recs = per_engine[e]
                drains = [(c, self.count[c]) for c, ce in self.chan_engine.items()
                          if ce == e and c in self.my_chans]
                if not recs and not drains:
                    continue

                def body(eng, recs=recs, e=e, drains=drains):
                    watermark = {}
                    for _, fn, deps, n_inc, tok in recs:
                        need = {}
                        for de, dv in deps:
                            need[de] = max(need.get(de, 0), dv)
                        for de in sorted(need, key=repr):
                            dv = need[de]
                            if watermark.get(de, 0) >= dv:
                                continue
                            eng.wait_ge(sems[de], dv)
                            watermark[de] = dv
                        if fn is not None:
                            ins = fn(eng)
                            if n_inc:
                                sem_key = tok[0] if tok else e
                                ins.then_inc(sems[sem_key], n_inc)
                    if self.count.get(e) and watermark.get(e, 0) < self.count[e]:
                        eng.wait_ge(sems[e], self.count[e])
                    for c, cv in drains:
                        if watermark.get(c, 0) < cv:
                            eng.wait_ge(sems[c], cv)
                getattr(blk, e)(body)
        return sems



D = 2048
DT = D // 128
NB = 2
T = 2048
NT = NB * T
RT = 512
NR = NT // RT
I0 = 5504
I1 = 5632
GU = 2 * I1
H = 16
DH = 128
HKV = 4
CH = 64
NL = 64
LNEPS = 1e-5
RMSEPS = 1e-6
NEG = -1.0e9
SCALE1 = 1.0 / float(np.sqrt(DH))


def host_prep(inputs, n_cores=8):
    f32 = np.float32

    shared = {}

    def tile_lhsT(A):
        K, M = A.shape
        return np.ascontiguousarray(
            A.reshape(K // 128, 128, M // 128, 128).transpose(2, 1, 0, 3)
            .reshape(M // 128, 128, K)).astype(np.float32)

    shared["wqT"] = tile_lhsT(inputs["L0_attn_wq"].T)
    shared["wkT"] = tile_lhsT(inputs["L0_attn_wk"].T)
    shared["wvT"] = tile_lhsT(inputs["L0_attn_wv"].T)
    shared["woT"] = tile_lhsT(inputs["L0_attn_wo"].T)

    shared["g_idxT"] = np.ascontiguousarray(inputs["L0_ffn_gate_idx"].T).astype(bf16)
    shared["u_idxT"] = np.ascontiguousarray(inputs["L0_ffn_up_idx"].T).astype(bf16)
    shared["d_idxT"] = np.ascontiguousarray(inputs["L0_ffn_down_idx"].T).astype(bf16)
    shared["lut"] = np.ascontiguousarray(inputs["L0_ffn_lut"].reshape(1, 16)).astype(f32)

    shared["Wk1"] = tile_lhsT(inputs["L1_attn_Wk"])
    shared["Wv1"] = tile_lhsT(inputs["L1_attn_Wv"])
    shared["Wg1"] = tile_lhsT(inputs["L1_attn_Wg"])
    shared["Wp1"] = tile_lhsT(inputs["L1_attn_Wp"])
    lq = np.asarray(inputs["L1_attn_latent_queries"])[0]
    shared["latq"] = np.ascontiguousarray(lq.transpose(1, 2, 0)).reshape(H, DH, NL).astype(f32)

    def unpack_T(packed):
        w = packed.astype(np.uint32)
        nib = np.zeros((w.shape[0], w.shape[1] * 8), np.uint8)
        for i in range(8):
            nib[:, i::8] = ((w >> np.uint32(4 * i)) & np.uint32(15)).astype(np.uint8)
        return np.ascontiguousarray(nib.T).astype(np.float32).astype(bf16)

    shared["gu_codesT"] = unpack_T(inputs["L1_ffn_gate_up_packed"])
    shared["dn_codesT"] = unpack_T(inputs["L1_ffn_down_packed"])
    shared["gu_scalesT"] = np.ascontiguousarray(
        np.repeat(np.asarray(inputs["L1_ffn_gate_up_scales"]), 128, axis=0))
    shared["dn_scalesT"] = np.ascontiguousarray(
        np.repeat(np.asarray(inputs["L1_ffn_down_scales"]), 128, axis=0))

    inv = 1.0 / 10000.0 ** (np.arange(0, DH, 2, dtype=np.float64) / DH)
    fr = np.arange(T, dtype=np.float64)[None, :] * inv[:, None]
    shared["cos0"] = np.concatenate([np.cos(fr), np.cos(fr)], 0).astype(f32)
    shared["sin0"] = np.concatenate([np.sin(fr), np.sin(fr)], 0).astype(f32)
    P0 = np.zeros((DH, DH), np.float32)
    for i in range(64):
        P0[i, i + 64] = -1.0
        P0[i + 64, i] = 1.0
    shared["permT0"] = np.ascontiguousarray(P0.T).astype(bf16)
    inv1 = 1.0 / 10000.0 ** (np.arange(0, 64, dtype=np.float64) * 2.0 / DH)
    ang = np.arange(128, dtype=np.float64)[None, :] * inv1[:, None]
    c1 = np.zeros((DH, 128), np.float64); s1_ = np.zeros((DH, 128), np.float64)
    c1[0::2] = np.cos(ang); c1[1::2] = np.cos(ang)
    s1_[0::2] = np.sin(ang); s1_[1::2] = np.sin(ang)
    shared["cos1"] = c1.astype(f32)
    shared["sin1"] = s1_.astype(f32)
    P1 = np.zeros((DH, DH), np.float32)
    for i in range(64):
        P1[2 * i, 2 * i + 1] = -1.0
        P1[2 * i + 1, 2 * i] = 1.0
    shared["permT1"] = np.ascontiguousarray(P1.T).astype(bf16)
    lam = np.exp2(-8.0 * np.arange(1, H + 1, dtype=np.float64) / H)
    r = np.arange(CH, dtype=np.float64)
    diagT = np.zeros((CH, H * CH), np.float32)
    qdec = np.zeros((DH, H * CH), np.float32)
    kdec = np.zeros((DH, H * CH), np.float32)
    for h in range(H):
        dd = r[None, :] - r[:, None]
        diagT[:, h * CH:(h + 1) * CH] = np.where(dd >= 0, np.exp(-lam[h] * dd), 0.0)
        qdec[:, h * CH:(h + 1) * CH] = np.exp(-lam[h] * (r + 1.0))[None, :]
        kdec[:, h * CH:(h + 1) * CH] = np.exp(-lam[h] * (CH - 1.0 - r))[None, :]
    shared["diagT"] = diagT
    shared["qdec"] = qdec
    shared["kdec"] = kdec
    inv0 = 1.0 / 10000.0 ** (np.arange(0, DH, 2, dtype=np.float64) / DH)
    cc, jj, ii = np.meshgrid(np.arange(T // CH), np.arange(CH), inv0,
                             indexing="ij")
    ang0 = (cc * CH + jj) * ii
    shared["cos0tm"] = np.ascontiguousarray(
        np.cos(ang0).transpose(1, 0, 2).reshape(CH, -1)).astype(f32)
    shared["sin0tm"] = np.ascontiguousarray(
        np.sin(ang0).transpose(1, 0, 2).reshape(CH, -1)).astype(f32)
    kdecT = np.zeros((CH, H), np.float64)
    for h in range(H):
        kdecT[:, h] = np.exp(-lam[h] * (CH - 1.0 - r))
    shared["kdecT"] = kdecT.astype(f32)
    shared["kdecTM"] = np.ascontiguousarray(
        np.repeat(kdecT.T[:, :, None], DH, axis=2).transpose(1, 0, 2)
        .reshape(CH, H * DH)).astype(f32)
    shared["blk_dec"] = np.exp(-lam * CH).astype(np.float64)
    lmask = np.where(np.arange(128)[None, :] <= np.arange(NL)[:, None], 0.0, NEG)
    shared["latmask"] = lmask.astype(f32)

    x = np.asarray(inputs["L0_attn_x"])
    def padrows(a, rows):
        out = np.zeros((rows, a.shape[1]), a.dtype)
        out[: a.shape[0]] = a
        return out
    d_pad = padrows(shared["d_idxT"], 6144)
    dn_pad = padrows(shared["dn_codesT"], 6144)
    dns_pad = padrows(shared["dn_scalesT"], 6144)
    per_core = []
    bpc = x.shape[0] // n_cores
    for c in range(n_cores):
        xc = x[c * bpc:(c + 1) * bpc]
        xfm = np.ascontiguousarray(xc.transpose(2, 0, 1).reshape(D, -1))
        pc = {"x0": xfm}
        pc["g_idxT_s"] = np.ascontiguousarray(shared["g_idxT"][c * 256:(c + 1) * 256])
        pc["u_idxT_s"] = np.ascontiguousarray(shared["u_idxT"][c * 256:(c + 1) * 256])
        pc["d_idxT_s"] = np.ascontiguousarray(d_pad[c * 768:(c + 1) * 768])
        pc["gu_codesT_s"] = np.ascontiguousarray(shared["gu_codesT"][c * 256:(c + 1) * 256])
        pc["gu_scalesT_s"] = np.ascontiguousarray(shared["gu_scalesT"][c * 256:(c + 1) * 256])
        pc["dn_codesT_s"] = np.ascontiguousarray(dn_pad[c * 768:(c + 1) * 768])
        pc["dn_scalesT_s"] = np.ascontiguousarray(dns_pad[c * 768:(c + 1) * 768])
        per_core.append(pc)
    return shared, per_core


def gather_out(results):
    outs = []
    for r in results:
        xfm = np.asarray(r["xout"])
        outs.append(xfm.reshape(D, NB, T).transpose(1, 2, 0))
    return np.concatenate(outs, 0).astype(np.float32)


from contextlib import ExitStack, contextmanager


class Prog:

    def __init__(self, nc, stack):
        self.nc = nc
        self.es = stack
        self.stack = stack
        self.t = {}
        self.scount = {}
        self.chan_engine = {}
        self._sems = {}

    def sem(self, key):
        if key not in self._sems:
            nm = key if isinstance(key, str) else f"c_{key[1]}"
            self._sems[key] = self.es.enter_context(
                self.nc.semaphore(str(nm)[:60]))
        return self._sems[key]

    @contextmanager
    def scope(self):
        old = self.stack
        with ExitStack() as s:
            self.stack = s
            try:
                yield
            finally:
                self.stack = old

    input_names = None

    def dram_in(self, name, shape, dt):
        self.t[name] = self.nc.dram_tensor(name, shape, dt, kind="ExternalInput")
        if Prog.input_names is not None:
            Prog.input_names.append(name)
        return self.t[name]

    def dram_out(self, name, shape, dt):
        self.t[name] = self.nc.dram_tensor(name, shape, dt, kind="ExternalOutput")
        return self.t[name]

    def dram_tmp(self, name, shape, dt, shared=False):
        kw = {"addr_space": "Shared"} if shared else {}
        self.t[name] = self.nc.dram_tensor(name, shape, dt, **kw)
        return self.t[name]

    def sbuf(self, name, shape, dt, persistent=False):
        st = self.es if persistent else self.stack
        h = st.enter_context(self.nc.sbuf_tensor(name, shape, dt))
        self.t[name] = h
        return h

    def psum(self, name, shape, dt=F32):
        h = self.stack.enter_context(self.nc.psum_tensor(name, shape, dt))
        self.t[name] = h
        return h


def emit_dequant_ghost(pr, ph, src, dst, nrows, ncols, lutB, cdiff, tag):
    nc = pr.nc
    CC = 1376 if ncols % 1376 == 0 else 1024
    idx = [pr.sbuf(f"dq_{tag}_idx{i}", [128, CC], BF16) for i in range(2)]
    tmp = pr.sbuf(f"dq_{tag}_tmp", [128, CC], BF16)
    p = [pr.sbuf(f"dq_{tag}_p{i}", [128, CC], BF16) for i in range(4)]
    s01 = pr.sbuf(f"dq_{tag}_s01", [128, CC], F32)
    s23 = pr.sbuf(f"dq_{tag}_s23", [128, CC], F32)
    w = [pr.sbuf(f"dq_{tag}_w{i}", [128, CC], BF16) for i in range(2)]

    n_rt = nrows // 128
    chunks = []
    c0 = 0
    while c0 < ncols:
        cw = min(CC, ncols - c0)
        chunks.append((c0, cw))
        c0 += cw

    for rt in range(n_rt):
        for ci, (c0, cw) in enumerate(chunks):
            ib = idx[(rt * len(chunks) + ci) % 2]
            wb = w[(rt * len(chunks) + ci) % 2]
            ph.op("sync", lambda e, ib=ib, rt=rt, c0=c0, cw=cw: e.dma_start(
                out=ib[:, 0:cw], in_=src[rt * 128:(rt + 1) * 128, c0:c0 + cw]),
                reads=[], writes=[ib], dma=True)
            for g in range(4):
                j0 = 4 * g
                ph.op("vector", lambda e, pg=p[g], ib=ib, j0=j0, cw=cw: e.tensor_scalar(
                    pg[:, 0:cw], ib[:, 0:cw], float(j0), cdiff[:, j0:j0 + 1],
                    OP.is_ge, OP.mult), reads=[ib, cdiff], writes=[p[g]])
                for j in (j0 + 1, j0 + 2, j0 + 3):
                    ph.op("vector", lambda e, ib=ib, j=j, cw=cw: e.tensor_scalar(
                        tmp[:, 0:cw], ib[:, 0:cw], float(j), cdiff[:, j:j + 1],
                        OP.is_ge, OP.mult), reads=[ib, cdiff], writes=[tmp])
                    ph.op("vector", lambda e, pg=p[g], cw=cw: e.tensor_add(
                        pg[:, 0:cw], pg[:, 0:cw], tmp[:, 0:cw]),
                        reads=[tmp, p[g]], writes=[p[g]])
            ph.op("vector", lambda e, cw=cw: e.tensor_add(
                s01[:, 0:cw], p[0][:, 0:cw], p[1][:, 0:cw]),
                reads=[p[0], p[1]], writes=[s01])
            ph.op("vector", lambda e, cw=cw: e.tensor_add(
                s23[:, 0:cw], p[2][:, 0:cw], p[3][:, 0:cw]),
                reads=[p[2], p[3]], writes=[s23])
            ph.op("vector", lambda e, wb=wb, cw=cw: e.tensor_add(
                wb[:, 0:cw], s01[:, 0:cw], s23[:, 0:cw]),
                reads=[s01, s23], writes=[wb])
            ph.op("gpsimd", lambda e, wb=wb, rt=rt, c0=c0, cw=cw: e.dma_start(
                out=dst[rt * 128:(rt + 1) * 128, c0:c0 + cw], in_=wb[:, 0:cw]),
                reads=[wb], writes=[(dst, (rt, c0))], dma=True,
                chan=f"dqs_{tag}_{(rt * len(chunks) + ci) % 2}")


_FP4_VALS = [0.0, 0.5, 1.0, 1.5, 2.0, 3.0, 4.0, 6.0,
             -0.0, -0.5, -1.0, -1.5, -2.0, -3.0, -4.0, -6.0]


def emit_dequant_fp4(pr, ph, src, scales, dst, nrows, ncols, tag):
    nc = pr.nc
    CC = 1024
    idx = [pr.sbuf(f"dq_{tag}_idx{i}", [128, CC], BF16) for i in range(2)]
    sc = [pr.sbuf(f"dq_{tag}_sc{i}", [128, CC], F16) for i in range(2)]
    tmp = pr.sbuf(f"dq_{tag}_tmp", [128, CC], BF16)
    acc = pr.sbuf(f"dq_{tag}_acc", [128, CC], BF16)
    w = [pr.sbuf(f"dq_{tag}_w{i}", [128, CC], BF16) for i in range(2)]
    diffs = [_FP4_VALS[j] - _FP4_VALS[j - 1] for j in range(1, 16)]

    n_rt = nrows // 128
    chunks = []
    c0 = 0
    while c0 < ncols:
        cw = min(CC, ncols - c0)
        chunks.append((c0, cw))
        c0 += cw

    for rt in range(n_rt):
        for ci, (c0, cw) in enumerate(chunks):
            par = (rt * len(chunks) + ci) % 2
            ib, sb, wb = idx[par], sc[par], w[par]
            ph.op("sync", lambda e, ib=ib, rt=rt, c0=c0, cw=cw: e.dma_start(
                out=ib[:, 0:cw], in_=src[rt * 128:(rt + 1) * 128, c0:c0 + cw]),
                writes=[ib], dma=True)
            ph.op("sync", lambda e, sb=sb, rt=rt, c0=c0, cw=cw: e.dma_start(
                out=sb[:, 0:cw], in_=scales[rt * 128:(rt + 1) * 128, c0:c0 + cw]),
                writes=[sb], dma=True)
            ph.op("vector", lambda e, ib=ib, cw=cw: e.tensor_scalar(
                acc[:, 0:cw], ib[:, 0:cw], 1.0, float(diffs[0]), OP.is_ge, OP.mult),
                reads=[ib], writes=[acc])
            for j in range(2, 16):
                dj = float(diffs[j - 1])
                ph.op("vector", lambda e, ib=ib, j=j, dj=dj, cw=cw: e.tensor_scalar(
                    tmp[:, 0:cw], ib[:, 0:cw], float(j), dj, OP.is_ge, OP.mult),
                    reads=[ib], writes=[tmp])
                ph.op("vector", lambda e, cw=cw: e.tensor_add(
                    acc[:, 0:cw], acc[:, 0:cw], tmp[:, 0:cw]),
                    reads=[tmp, acc], writes=[acc])
            ph.op("vector", lambda e, wb=wb, sb=sb, cw=cw: e.tensor_mul(
                wb[:, 0:cw], acc[:, 0:cw], sb[:, 0:cw]),
                reads=[acc, sb], writes=[wb])
            ph.op("gpsimd", lambda e, wb=wb, rt=rt, c0=c0, cw=cw: e.dma_start(
                out=dst[rt * 128:(rt + 1) * 128, c0:c0 + cw], in_=wb[:, 0:cw]),
                reads=[wb], writes=[(dst, (rt, c0))], dma=True,
                chan=f"dqs_{tag}_{par}")

def emit_ffn(pr, ph, xin, xout, g_scr, g_off, u_scr, u_off, dn_scr, inter,
             rms, tag, out_toks=None):
    nc = pr.nc
    n_oi = inter // 128
    ones_bf = pr.t["ones_bf"]
    ones_f32 = pr.t["ones_f32"]

    xb2 = [[pr.sbuf(f"ffn_{tag}_xb{p}_{i}", [128, RT], BF16) for i in range(DT)]
           for p in range(2)]
    harr = [pr.sbuf(f"ffn_{tag}_h{i}", [128, RT], BF16) for i in range(n_oi)]
    z = [pr.sbuf(f"ffn_{tag}_z{i}", [128, RT], BF16) for i in range(DT)]
    nq = 2 if rms else DT
    zsq = [pr.sbuf(f"ffn_{tag}_q{i}", [128, RT], BF16) for i in range(nq)]
    sg = [pr.sbuf(f"ffn_{tag}_sg{i}", [128, RT], BF16) for i in range(4)]
    wg = [pr.sbuf(f"ffn_{tag}_wg{i}", [128, DT * RT], BF16) for i in range(2)]
    wu = [pr.sbuf(f"ffn_{tag}_wu{i}", [128, DT * RT], BF16) for i in range(2)]
    wdn = [pr.sbuf(f"ffn_{tag}_wd{i}", [128, 4 * RT], BF16) for i in range(2)]
    if rms:
        yarr = [pr.sbuf(f"ffn_{tag}_y{i}", [128, RT], BF16) for i in range(DT)]
    utmp = [pr.sbuf(f"ffn_{tag}_u{i}", [128, RT], F32) for i in range(3)]
    ost = [pr.sbuf(f"ffn_{tag}_o{i}", [128, RT], F32) for i in range(3)]
    rows = pr.sbuf(f"ffn_{tag}_rows", [1, 3 * RT], F32)

    work = [pr.psum(f"ffn_{tag}_pw{i}", [128, RT]) for i in range(4)]
    bc1 = pr.psum(f"ffn_{tag}_bc1", [128, RT])
    bc2 = pr.psum(f"ffn_{tag}_bc2", [128, RT])
    st1 = pr.psum(f"ffn_{tag}_st1", [1, RT])
    st2 = pr.psum(f"ffn_{tag}_st2", [1, RT])

    groups = [list(range(g, min(g + 4, n_oi))) for g in range(0, n_oi, 4)]

    def stat_mm(bank, rhs, first, last):
        def fn(e, bank=bank, rhs=rhs, first=first, last=last):
            return e.matmul(bank[:], ones_bf[:, 0:1], rhs[:],
                            start=first, stop=last)
        return fn

    def load_x(r):
        if r >= NR:
            return
        t0 = r * RT
        xb = xb2[r % 2]
        for ot in range(DT):
            ph.op("gpsimd", lambda e, ot=ot, t0=t0, xb=xb: e.dma_start(
                out=xb[ot][:], in_=xin[ot * 128:(ot + 1) * 128, t0:t0 + RT]),
                writes=[xb[ot]], dma=True, chan=f"ffn_{tag}_xbg{r % 2}")

    load_x(0)
    for r in range(NR):
        t0 = r * RT
        xb = xb2[r % 2]

        def wload(dst, scr, off, g0, ncols):
            rs = scr[:].ap[0][0]
            src_ap = bass.AP(scr, off + g0 * 128,
                             [[rs, 128], [128 * rs, DT], [1, ncols]])
            dst_ap = bass.AP(dst, 0, [[dst[:].ap[0][0], 128], [RT, DT], [1, ncols]])
            return lambda e: e.dma_start(out=dst_ap, in_=src_ap)

        for gi, G in enumerate(groups):
            g0 = G[0]
            ncols = len(G) * 128
            wgb = wg[gi % 2]
            wub = wu[gi % 2]
            ph.op("sync", wload(wgb, g_scr, g_off, g0, ncols),
                  writes=[wgb], dma=True)
            ph.op("sync", wload(wub, u_scr, u_off, g0, ncols),
                  writes=[wub], dma=True)
            for j, oi in enumerate(G):
                def gmm(e, j=j, wgb=wgb, xb=xb):
                    ins = None
                    for di in range(DT):
                        ins = e.matmul(work[j][:],
                                       wgb[:, di * RT + j * 128:di * RT + (j + 1) * 128],
                                       xb[di][:], start=(di == 0), stop=(di == DT - 1))
                    return ins
                ph.op("tensor", gmm,
                      reads=[wgb] + [xb[di] for di in range(DT)],
                      writes=[work[j]])
                ph.op("scalar", lambda e, j=j: e.activation(
                    sg[j][:], work[j][:], AF.Silu), reads=[work[j]], writes=[sg[j]])
            for j, oi in enumerate(G):
                def umm(e, j=j, wub=wub, xb=xb):
                    ins = None
                    for di in range(DT):
                        ins = e.matmul(work[j][:],
                                       wub[:, di * RT + j * 128:di * RT + (j + 1) * 128],
                                       xb[di][:], start=(di == 0), stop=(di == DT - 1))
                    return ins
                ph.op("tensor", umm,
                      reads=[wub] + [xb[di] for di in range(DT)],
                      writes=[work[j]])
                ph.op("vector", lambda e, j=j, oi=oi: e.tensor_mul(
                    harr[oi][:], sg[j][:], work[j][:]),
                    reads=[sg[j], work[j]], writes=[harr[oi]])

        kquads = [list(range(k, min(k + 4, n_oi))) for k in range(0, n_oi, 4)]
        for go in range(4):
            ots = [4 * go + i for i in range(4)]
            for qi, KQ in enumerate(kquads):
                wdb = wdn[qi % 2]
                nk = len(KQ)
                k0 = KQ[0]
                def dload(e, wdb=wdb, k0=k0, nk=nk, go=go):
                    rs = dn_scr[:].ap[0][0]
                    src_ap = bass.AP(dn_scr, k0 * 128 * rs + go * RT,
                                     [[rs, 128], [128 * rs, nk], [1, RT]])
                    dst_ap = bass.AP(wdb, 0, [[wdb[:].ap[0][0], 128], [RT, nk], [1, RT]])
                    return e.dma_start(out=dst_ap, in_=src_ap)
                ph.op("sync", dload, writes=[wdb], dma=True)
                def dmm(e, wdb=wdb, KQ=KQ, ots=ots):
                    ins = None
                    for kk, ki in enumerate(KQ):
                        for j, ot in enumerate(ots):
                            ins = e.matmul(work[j][:],
                                           wdb[:, kk * RT + j * 128:kk * RT + (j + 1) * 128],
                                           harr[ki][:], start=(ki == 0),
                                           stop=(ki == n_oi - 1))
                    return ins
                ph.op("tensor", dmm,
                      reads=[wdb] + [harr[ki] for ki in KQ],
                      writes=[work[j] for j in range(4)])
            for j, ot in enumerate(ots):
                q = zsq[ot % nq]
                if rms:
                    ph.op("vector", lambda e, j=j, ot=ot: e.tensor_copy(
                        yarr[ot][:], work[j][:]), reads=[work[j]], writes=[yarr[ot]])
                    ph.op("scalar", lambda e, ot=ot, q=q: e.activation(
                        q[:], yarr[ot][:], AF.Square),
                        reads=[yarr[ot]], writes=[q])
                    ph.op("tensor", stat_mm(st2, q, ot == 0, ot == DT - 1),
                          reads=[q, ones_bf], writes=[(st2, "acc")])
                else:
                    ph.op("vector", lambda e, j=j, ot=ot, xb=xb: e.scalar_tensor_tensor(
                        z[ot][:], work[j][:], 1.0, xb[ot][:], OP.mult, OP.add),
                        reads=[work[j], xb[ot]], writes=[z[ot]])
                    ph.op("scalar", lambda e, ot=ot, q=q: e.activation(
                        q[:], z[ot][:], AF.Square),
                        reads=[z[ot]], writes=[q])
                    ph.op("tensor", stat_mm(st1, z[ot], ot == 0, ot == DT - 1),
                          reads=[z[ot], ones_bf], writes=[(st1, "acc")])
                    ph.op("tensor", stat_mm(st2, q, ot == 0, ot == DT - 1),
                          reads=[q, ones_bf], writes=[(st2, "acc")])

        R = rows
        if rms:
            ph.op("vector", lambda e: e.tensor_scalar(
                R[:, 0:RT], st2[:], 1.0 / D, RMSEPS, OP.mult, OP.add),
                reads=[(st2, "acc")], writes=[(rows, "A")])
            ph.op("vector", lambda e: e.reciprocal(R[:, RT:2 * RT], R[:, 0:RT]),
                reads=[(rows, "A")], writes=[(rows, "B")])
            ph.op("scalar", lambda e: e.activation(
                R[:, 2 * RT:3 * RT], R[:, RT:2 * RT], AF.Sqrt),
                reads=[(rows, "B")], writes=[(rows, "C")])
            ph.op("tensor", lambda e: e.matmul(
                bc1[:], ones_f32[0:1, :], R[:, 2 * RT:3 * RT],
                start=True, stop=True), reads=[(rows, "C"), ones_f32], writes=[bc1])
            for ot in range(DT):
                q = zsq[ot % nq]
                ph.op("vector", lambda e, ot=ot: e.tensor_mul(
                    utmp[ot % 2][:], yarr[ot][:], bc1[:]),
                    reads=[yarr[ot], bc1], writes=[utmp[ot % 2]])
                ph.op("vector", lambda e, ot=ot, xb=xb: e.tensor_add(
                    z[ot][:], utmp[ot % 2][:], xb[ot][:]),
                    reads=[utmp[ot % 2], xb[ot]], writes=[z[ot]])
                ph.op("scalar", lambda e, ot=ot, q=q: e.activation(
                    q[:], z[ot][:], AF.Square),
                    reads=[z[ot]], writes=[q])
                ph.op("tensor", stat_mm(st1, z[ot], ot == 0, ot == DT - 1),
                      reads=[z[ot], ones_bf], writes=[(st1, "acc")])
                ph.op("tensor", stat_mm(st2, q, ot == 0, ot == DT - 1),
                      reads=[q, ones_bf], writes=[(st2, "acc")])

        load_x(r + 1)
        emit_ln_tail(pr, ph, tag, z, zsq, st1, st2, bc1, bc2, rows,
                     utmp, ost, xout, r, out_toks)


def emit_lut_prep(pr, ph):
    nc = pr.nc
    lut_s = pr.sbuf("lut_s", [1, 16], F32, persistent=True)
    lutB = pr.sbuf("lutB", [128, 16], F32, persistent=True)
    cdiff = pr.sbuf("cdiff", [128, 16], F32, persistent=True)
    pb = pr.psum("lut_pb", [128, 16])
    ph.op("sync", lambda e: e.dma_start(out=lut_s[:], in_=pr.t["lut"][:, :]),
          writes=[lut_s], dma=True)
    ph.op("tensor", lambda e: e.matmul(
        pb[:], pr.t["ones_f32"][0:1, :], lut_s[:], start=True, stop=True),
        reads=[lut_s, pr.t["ones_f32"]], writes=[pb])
    ph.op("vector", lambda e: e.tensor_copy(lutB[:], pb[:]), reads=[pb], writes=[lutB])
    ph.op("vector", lambda e: e.tensor_copy(cdiff[:, 0:1], lutB[:, 0:1]),
          reads=[lutB], writes=[(cdiff, 0)])
    ph.op("vector", lambda e: e.tensor_sub(cdiff[:, 1:16], lutB[:, 1:16], lutB[:, 0:15]),
          reads=[lutB], writes=[(cdiff, 1)])
    return lutB, cdiff


def emit_ones(pr, ph):
    ones_bf = pr.sbuf("ones_bf", [128, 1], BF16, persistent=True)
    ones_f32 = pr.sbuf("ones_f32", [1, 128], F32, persistent=True)
    ph.op("gpsimd", lambda e: e.memset(ones_bf[:], 1.0), writes=[ones_bf])
    ph.op("gpsimd", lambda e: e.memset(ones_f32[:], 1.0), writes=[ones_f32])


def build_l0_ffn_program():
    nc = bass.Bass("TRN2", target_bir_lowering=False)
    with ExitStack() as es:
        pr = Prog(nc, es)
        xin = pr.dram_in("xin", [D, NT], F32)
        xout = pr.dram_out("xout", [D, NT], F32)
        g_idxT = pr.dram_in("g_idxT", [D, I0], BF16)
        u_idxT = pr.dram_in("u_idxT", [D, I0], BF16)
        d_idxT = pr.dram_in("d_idxT", [I0, D], BF16)
        pr.dram_in("lut", [1, 16], F32)
        gw = pr.dram_tmp("gw", [D, I0], BF16, shared=True)
        uw = pr.dram_tmp("uw", [D, I0], BF16, shared=True)
        dw = pr.dram_tmp("dw", [I0, D], BF16)

        with pr.scope():
            ph0 = Ph(nc, pr=pr, name="prep")
            emit_ones(pr, ph0)
            emit_lut_prep(pr, ph0)
            ph0.emit(es)
        lutB, cdiff = pr.t["lutB"], pr.t["cdiff"]
        for tag, src, dst, nr_, nc_ in (
            ("g", g_idxT, gw, D, I0),
            ("u", u_idxT, uw, D, I0),
            ("d", d_idxT, dw, I0, D),
        ):
            with pr.scope():
                ph = Ph(nc, pr=pr, name="dq_" + tag)
                emit_dequant_ghost(pr, ph, src, dst, nr_, nc_, lutB, cdiff, tag)
                ph.emit(es)
        with pr.scope():
            ph = Ph(nc, pr=pr, name="ffn0")
            emit_ffn(pr, ph, xin, xout, gw, 0, uw, 0, dw, I0, False, "f0")
            ph.emit(es)
    return nc


def build_l1_ffn_program(debug_w=False):
    nc = bass.Bass("TRN2", target_bir_lowering=False)
    with ExitStack() as es:
        pr = Prog(nc, es)
        xin = pr.dram_in("xin", [D, NT], F32)
        xout = pr.dram_out("xout", [D, NT], F32)
        gu_codesT = pr.dram_in("gu_codesT", [D, GU], BF16)
        dn_codesT = pr.dram_in("dn_codesT", [I1, D], BF16)
        gu_scalesT = pr.dram_in("gu_scalesT", [D, GU], F16)
        dn_scalesT = pr.dram_in("dn_scalesT", [I1, D], F16)
        mk_w = pr.dram_out if debug_w else pr.dram_tmp
        guw = mk_w("guw", [D, GU], BF16)
        dnw = mk_w("dnw", [I1, D], BF16)

        with pr.scope():
            ph0 = Ph(nc, pr=pr, name="prep")
            emit_ones(pr, ph0)
            ph0.emit(es)
        with pr.scope():
            ph = Ph(nc, pr=pr, name="dq_gu")
            emit_dequant_fp4(pr, ph, gu_codesT, gu_scalesT, guw, D, GU, "gu")
            ph.emit(es)
        with pr.scope():
            ph = Ph(nc, pr=pr, name="dq_dn")
            emit_dequant_fp4(pr, ph, dn_codesT, dn_scalesT, dnw, I1, D, "dn")
            ph.emit(es)
        with pr.scope():
            ph = Ph(nc, pr=pr, name="ffn1")
            emit_ffn(pr, ph, xin, xout, guw, 0, guw, I1, dnw, I1, True, "f1")
            ph.emit(es)
    return nc

def emit_identity(pr, ph):
    ident = pr.sbuf("ident", [128, 128], BF16, persistent=True)
    ph.op("gpsimd", lambda e: e.iota(ident[:], [[1, 128]], base=0,
          channel_multiplier=-1, allow_small_or_imprecise_dtypes=True),
          writes=[ident])
    ph.op("vector", lambda e: e.tensor_scalar(ident[:], ident[:], 0.0, 1.0,
          OP.is_equal, OP.mult), reads=[ident], writes=[ident])
    return ident


def emit_ln_tail(pr, ph, tag, z, zsq_pair, st1, st2, bc1, bc2, rows, utmp, ost,
                 xout, r, out_toks):
    ones_f32 = pr.t["ones_f32"]
    A = rows[:, 0:RT]; B = rows[:, RT:2 * RT]; C = rows[:, 2 * RT:3 * RT]
    kA, kB, kC = (rows, "A"), (rows, "B"), (rows, "C")
    ph.op("vector", lambda e: e.tensor_scalar_mul(A, st1[:], 1.0 / D),
          reads=[(st1, "acc")], writes=[kA])
    ph.op("vector", lambda e: e.tensor_scalar_mul(B, st2[:], 1.0 / D),
          reads=[(st2, "acc")], writes=[kB])
    ph.op("vector", lambda e: e.tensor_mul(C, A, A), reads=[kA], writes=[kC])
    ph.op("vector", lambda e: e.tensor_sub(B, B, C), reads=[kB, kC], writes=[kB])
    ph.op("vector", lambda e: e.tensor_scalar_add(B, B, LNEPS), reads=[kB], writes=[kB])
    ph.op("vector", lambda e: e.reciprocal(C, B), reads=[kB], writes=[kC])
    ph.op("scalar", lambda e: e.activation(B, C, AF.Sqrt), reads=[kC], writes=[kB])
    ph.op("tensor", lambda e: e.matmul(
        bc1[:], ones_f32[0:1, :], A, start=True, stop=True),
        reads=[kA, ones_f32], writes=[bc1])
    ph.op("tensor", lambda e: e.matmul(
        bc2[:], ones_f32[0:1, :], B, start=True, stop=True),
        reads=[kB, ones_f32], writes=[bc2])
    t0 = r * RT
    nu, no = len(utmp), len(ost)
    for ot in range(DT):
        u = utmp[ot % nu]
        o = ost[ot % no]
        ph.op("vector", lambda e, ot=ot, u=u: e.scalar_tensor_tensor(
            u[:], z[ot][:], 1.0, bc1[:], OP.mult, OP.subtract),
            reads=[z[ot], bc1], writes=[u])
        ph.op("vector", lambda e, ot=ot, u=u, o=o: e.tensor_mul(
            o[:], u[:], bc2[:]),
            reads=[u, bc2], writes=[o])
        tok = ph.op("gpsimd", lambda e, ot=ot, t0=t0, o=o: e.dma_start(
            out=xout[ot * 128:(ot + 1) * 128, t0:t0 + RT], in_=o[:]),
            reads=[o], writes=[(xout, (ot, r))], dma=True,
            chan=f"xo_{tag}_{ot % no}")
        if out_toks is not None:
            out_toks.append(tok)


def emit_l1_attn(pr, es, xin, xout, out_toks=None):
    nc = pr.nc
    Wk1, Wv1, Wg1, Wp1 = pr.t["Wk1"], pr.t["Wv1"], pr.t["Wg1"], pr.t["Wp1"]
    wpb1 = pr.dram_tmp("wpb1", [DT, 128, D], BF16)
    latq, cos1d, sin1d, permT1d, latmaskd = (
        pr.t["latq"], pr.t["cos1"], pr.t["sin1"], pr.t["permT1"], pr.t["latmask"])

    attnb = pr.sbuf("a1_attnb", [64, NB * H * 128], BF16)
    latqb = pr.sbuf("a1_latq", [128, H * NL], BF16)
    cos1 = pr.sbuf("a1_cos1", [128, 128], F32)
    sin1 = pr.sbuf("a1_sin1", [128, 128], F32)
    permT1 = pr.sbuf("a1_permT1", [128, 128], BF16)
    latmask = pr.sbuf("a1_mask", [64, 128], F32)

    with pr.scope():
        ph = Ph(nc, pr=pr, name="a1kv")
        ident = pr.t["ident"]
        karr = [pr.sbuf(f"a1_k{b}", [128, H * 128], BF16) for b in range(NB)]
        varr = [pr.sbuf(f"a1_v{b}", [128, H * 128], BF16) for b in range(NB)]
        stage = [pr.sbuf(f"a1kv_st{i}", [128, 2048], F32) for i in range(2)]
        wkv = [pr.sbuf(f"a1kv_w{i}", [128, 2048], BF16) for i in range(2)]
        xk = [pr.sbuf(f"a1kv_x{b}", [128, DT * 128], BF16) for b in range(NB)]
        xst = [pr.sbuf(f"a1kv_xs{i}", [128, 128], F32) for i in range(2)]
        kev = pr.sbuf("a1kv_kev", [128, 128], BF16)
        tc = pr.sbuf("a1kv_tc", [128, 128], BF16)
        ts = pr.sbuf("a1kv_ts", [128, 128], BF16)
        sm = pr.sbuf("a1kv_sm", [64, 128], F32)
        ngm = pr.sbuf("a1kv_ngm", [64, 1], F32)
        wexp = pr.sbuf("a1kv_wexp", [64, 128], BF16)
        ssum = pr.sbuf("a1kv_ssum", [64, 1], F32)
        rs = pr.sbuf("a1kv_rs", [64, 1], F32)
        wn = pr.sbuf("a1kv_wn", [64, 128], BF16)
        wT = pr.sbuf("a1kv_wT", [128, 64], BF16)
        vT = pr.sbuf("a1kv_vT", [128, 128], BF16)
        pk = pr.psum("a1kv_pk", [128, 128])
        pr_ = pr.psum("a1kv_pr", [128, 128])
        psc = pr.psum("a1kv_psc", [64, 128])
        ptw = pr.psum("a1kv_ptw", [128, 64], BF16)
        ptv = pr.psum("a1kv_ptv", [128, 128], BF16)
        pat = pr.psum("a1kv_pat", [64, 128])

        ph.op("sync", lambda e: e.dma_start(out=cos1[:], in_=cos1d[:, :]),
              writes=[cos1], dma=True)
        ph.op("sync", lambda e: e.dma_start(out=sin1[:], in_=sin1d[:, :]),
              writes=[sin1], dma=True)
        ph.op("sync", lambda e: e.dma_start(out=permT1[:], in_=permT1d[:, :]),
              writes=[permT1], dma=True)
        ph.op("sync", lambda e: e.dma_start(out=latmask[:], in_=latmaskd[:, :]),
              writes=[latmask], dma=True)
        lqs = stage[0]
        ph.op("sync", lambda e: e.dma_start(
            out=lqs[:, 0:H * NL],
            in_=bass.AP(latq, 0, [[NL, 128], [128 * NL, H], [1, NL]])),
            writes=[stage[0]], dma=True)
        ph.op("vector", lambda e: e.tensor_copy(latqb[:], lqs[:, 0:H * NL]),
              reads=[stage[0]], writes=[latqb])

        for b in range(NB):
            for di in range(DT):
                xs = xst[di % 2]
                ph.op("gpsimd", lambda e, xs=xs, b=b, di=di: e.dma_start(
                    out=xs[:], in_=xin[di * 128:(di + 1) * 128,
                                       b * T:b * T + 128]),
                    writes=[xs], dma=True)
                ph.op("vector", lambda e, xs=xs, b=b, di=di: e.tensor_copy(
                    xk[b][:, di * 128:(di + 1) * 128], xs[:]),
                    reads=[xs], writes=[(xk[b], di)])

        wpc = pr.sbuf("a1kv_wpc", [128, D], BF16)
        for mt in range(DT):
            st = stage[mt % 2]
            ph.op("sync", lambda e, st=st, mt=mt: e.dma_start(
                out=st[:], in_=Wp1[mt, :, :]), writes=[st], dma=True)
            ph.op("vector", lambda e, st=st: e.tensor_copy(wpc[:], st[:]),
                  reads=[st], writes=[wpc])
            ph.op("gpsimd", lambda e, mt=mt: e.dma_start(
                out=wpb1[mt, :, :], in_=wpc[:]), reads=[wpc],
                writes=[(wpb1, mt)], dma=True)
        for mat, dest, dorope in ((Wk1, karr, True), (Wv1, varr, False)):
            for et in range(H):
                st = stage[et % 2]
                wb = wkv[et % 2]
                ph.op("sync", lambda e, st=st, et=et, mat=mat: e.dma_start(
                    out=st[:], in_=mat[et, :, :]), writes=[st], dma=True)
                ph.op("vector", lambda e, st=st, wb=wb: e.tensor_copy(wb[:], st[:]),
                      reads=[st], writes=[wb])
                for b in range(NB):
                    def kmm(e, wb=wb, b=b):
                        ins = None
                        for di in range(DT):
                            ins = e.matmul(pk[:], wb[:, di * 128:(di + 1) * 128],
                                           xk[b][:, di * 128:(di + 1) * 128],
                                           start=(di == 0), stop=(di == DT - 1))
                        return ins
                    ph.op("tensor", kmm,
                          reads=[wb] + [(xk[b], di) for di in range(DT)],
                          writes=[pk])
                    if dorope:
                        ph.op("vector", lambda e: e.tensor_copy(kev[:], pk[:]),
                              reads=[pk], writes=[kev])
                        ph.op("tensor", lambda e: e.matmul(
                            pr_[:], permT1[:], kev[:], start=True, stop=True),
                            reads=[kev, permT1], writes=[pr_])
                        ph.op("vector", lambda e: e.tensor_mul(tc[:], kev[:], cos1[:]),
                              reads=[kev, cos1], writes=[tc])
                        ph.op("vector", lambda e: e.tensor_mul(ts[:], pr_[:], sin1[:]),
                              reads=[pr_, sin1], writes=[ts])
                        ph.op("vector", lambda e, dest=dest, b=b, et=et: e.tensor_add(
                            dest[b][:, et * 128:(et + 1) * 128], tc[:], ts[:]),
                            reads=[tc, ts], writes=[(dest[b], et)])
                    else:
                        ph.op("vector", lambda e, dest=dest, b=b, et=et: e.tensor_copy(
                            dest[b][:, et * 128:(et + 1) * 128], pk[:]),
                            reads=[pk], writes=[(dest[b], et)])

        for b in range(NB):
            for h in range(H):
                ph.op("tensor", lambda e, b=b, h=h: e.matmul(
                    psc[:], latqb[:, h * NL:(h + 1) * NL],
                    karr[b][:, h * 128:(h + 1) * 128], start=True, stop=True),
                    reads=[latqb, (karr[b], h)], writes=[psc])
                ph.op("vector", lambda e: e.scalar_tensor_tensor(
                    sm[:], psc[:], SCALE1, latmask[:], OP.mult, OP.add),
                    reads=[psc, latmask], writes=[sm])
                ph.op("vector", lambda e: e.reduce_max(
                    ngm[:], sm[:], axis=AX.X, negate=True),
                    reads=[sm], writes=[ngm])
                ph.op("scalar", lambda e: e.activation(
                    wexp[:], sm[:], AF.Exp, bias=ngm[:, 0:1]),
                    reads=[sm, ngm], writes=[wexp])
                ph.op("vector", lambda e: e.reduce_sum(ssum[:], wexp[:], axis=AX.X),
                      reads=[wexp], writes=[ssum])
                ph.op("vector", lambda e: e.reciprocal(rs[:], ssum[:]),
                      reads=[ssum], writes=[rs])
                ph.op("vector", lambda e: e.tensor_scalar_mul(
                    wn[:], wexp[:], rs[:, 0:1]), reads=[wexp, rs], writes=[wn])
                ph.op("tensor", lambda e: e.transpose(ptw[:], wn[:], ident[0:64, 0:64]),
                      reads=[wn, ident], writes=[ptw])
                ph.op("vector", lambda e: e.tensor_copy(wT[:], ptw[:]),
                      reads=[ptw], writes=[wT])
                ph.op("tensor", lambda e, b=b, h=h: e.transpose(
                    ptv[:], varr[b][:, h * 128:(h + 1) * 128], ident[:]),
                    reads=[(varr[b], h), ident], writes=[ptv])
                ph.op("vector", lambda e: e.tensor_copy(vT[:], ptv[:]),
                      reads=[ptv], writes=[vT])
                ph.op("tensor", lambda e: e.matmul(
                    pat[:], wT[:], vT[:], start=True, stop=True),
                    reads=[wT, vT], writes=[pat])
                ph.op("vector", lambda e, b=b, h=h: e.tensor_copy(
                    attnb[:, (b * H + h) * 128:(b * H + h + 1) * 128], pat[:]),
                    reads=[pat], writes=[(attnb, (b, h))])
        ph.emit(es)

    with pr.scope():
        ph = Ph(nc, pr=pr, name="a1main")
        NG = 1024 // 128
        wgres = pr.sbuf("a1m_wg", [128, NG * 2048], BF16)
        wpst = [pr.sbuf(f"a1m_wp{i}", [128, D], BF16) for i in range(2)]
        xb2 = [[pr.sbuf(f"a1m_xb{p}_{i}", [128, RT], BF16) for i in range(DT)]
               for p in range(2)]
        gexp = [pr.sbuf(f"a1m_ge{i}", [64, RT], BF16) for i in range(2)] * 8
        gr = [pr.sbuf(f"a1m_gr{i}", [1, RT], F32) for i in range(2)]
        warr = [pr.sbuf(f"a1m_wa{i}", [128, RT], BF16) for i in range(H)]
        z = [pr.sbuf(f"a1m_z{i}", [128, RT], BF16) for i in range(DT)]
        zsq = [pr.sbuf(f"a1m_q{i}", [128, RT], BF16) for i in range(DT)]
        utmp = [pr.sbuf(f"a1m_u{i}", [128, RT], F32) for i in range(4)]
        ost = [pr.sbuf(f"a1m_o{i}", [128, RT], F32) for i in range(4)]
        rows = pr.sbuf("a1m_rows", [1, 3 * RT], F32)
        work = [pr.psum(f"a1m_pw{i}", [128, RT]) for i in range(3)]
        gs = pr.psum("a1m_gs", [1, RT])
        st1 = pr.psum("a1m_st1", [1, RT])
        st2 = pr.psum("a1m_st2", [1, RT])
        bc1 = pr.psum("a1m_bc1", [128, RT])
        bc2 = pr.psum("a1m_bc2", [128, RT])
        ones_bf = pr.t["ones_bf"]

        for mt in range(NG):
            ph.op("gpsimd", lambda e, mt=mt: e.dma_start(
                out=wgres[:, mt * 2048:(mt + 1) * 2048], in_=Wg1[mt, :, :]),
                writes=[(wgres, "all")], dma=True)


        def load_x(r):
            if r >= NR:
                return
            t0 = r * RT
            xb = xb2[r % 2]
            for ot in range(DT):
                ph.op("gpsimd", lambda e, ot=ot, t0=t0, xb=xb: e.dma_start(
                    out=xb[ot][:], in_=xin[ot * 128:(ot + 1) * 128, t0:t0 + RT]),
                    writes=[xb[ot]], dma=True, chan=f"a1m_xbg{r % 2}")

        load_x(0)
        for r in range(NR):
            t0 = r * RT
            b = r // (T // RT)
            xb = xb2[r % 2]
            ones_f32 = pr.t["ones_f32"]
            gsb = [gs[:], st1[:]]
            gskey = ["a1m_gs", "a1m_st1"]
            bcb = [bc1, bc2]
            bckey = ["a1m_bc1", "a1m_bc2"]
            for h in range(H):
                og, half = h // 2, (h % 2) * 64
                hp = h % 2
                w = work[0]
                def gmm(e, og=og, half=half, w=w, xb=xb):
                    ins = None
                    for di in range(DT):
                        c0 = og * 2048 + di * 128 + half
                        ins = e.matmul(w[0:64, :], wgres[:, c0:c0 + 64],
                                       xb[di][:], start=(di == 0), stop=(di == DT - 1))
                    return ins
                ph.op("tensor", gmm,
                      reads=[(wgres, "all")] + [xb[di] for di in range(DT)],
                      writes=[w])
                ph.op("scalar", lambda e, h=h, w=w: e.activation(
                    gexp[h][:], w[0:64, :], AF.Exp, scale=SCALE1),
                    reads=[w], writes=[gexp[h]])
                ph.op("tensor", lambda e, h=h, hp=hp: e.matmul(
                    gsb[hp], ones_bf[0:64, 0:1], gexp[h][:], start=True, stop=True),
                    reads=[gexp[h], ones_bf], writes=[(pr.t[gskey[hp]], "g")])
                g = gr[h % 2]
                ph.op("vector", lambda e, g=g, hp=hp: e.reciprocal(g[:], gsb[hp]),
                      reads=[(pr.t[gskey[hp]], "g")], writes=[g])
                ph.op("tensor", lambda e, g=g, hp=hp: e.matmul(
                    bcb[hp][0:64, :], ones_f32[0:1, 0:64], g[:], start=True, stop=True),
                    reads=[g, ones_f32], writes=[(pr.t[bckey[hp]], "g")])
                ph.op("vector", lambda e, h=h, hp=hp: e.tensor_mul(
                    gexp[h][:], gexp[h][:], bcb[hp][0:64, :]),
                    reads=[gexp[h], (pr.t[bckey[hp]], "g")], writes=[gexp[h]])
                wv = work[1 + h % 2]
                ph.op("tensor", lambda e, b=b, h=h, wv=wv: e.matmul(
                    wv[:], attnb[:, (b * H + h) * 128:(b * H + h + 1) * 128],
                    gexp[h][:], start=True, stop=True),
                    reads=[(attnb, (b, h)), gexp[h]], writes=[wv])
                ph.op("scalar", lambda e, h=h, wv=wv: e.copy(
                    warr[h][:], wv[:]), reads=[wv], writes=[warr[h]])
            for ot in range(DT):
                w = work[1 + ot % 2]
                wp = wpst[ot % 2]
                ph.op("sync", lambda e, ot=ot, wp=wp: e.dma_start(
                    out=wp[:], in_=wpb1[ot, :, :]), writes=[wp], dma=True)
                def pmm(e, ot=ot, w=w, wp=wp):
                    ins = None
                    for et in range(H):
                        ins = e.matmul(w[:],
                                       wp[:, et * 128:(et + 1) * 128],
                                       warr[et][:], start=(et == 0), stop=(et == H - 1))
                    return ins
                ph.op("tensor", pmm,
                      reads=[wp] + [warr[et] for et in range(H)],
                      writes=[w])
                q = zsq[ot]
                ph.op("vector", lambda e, ot=ot, w=w, xb=xb: e.scalar_tensor_tensor(
                    z[ot][:], w[:], 1.0, xb[ot][:], OP.mult, OP.add),
                    reads=[w, xb[ot]], writes=[z[ot]])
                ph.op("scalar", lambda e, ot=ot, q=q: e.activation(
                    q[:], z[ot][:], AF.Square), reads=[z[ot]], writes=[q])
                ph.op("tensor", lambda e, ot=ot: e.matmul(
                    st1[:], ones_bf[:, 0:1], z[ot][:],
                    start=(ot == 0), stop=(ot == DT - 1)),
                    reads=[z[ot], ones_bf], writes=[(st1, "acc"), (st1, "g")])
                ph.op("tensor", lambda e, ot=ot, q=q: e.matmul(
                    st2[:], ones_bf[:, 0:1], q[:],
                    start=(ot == 0), stop=(ot == DT - 1)),
                    reads=[q, ones_bf], writes=[(st2, "acc")])
            load_x(r + 1)
            emit_ln_tail(pr, ph, "a1", z, zsq, st1, st2, bc1, bc2, rows,
                         utmp, ost, xout, r, out_toks)
        ph.emit(es)


def build_l1_attn_program():
    nc = bass.Bass("TRN2", target_bir_lowering=False)
    with ExitStack() as es:
        pr = Prog(nc, es)
        xin = pr.dram_in("xin", [D, NT], F32)
        xout = pr.dram_out("xout", [D, NT], F32)
        pr.dram_in("Wk1", [H, 128, D], F32)
        pr.dram_in("Wv1", [H, 128, D], F32)
        pr.dram_in("Wg1", [8, 128, D], F32)
        pr.dram_in("Wp1", [DT, 128, D], F32)
        pr.dram_in("latq", [H, 128, NL], F32)
        pr.dram_in("cos1", [128, 128], F32)
        pr.dram_in("sin1", [128, 128], F32)
        pr.dram_in("permT1", [128, 128], BF16)
        pr.dram_in("latmask", [64, 128], F32)
        with pr.scope():
            ph0 = Ph(nc, pr=pr, name="prep")
            emit_ones(pr, ph0)
            emit_identity(pr, ph0)
            ph0.emit(es)
        with pr.scope():
            emit_l1_attn(pr, es, xin, xout)
    return nc

A0_DEBUG = False
A0_DEBUG2 = False


def emit_l0_attn(pr, es, xin, xout, out_toks=None, pre_ops=None):
    nc = pr.nc
    wqT, wkT, wvT, woT = pr.t["wqT"], pr.t["wkT"], pr.t["wvT"], pr.t["woT"]
    SC = 512
    NSC = T // SC
    NCH = SC // CH
    lam = np.exp2(-8.0 * np.arange(1, H + 1, dtype=np.float64) / H)
    blk_dec = [float(np.exp(-lam[h] * CH)) for h in range(H)]

    wqb = pr.dram_tmp("wqb", [H, 128, D], BF16)
    wkb = pr.dram_tmp("wkb", [HKV, 128, D], BF16)
    wob = pr.dram_tmp("wob", [DT, 128, D], BF16)
    wk_rhs = pr.dram_tmp("wk_rhs", [DT, 128, HKV * 128], BF16)
    wv_rhs = pr.dram_tmp("wv_rhs", [DT, 128, HKV * 128], BF16)

    cos0 = pr.sbuf("a0_cos0", [128, T], BF16)
    sin0 = pr.sbuf("a0_sin0", [128, T], BF16)
    cos0tm = pr.sbuf("a0_cos0tm", [CH, (T // CH) * 64], BF16)
    sin0tm = pr.sbuf("a0_sin0tm", [CH, (T // CH) * 64], BF16)
    permT0 = pr.sbuf("a0_permT0", [128, 128], BF16)
    diagT = pr.sbuf("a0_diagT", [64, H * CH], BF16)
    qdec = pr.sbuf("a0_qdec", [128, H * CH], BF16)
    kdecT = pr.sbuf("a0_kdecT", [CH, H], F32)
    kdecTM = pr.sbuf("a0_kdecTM", [CH, H * DH], BF16)

    with pr.scope():
        ph = pr.t.get("_shared_prep_ph") or Ph(nc, pr=pr, name="a0prep")
        for dst, srcn in ((permT0, "permT0"), (kdecT, "kdecT")):
            ph.op("sync", lambda e, dst=dst, srcn=srcn: e.dma_start(
                out=dst[:], in_=pr.t[srcn][:, :]), writes=[dst], dma=True)
        for dst, srcn in ((cos0, "cos0"), (sin0, "sin0"),
                          (cos0tm, "cos0tm"), (sin0tm, "sin0tm"),
                          (diagT, "diagT"), (qdec, "qdec"),
                          (kdecTM, "kdecTM")):
            ph.op("gpsimd", lambda e, dst=dst, srcn=srcn: e.dma_start(
                out=dst[:], in_=pr.t[srcn][:, :]), writes=[dst], dma=True)
        stage = [pr.sbuf(f"a0p_st{i}", [128, D], F32) for i in range(2)]
        wcast = [pr.sbuf(f"a0p_w{i}", [128, D], BF16) for i in range(2)]
        jobs = []
        for src_, dst, n, rhs in ((wqT, wqb, H, None), (wkT, wkb, HKV, wk_rhs),
                                  (wvT, None, HKV, wv_rhs), (woT, wob, DT, None)):
            for mt in range(n):
                jobs.append((src_, dst, mt, rhs))
        for i, (src_, dst, mt, rhs) in enumerate(jobs):
            st, wc = stage[i % 2], wcast[i % 2]
            ph.op("sync", lambda e, st=st, src_=src_, mt=mt: e.dma_start(
                out=st[:], in_=src_[mt, :, :]), writes=[st], dma=True)
            ph.op("vector", lambda e, st=st, wc=wc: e.tensor_copy(wc[:], st[:]),
                  reads=[st], writes=[wc])
            if dst is not None:
                ph.op("gpsimd", lambda e, wc=wc, dst=dst, mt=mt: e.dma_start(
                    out=dst[mt, :, :], in_=wc[:]), reads=[wc],
                    writes=[(dst, mt)], dma=True)
            if rhs is not None:
                for kt in range(DT):
                    ph.op("gpsimd", lambda e, wc=wc, rhs=rhs, mt=mt, kt=kt: e.dma_start(
                        out=rhs[kt, :, mt * 128:(mt + 1) * 128],
                        in_=wc[:, kt * 128:(kt + 1) * 128]), reads=[wc],
                        writes=[(rhs, (mt, kt))], dma=True,
                        chan=f"rhs_{kt % 4}")
        if pr.t.get("_shared_prep_ph") is None:
            ph.emit(es)
        else:
            ph.emit(es)
            pr.t["_shared_prep_ph"] = None

    with pr.scope():
        ph = Ph(nc, pr=pr, name="a0main")
        if pre_ops is not None:
            pre_ops(ph)
        ones_bf = pr.t["ones_bf"]
        ones_f32 = pr.t["ones_f32"]
        xb2 = [[pr.sbuf(f"a0m_xb{p}_{i}", [128, SC], BF16) for i in range(DT)]
               for p in range(2)]
        wbuf = [pr.sbuf(f"a0m_w{i}", [128, D], BF16) for i in range(2)]
        wobuf = [pr.sbuf(f"a0m_wo{i}", [128, D], BF16) for i in range(2)]
        kvres = pr.sbuf("a0m_kvres", [128, DT * 512], BF16)
        qarr = pr.sbuf("a0m_q", [128, H * SC], BF16)
        karr = pr.sbuf("a0m_k", [128, HKV * SC], BF16)
        ktm_raw = pr.sbuf("a0m_ktr", [CH, NCH * 512], BF16)
        ktm = ktm_raw
        vtm = pr.sbuf("a0m_vtm", [CH, NCH * 512], BF16)
        obuf = pr.sbuf("a0m_o", [128, H * SC], BF16)
        kv_sb = [[pr.sbuf(f"a0m_kv{h}_{p}", [128, 128], BF16) for p in range(2)]
                 for h in range(H)]
        qev = pr.sbuf("a0m_qev", [128, SC], BF16)
        tc = pr.sbuf("a0m_tc", [128, SC], BF16)
        ts2 = pr.sbuf("a0m_ts", [128, SC], BF16)
        qd = pr.sbuf("a0m_qd", [128, H * CH], BF16)
        kdT = [pr.sbuf(f"a0m_kdT{i}", [64, 128], BF16) for i in range(2)]
        z = [pr.sbuf(f"a0m_z{i}", [128, SC], BF16) for i in range(DT)]
        zsq = [pr.sbuf(f"a0m_q{i}", [128, SC], BF16) for i in range(DT)]
        utmp = [pr.sbuf(f"a0m_u{i}", [128, SC], F32) for i in range(2)]
        ost = [pr.sbuf(f"a0m_os{i}", [128, SC], F32) for i in range(2)]
        rows = pr.sbuf("a0m_rows", [1, 3 * SC], F32)

        qp = pr.psum("a0m_qp", [128, SC])
        qr = pr.psum("a0m_qr", [128, SC])
        pj = pr.psum("a0m_pj", [128, SC])
        bc1 = pr.psum("a0m_bc1", [128, SC])
        bc2 = pr.psum("a0m_bc2", [128, SC])
        pkv0 = pr.psum("a0m_pkv0", [128, 128])
        pkv1 = pr.psum("a0m_pkv1", [128, 128])
        stx = pr.psum("a0m_st", [1, SC])
        po = [qr[:, 0:CH], bc2[:, 0:CH]]
        pkv = [pkv0[:, :], pkv1[:, :]]
        pokey = ["qr", "bc2"]
        pkvkey = ["pkv0", "pkv1"]
        psh = [pj, bc1]
        pshkey = ["pj", "bc1"]
        smaskb = pr.sbuf("a0m_smb", [64, H * CH], BF16)


        def load_x(b, S):
            if S >= NSC:
                b, S = b + 1, 0
            if b >= NB:
                return
            t0 = b * T + S * SC
            xb = xb2[(b * NSC + S) % 2]
            for di in range(DT):
                ph.op("gpsimd", lambda e, di=di, t0=t0, xb=xb: e.dma_start(
                    out=xb[di][:], in_=xin[di * 128:(di + 1) * 128, t0:t0 + SC]),
                    writes=[xb[di]], dma=True, chan=f"a0m_xbg{(b * NSC + S) % 2}")

        load_x(0, 0)
        for b in range(NB):
            for h in range(H):
                ph.op("gpsimd", lambda e, h=h, b=b: e.memset(
                    kv_sb[h][0][:], 0.0), writes=[(kv_sb[h][0], "kv")])
            for S in range(NSC):
                t0 = b * T + S * SC
                p0 = S * SC
                r = (b * T + S * SC) // RT
                xb = xb2[(b * NSC + S) % 2]
                for (wsrc, n_et, dest) in ((wqb, H, qarr), (wkb, HKV, karr)):
                    for et in range(n_et):
                        wb = wbuf[et % 2]
                        ph.op("sync", lambda e, wb=wb, wsrc=wsrc, et=et: e.dma_start(
                            out=wb[:], in_=wsrc[et, :, :]), writes=[wb], dma=True)
                        def qmm(e, wb=wb, xb=xb):
                            ins = None
                            for di in range(DT):
                                ins = e.matmul(qp[:], wb[:, di * 128:(di + 1) * 128],
                                               xb[di][:], start=(di == 0),
                                               stop=(di == DT - 1))
                            return ins
                        ph.op("tensor", qmm,
                              reads=[wb] + [xb[di] for di in range(DT)], writes=[qp])
                        ph.op("vector", lambda e: e.tensor_copy(qev[:], qp[:]),
                              reads=[qp], writes=[qev])
                        ph.op("tensor", lambda e: e.matmul(
                            qr[:], permT0[:], qev[:], start=True, stop=True),
                            reads=[qev, permT0], writes=[(qr, "qr")])
                        ph.op("vector", lambda e, p0=p0: e.tensor_mul(
                            tc[:], qev[:], cos0[:, p0:p0 + SC]),
                            reads=[qev, cos0], writes=[tc])
                        ph.op("vector", lambda e, p0=p0: e.tensor_mul(
                            ts2[:], qr[:], sin0[:, p0:p0 + SC]),
                            reads=[(qr, "qr"), sin0], writes=[ts2])
                        ph.op("vector", lambda e, dest=dest, et=et: e.tensor_add(
                            dest[:, et * SC:(et + 1) * SC], tc[:], ts2[:]),
                            reads=[tc, ts2], writes=[(dest, et)])
                for (wsc, dest, dkey) in ((wk_rhs, ktm_raw, "ktr"), (wv_rhs, vtm, "vtm")):
                    for kt in range(DT):
                        ph.op("sync", lambda e, kt=kt, wsc=wsc: e.dma_start(
                            out=kvres[:, kt * 512:(kt + 1) * 512], in_=wsc[kt, :, :]),
                            writes=[kvres], dma=True, chan="kvres")
                    for c in range(NCH):
                        def tmm(e, c=c, xb=xb):
                            ins = None
                            for di in range(DT):
                                ins = e.matmul(
                                    qp[0:64, :],
                                    xb[di][:, c * CH:(c + 1) * CH],
                                    kvres[:, di * 512:(di + 1) * 512],
                                    start=(di == 0), stop=(di == DT - 1))
                            return ins
                        ph.op("tensor", tmm,
                              reads=[kvres] + [xb[di] for di in range(DT)],
                              writes=[qp])
                        ph.op("vector", lambda e, dest=dest, c=c: e.tensor_copy(
                            dest[:, c * 512:(c + 1) * 512], qp[0:64, :]),
                            reads=[qp], writes=[(dest, c)])
                cA = S * NCH * 64
                kraw_reads = [(ktm_raw, cc2) for cc2 in range(NCH)]
                pairs = [(c, g) for c in range(NCH) for g in range(HKV)]
                if A0_DEBUG2 and b == 0 and S == 0:
                    pairs = pairs[:1]
                for c, g in pairs:
                    o0 = c * 512 + g * 128
                    tco = cA + c * 64
                    ph.op("vector", lambda e, o0=o0, tco=tco: e.tensor_mul(
                        tc[0:CH, 0:64], ktm_raw[:, o0:o0 + 64],
                        cos0tm[:, tco:tco + 64]),
                        reads=kraw_reads + [cos0tm], writes=[(tc, "r")])
                    ph.op("vector", lambda e, o0=o0, tco=tco: e.tensor_mul(
                        ts2[0:CH, 0:64], ktm_raw[:, o0 + 64:o0 + 128],
                        sin0tm[:, tco:tco + 64]),
                        reads=kraw_reads + [sin0tm], writes=[(ts2, "r")])
                    ph.op("vector", lambda e, o0=o0, tco=tco: e.tensor_mul(
                        tc[0:CH, 64:128], ktm_raw[:, o0 + 64:o0 + 128],
                        cos0tm[:, tco:tco + 64]),
                        reads=kraw_reads + [cos0tm], writes=[(tc, "r")])
                    ph.op("vector", lambda e, o0=o0, tco=tco: e.tensor_mul(
                        ts2[0:CH, 64:128], ktm_raw[:, o0:o0 + 64],
                        sin0tm[:, tco:tco + 64]),
                        reads=kraw_reads + [sin0tm], writes=[(ts2, "r")])
                    ph.op("vector", lambda e, o0=o0: e.tensor_sub(
                        ktm_raw[:, o0:o0 + 64],
                        tc[0:CH, 0:64], ts2[0:CH, 0:64]),
                        reads=[(tc, "r"), (ts2, "r")] + kraw_reads,
                        writes=[(ktm_raw, c)])
                    ph.op("vector", lambda e, o0=o0: e.tensor_add(
                        ktm_raw[:, o0 + 64:o0 + 128],
                        tc[0:CH, 64:128], ts2[0:CH, 64:128]),
                        reads=[(tc, "r"), (ts2, "r")] + kraw_reads,
                        writes=[(ktm_raw, c)])
                for c in range(NCH):
                    cg = S * NCH + c
                    par = cg % 2
                    j0 = c * CH
                    ph.op("vector", lambda e, j0=j0: e.tensor_mul(
                        qd[:], bass.AP(qarr, j0, [[qarr[:].ap[0][0], 128], [SC, H], [1, CH]]),
                        qdec[:]), reads=[(qarr, et) for et in range(H)] + [qdec],
                        writes=[qd])
                    for hb in range(2):
                        bank = psh[hb]
                        bkey = (pr.t[f"a0m_{pshkey[hb]}"], pshkey[hb])
                        def stmm(e, hb=hb, j0=j0, bank=bank):
                            ins = None
                            for hh in range(8):
                                h = hb * 8 + hh
                                g = h // 4
                                ins = e.matmul(
                                    bank[0:64, hh * CH:(hh + 1) * CH],
                                    karr[:, g * SC + j0:g * SC + j0 + CH],
                                    qarr[:, h * SC + j0:h * SC + j0 + CH],
                                    start=True, stop=True)
                            return ins
                        ph.op("tensor", stmm,
                              reads=[(karr, g2) for g2 in range(HKV)]
                                    + [(qarr, h2) for h2 in range(hb * 8, hb * 8 + 8)],
                              writes=[bkey])
                        ph.op("vector", lambda e, hb=hb, bank=bank: e.tensor_mul(
                            smaskb[:, hb * 8 * CH:(hb + 1) * 8 * CH],
                            bank[0:64, 0:8 * CH],
                            diagT[:, hb * 8 * CH:(hb + 1) * 8 * CH]),
                            reads=[bkey, diagT], writes=[(smaskb, hb)])
                    for h in range(H):
                        g = h // 4
                        hp = h % 2
                        kt_ = kdT[hp]
                        ph.op("vector", lambda e, h=h, g=g, c=c, kt_=kt_: e.tensor_mul(
                            kt_[:], ktm[:, c * 512 + g * 128:c * 512 + (g + 1) * 128],
                            kdecTM[:, h * DH:(h + 1) * DH]),
                            reads=[(ktm, c), kdecTM], writes=[kt_])
                        def omm(e, h=h, g=g, c=c, par=par, hp=hp):
                            e.matmul(po[hp], vtm[:, c * 512 + g * 128:c * 512 + (g + 1) * 128],
                                     smaskb[0:64, h * CH:(h + 1) * CH],
                                     start=True, stop=False)
                            return e.matmul(po[hp], kv_sb[h][par][:],
                                            qd[:, h * CH:(h + 1) * CH],
                                            start=False, stop=True)
                        ph.op("tensor", omm,
                              reads=[(vtm, c), (smaskb, h // 8), (kv_sb[h][par], "kv"), qd],
                              writes=[(pr.t[f"a0m_{pokey[hp]}"], pokey[hp])])
                        ph.op("scalar", lambda e, h=h, j0=j0, hp=hp: e.copy(
                            obuf[:, h * SC + j0:h * SC + j0 + CH], po[hp]),
                            reads=[(pr.t[f"a0m_{pokey[hp]}"], pokey[hp])],
                            writes=[(obuf, h)])
                        ph.op("tensor", lambda e, g=g, c=c, hp=hp, kt_=kt_: e.matmul(
                            pkv[hp], kt_[:], vtm[:, c * 512 + g * 128:c * 512 + (g + 1) * 128],
                            start=True, stop=True),
                            reads=[kt_, (vtm, c)],
                            writes=[(pr.t[f"a0m_{pkvkey[hp]}"], pkvkey[hp])])
                        ph.op("vector", lambda e, h=h, par=par, hp=hp, bd=blk_dec[h]: e.scalar_tensor_tensor(
                            kv_sb[h][1 - par][:], kv_sb[h][par][:], bd, pkv[hp],
                            OP.mult, OP.add),
                            reads=[(kv_sb[h][par], "kv"),
                                   (pr.t[f"a0m_{pkvkey[hp]}"], pkvkey[hp])],
                            writes=[(kv_sb[h][1 - par], "kv")])
                for ot in range(DT):
                    wb = wobuf[ot % 2]
                    ph.op("sync", lambda e, wb=wb, ot=ot: e.dma_start(
                        out=wb[:], in_=wob[ot, :, :]), writes=[wb], dma=True)
                    def pmm(e, wb=wb):
                        ins = None
                        for et in range(H):
                            ins = e.matmul(pj[:], wb[:, et * 128:(et + 1) * 128],
                                           obuf[:, et * SC:(et + 1) * SC],
                                           start=(et == 0), stop=(et == H - 1))
                        return ins
                    ph.op("tensor", pmm,
                          reads=[wb] + [(obuf, et) for et in range(H)],
                          writes=[(pr.t["a0m_pj"], "pj")])
                    ph.op("vector", lambda e, ot=ot, xb=xb: e.scalar_tensor_tensor(
                        z[ot][:], pj[:], 1.0, xb[ot][:], OP.mult, OP.add),
                        reads=[(pr.t["a0m_pj"], "pj"), xb[ot]], writes=[z[ot]])
                    ph.op("scalar", lambda e, ot=ot: e.activation(
                        zsq[ot][:], z[ot][:], AF.Square),
                        reads=[z[ot]], writes=[zsq[ot]])
                load_x(b, S + 1)
                R = rows
                A = R[:, 0:SC]; B2 = R[:, SC:2 * SC]; C2 = R[:, 2 * SC:3 * SC]
                kA, kB, kC = (rows, "A"), (rows, "B"), (rows, "C")
                def s1mm(e):
                    ins = None
                    for ot in range(DT):
                        ins = e.matmul(stx[:], ones_bf[:, 0:1], z[ot][:],
                                       start=(ot == 0), stop=(ot == DT - 1))
                    return ins
                ph.op("tensor", s1mm, reads=[z[ot] for ot in range(DT)] + [ones_bf],
                      writes=[(stx, "acc")])
                ph.op("vector", lambda e: e.tensor_scalar_mul(A, stx[:], 1.0 / D),
                      reads=[(stx, "acc")], writes=[kA])
                def s2mm(e):
                    ins = None
                    for ot in range(DT):
                        ins = e.matmul(stx[:], ones_bf[:, 0:1], zsq[ot][:],
                                       start=(ot == 0), stop=(ot == DT - 1))
                    return ins
                ph.op("tensor", s2mm, reads=[zsq[ot] for ot in range(DT)] + [ones_bf],
                      writes=[(stx, "acc")])
                ph.op("vector", lambda e: e.tensor_scalar_mul(B2, stx[:], 1.0 / D),
                      reads=[(stx, "acc")], writes=[kB])
                ph.op("vector", lambda e: e.tensor_mul(C2, A, A), reads=[kA], writes=[kC])
                ph.op("vector", lambda e: e.tensor_sub(B2, B2, C2), reads=[kB, kC], writes=[kB])
                ph.op("vector", lambda e: e.tensor_scalar_add(B2, B2, LNEPS),
                      reads=[kB], writes=[kB])
                ph.op("vector", lambda e: e.reciprocal(C2, B2), reads=[kB], writes=[kC])
                ph.op("scalar", lambda e: e.activation(B2, C2, AF.Sqrt),
                      reads=[kC], writes=[kB])
                ph.op("tensor", lambda e: e.matmul(
                    bc1[:], ones_f32[0:1, :], A, start=True, stop=True),
                    reads=[kA, ones_f32], writes=[(pr.t["a0m_bc1"], "bc1")])
                ph.op("tensor", lambda e: e.matmul(
                    bc2[:], ones_f32[0:1, :], B2, start=True, stop=True),
                    reads=[kB, ones_f32], writes=[(pr.t["a0m_bc2"], "bc2")])
                for ot in range(DT):
                    u = utmp[ot % 2]
                    o = ost[ot % 2]
                    ph.op("vector", lambda e, ot=ot, u=u: e.scalar_tensor_tensor(
                        u[:], z[ot][:], 1.0, bc1[:], OP.mult, OP.subtract),
                        reads=[z[ot], (pr.t["a0m_bc1"], "bc1")], writes=[u])
                    ph.op("vector", lambda e, u=u, o=o: e.tensor_mul(
                        o[:], u[:], bc2[:]),
                        reads=[u, (pr.t["a0m_bc2"], "bc2")], writes=[o])
                    tok = ph.op("gpsimd", lambda e, ot=ot, t0=t0, o=o: e.dma_start(
                        out=xout[ot * 128:(ot + 1) * 128, t0:t0 + SC], in_=o[:]),
                        reads=[o], writes=[(xout, (ot, r))], dma=True,
                        chan=f"xo_a0_{ot % 2}")
                    if out_toks is not None:
                        out_toks.append(tok)
        ph.emit(es)


def build_l0_attn_program():
    nc = bass.Bass("TRN2", target_bir_lowering=False)
    with ExitStack() as es:
        pr = Prog(nc, es)
        xin = pr.dram_in("xin", [D, NT], F32)
        xout = pr.dram_out("xout", [D, NT], F32)
        pr.dram_in("wqT", [H, 128, D], F32)
        pr.dram_in("wkT", [HKV, 128, D], F32)
        pr.dram_in("wvT", [HKV, 128, D], F32)
        pr.dram_in("woT", [DT, 128, D], F32)
        pr.dram_in("cos0", [128, T], F32)
        pr.dram_in("sin0", [128, T], F32)
        pr.dram_in("permT0", [128, 128], BF16)
        pr.dram_in("diagT", [64, H * CH], F32)
        pr.dram_in("qdec", [128, H * CH], F32)
        pr.dram_in("cos0tm", [CH, (T // CH) * 64], F32)
        pr.dram_in("sin0tm", [CH, (T // CH) * 64], F32)
        pr.dram_in("kdecT", [CH, H], F32)
        pr.dram_in("kdecTM", [CH, H * DH], F32)
        if A0_DEBUG:
            pr.dram_out("dbg_q", [128, H * 512], BF16)
            pr.dram_out("dbg_k", [128, HKV * 512], BF16)
            pr.dram_out("dbg_ktm", [CH, 8 * 512], BF16)
            pr.dram_out("dbg_vtm", [CH, 8 * 512], BF16)
            pr.dram_out("dbg_ktr", [CH, 8 * 512], BF16)
            pr.dram_out("dbg_cos", [CH, (T // CH) * 64], BF16)
            pr.dram_out("dbg_sin", [CH, (T // CH) * 64], BF16)
        if A0_DEBUG2:
            pr.dram_out("dbg_tc", [64, 512], BF16)
            pr.dram_out("dbg_ts", [64, 512], BF16)
        with pr.scope():
            ph0 = Ph(nc, pr=pr, name="prep")
            emit_ones(pr, ph0)
            emit_identity(pr, ph0)
            ph0.emit(es)
        with pr.scope():
            emit_l0_attn(pr, es, xin, xout)
    return nc


def kernel(**inputs):
    return kernel_fused(**inputs)


def kernel_multilaunch(**inputs):
    n_cores = 8
    shared, per_core = host_prep(inputs, n_cores)
    cores = list(range(n_cores))

    def launch(nc, names, xin_list):
        in_maps = []
        for c in range(n_cores):
            im = {k: shared[k] for k in names}
            im["xin"] = xin_list[c]
            in_maps.append(im)
        res = run_bass_kernel_spmd(nc, in_maps, core_ids=cores)
        return [np.asarray(r["xout"]) for r in res.results]

    x = [pc["x0"] for pc in per_core]
    nc = build_l0_attn_program()
    x = launch(nc, ("wqT", "wkT", "wvT", "woT", "cos0", "sin0", "permT0",
                    "diagT", "qdec", "cos0tm", "sin0tm", "kdecT", "kdecTM"), x)
    nc = build_l0_ffn_program_x()
    x = launch(nc, ("g_idxT", "u_idxT", "d_idxT", "lut"), x)
    nc = build_l1_attn_program()
    x = launch(nc, ("Wk1", "Wv1", "Wg1", "Wp1", "latq", "cos1", "sin1",
                    "permT1", "latmask"), x)
    nc = build_l1_ffn_program_x()
    x = launch(nc, ("gu_codesT", "dn_codesT", "gu_scalesT", "dn_scalesT"), x)

    out = np.empty((16, T, D), np.float32)
    bpc = 16 // n_cores
    for c in range(n_cores):
        out[c * bpc:(c + 1) * bpc] = x[c].reshape(D, NB, T).transpose(1, 2, 0)
    return out


def build_l0_ffn_program_x():
    return build_l0_ffn_program()


def build_l1_ffn_program_x():
    return build_l1_ffn_program()


_FUSED_INPUTS = None


def build_full_program():
    global _FUSED_INPUTS
    nc = bass.Bass("TRN2", target_bir_lowering=False)
    Prog.input_names = []
    with ExitStack() as es:
        pr = Prog(nc, es)
        x0 = pr.dram_in("xin", [D, NT], F32)
        xout = pr.dram_out("xout", [D, NT], F32)
        xA = pr.dram_tmp("xA", [D, NT], F32)
        xB = pr.dram_tmp("xB", [D, NT], F32)
        pr.dram_in("wqT", [H, 128, D], F32)
        pr.dram_in("wkT", [HKV, 128, D], F32)
        pr.dram_in("wvT", [HKV, 128, D], F32)
        pr.dram_in("woT", [DT, 128, D], F32)
        pr.dram_in("cos0", [128, T], F32)
        pr.dram_in("sin0", [128, T], F32)
        pr.dram_in("permT0", [128, 128], BF16)
        pr.dram_in("diagT", [64, H * CH], F32)
        pr.dram_in("qdec", [128, H * CH], F32)
        pr.dram_in("cos0tm", [CH, (T // CH) * 64], F32)
        pr.dram_in("sin0tm", [CH, (T // CH) * 64], F32)
        pr.dram_in("kdecT", [CH, H], F32)
        pr.dram_in("kdecTM", [CH, H * DH], F32)
        g_idxT = pr.dram_in("g_idxT_s", [256, I0], BF16)
        u_idxT = pr.dram_in("u_idxT_s", [256, I0], BF16)
        d_idxT = pr.dram_in("d_idxT_s", [768, D], BF16)
        pr.dram_in("lut", [1, 16], F32)
        gw = pr.dram_tmp("gw", [D, I0], BF16, shared=True)
        uw = pr.dram_tmp("uw", [D, I0], BF16, shared=True)
        dw = pr.dram_tmp("dw", [6144, D], BF16, shared=True)
        gw_s = pr.dram_tmp("gw_s", [256, I0], BF16)
        uw_s = pr.dram_tmp("uw_s", [256, I0], BF16)
        dw_s = pr.dram_tmp("dw_s", [768, D], BF16)
        pr.dram_in("Wk1", [H, 128, D], F32)
        pr.dram_in("Wv1", [H, 128, D], F32)
        pr.dram_in("Wg1", [8, 128, D], F32)
        pr.dram_in("Wp1", [DT, 128, D], F32)
        pr.dram_in("latq", [H, 128, NL], F32)
        pr.dram_in("cos1", [128, 128], F32)
        pr.dram_in("sin1", [128, 128], F32)
        pr.dram_in("permT1", [128, 128], BF16)
        pr.dram_in("latmask", [64, 128], F32)
        gu_codesT = pr.dram_in("gu_codesT_s", [256, GU], BF16)
        dn_codesT = pr.dram_in("dn_codesT_s", [768, D], BF16)
        gu_scalesT = pr.dram_in("gu_scalesT_s", [256, GU], F16)
        dn_scalesT = pr.dram_in("dn_scalesT_s", [768, D], F16)
        guw = pr.dram_tmp("guw", [D, GU], BF16, shared=True)
        dnw = pr.dram_tmp("dnw", [6144, D], BF16, shared=True)
        guw_s = pr.dram_tmp("guw_s", [256, GU], BF16)
        dnw_s = pr.dram_tmp("dnw_s", [768, D], BF16)

        with pr.scope():
            ph0 = Ph(nc, pr=pr, name="prep")
            emit_ones(pr, ph0)
            emit_identity(pr, ph0)
            emit_lut_prep(pr, ph0)
            ph0.emit(es)
        lutB, cdiff = pr.t["lutB"], pr.t["cdiff"]
        with pr.scope():
            ph = Ph(nc, pr=pr, name="dq")
            pr.t["_shared_prep_ph"] = ph
            for tag, src, dst, nr_, nc_ in (
                    ("g", g_idxT, gw_s, 256, I0), ("u", u_idxT, uw_s, 256, I0),
                    ("d", d_idxT, dw_s, 768, D)):
                emit_dequant_ghost(pr, ph, src, dst, nr_, nc_, lutB, cdiff, tag)
            emit_dequant_fp4(pr, ph, gu_codesT, gu_scalesT, guw_s, 256, GU, "gu")
            emit_dequant_fp4(pr, ph, dn_codesT, dn_scalesT, dnw_s, 768, D, "dn")

        def gathers(ph):
            for shard, full in ((gw_s, gw), (uw_s, uw), (dw_s, dw),
                                (guw_s, guw), (dnw_s, dnw)):
                ph.op("gpsimd", lambda e, shard=shard, full=full: e.collective_compute(
                    "AllGather", OP.bypass, replica_groups=[list(range(8))],
                    ins=[shard.ap().opt()], outs=[full.ap().opt()]),
                    reads=[shard], writes=[full], inc=1)

        with pr.scope():
            emit_l0_attn(pr, es, x0, xA, pre_ops=gathers)
        pr.t["_shared_prep_ph"] = None
        with pr.scope():
            ph = Ph(nc, pr=pr, name="ffn0")
            emit_ffn(pr, ph, xA, xB, gw, 0, uw, 0, dw, I0, False, "f0")
            ph.emit(es)
        with pr.scope():
            emit_l1_attn(pr, es, xB, xA)
        with pr.scope():
            ph = Ph(nc, pr=pr, name="ffn1")
            emit_ffn(pr, ph, xA, xout, guw, 0, guw, I1, dnw, I1, True, "f1")
            ph.emit(es)
    _FUSED_INPUTS = set(Prog.input_names)
    Prog.input_names = None
    return nc


def kernel_fused(**inputs):
    n_cores = 8
    shared, per_core = host_prep(inputs, n_cores)
    nc = build_full_program()

    in_maps = []
    for c in range(n_cores):
        im = dict(shared)
        for k, v in per_core[c].items():
            im[k] = v
        im["xin"] = per_core[c]["x0"]
        im = {k: v for k, v in im.items() if _FUSED_INPUTS is None or k in _FUSED_INPUTS}
        in_maps.append(im)
    res = run_bass_kernel_spmd(nc, in_maps, core_ids=list(range(n_cores)))
    out = np.empty((16, T, D), np.float32)
    bpc = 16 // n_cores
    for c in range(n_cores):
        xf = np.asarray(res.results[c]["xout"])
        out[c * bpc:(c + 1) * bpc] = xf.reshape(D, NB, T).transpose(1, 2, 0)
    return out


# version 16 (final)
# speedup vs baseline: 1.0848x; 1.0251x over previous
import numpy as np
import ml_dtypes
import concourse.bass as bass
import concourse.mybir as mybir
from concourse.bass_utils import run_bass_kernel_spmd

F32 = mybir.dt.float32
BF16 = mybir.dt.bfloat16
F16 = mybir.dt.float16
AF = mybir.ActivationFunctionType
OP = mybir.AluOpType
AX = mybir.AxisListType
bf16 = ml_dtypes.bfloat16

ENGINES = ("sync", "scalar", "vector", "tensor", "gpsimd")


class Ph:

    def __init__(self, nc, name=None, pr=None):
        self.nc = nc
        self.name = name
        self.pr = pr
        self.ops = []
        if pr is not None:
            self.count = pr.scount
            self.chan_engine = pr.chan_engine
        else:
            self.count = {e: 0 for e in ENGINES}
            self.chan_engine = {}
        for e in ENGINES:
            self.count.setdefault(e, 0)
        self.last_writer = {}
        self.readers = {}
        self.my_chans = set()

    @staticmethod
    def _key(x):
        if isinstance(x, tuple):
            h, region = x
        else:
            h, region = x, None
        name = getattr(h, "name", None)
        if name is None:
            t = getattr(h, "tensor", None)
            name = getattr(t, "name", repr(h))
        return (name, region)

    def op(self, engine, fn, reads=(), writes=(), dma=False, inc=None,
           deps=(), chan=None):
        rkeys = [self._key(r) for r in reads]
        wkeys = [self._key(w) for w in writes]
        dep_toks = set(t for t in deps if t is not None)
        for k in rkeys:
            t = self.last_writer.get(k)
            if t is not None:
                dep_toks.add(t)
        for k in wkeys:
            t = self.last_writer.get(k)
            if t is not None:
                dep_toks.add(t)
            for t in self.readers.get(k, ()):
                dep_toks.add(t)
        if dma:
            assert wkeys, "dma op needs a write target"
            chan = ("dma", chan if chan is not None else wkeys[0][0])
            if chan not in self.count:
                self.count[chan] = 0
                self.chan_engine[chan] = engine
            assert self.chan_engine[chan] == engine, (chan, engine)
            self.my_chans.add(chan)
            n_inc = 16 if inc is None else inc
            self.count[chan] += n_inc
            tok = (chan, self.count[chan])
        else:
            n_inc = 1 if inc is None else inc
            self.count[engine] += n_inc
            tok = (engine, self.count[engine])
        rounded = set()
        for de, dv in dep_toks:
            if isinstance(de, tuple) and de[0] == "dma":
                rounded.add((de, self.count[de]))
            else:
                rounded.add((de, dv))
        dep_toks = {t for t in rounded if t != tok}
        self.ops.append((engine, fn, dep_toks, n_inc, tok))
        for k in rkeys:
            self.readers.setdefault(k, []).append(tok)
        for k in wkeys:
            self.last_writer[k] = tok
            self.readers[k] = []
        return tok

    def barrier_wait(self, engine, toks):
        self.ops.append((engine, None, set(t for t in toks if t), 0, None))

    def emit(self, stack):
        nc = self.nc
        if self.pr is not None:
            sems = {key: self.pr.sem(key) for key in self.count}
        else:
            sems = {}
            for key in self.count:
                nm = key if isinstance(key, str) else f"c_{key[1]}"
                sems[key] = stack.enter_context(
                    nc.semaphore(f"{self.name}_{nm}"[:60]))
        per_engine = {e: [] for e in ENGINES}
        for rec in self.ops:
            per_engine[rec[0]].append(rec)
        import contextlib
        scope = nc.named_scope(self.name) if self.name else contextlib.nullcontext()
        with scope, nc.Block(self.name) as blk:
            for e in ENGINES:
                recs = per_engine[e]
                drains = [(c, self.count[c]) for c, ce in self.chan_engine.items()
                          if ce == e and c in self.my_chans]
                if not recs and not drains:
                    continue

                def body(eng, recs=recs, e=e, drains=drains):
                    watermark = {}
                    for _, fn, deps, n_inc, tok in recs:
                        need = {}
                        for de, dv in deps:
                            need[de] = max(need.get(de, 0), dv)
                        for de in sorted(need, key=repr):
                            dv = need[de]
                            if watermark.get(de, 0) >= dv:
                                continue
                            eng.wait_ge(sems[de], dv)
                            watermark[de] = dv
                        if fn is not None:
                            ins = fn(eng)
                            if n_inc:
                                sem_key = tok[0] if tok else e
                                ins.then_inc(sems[sem_key], n_inc)
                    if self.count.get(e) and watermark.get(e, 0) < self.count[e]:
                        eng.wait_ge(sems[e], self.count[e])
                    for c, cv in drains:
                        if watermark.get(c, 0) < cv:
                            eng.wait_ge(sems[c], cv)
                getattr(blk, e)(body)
        return sems



D = 2048
DT = D // 128
NB = 2
T = 2048
NT = NB * T
RT = 512
NR = NT // RT
I0 = 5504
I1 = 5632
GU = 2 * I1
H = 16
DH = 128
HKV = 4
CH = 64
NL = 64
LNEPS = 1e-5
RMSEPS = 1e-6
NEG = -1.0e9
SCALE1 = 1.0 / float(np.sqrt(DH))


def host_prep(inputs, n_cores=8):
    f32 = np.float32

    shared = {}

    def tile_lhsT(A):
        K, M = A.shape
        return np.ascontiguousarray(
            A.reshape(K // 128, 128, M // 128, 128).transpose(2, 1, 0, 3)
            .reshape(M // 128, 128, K)).astype(np.float32)

    shared["wqT"] = tile_lhsT(inputs["L0_attn_wq"].T)
    shared["wkT"] = tile_lhsT(inputs["L0_attn_wk"].T)
    shared["wvT"] = tile_lhsT(inputs["L0_attn_wv"].T)
    shared["woT"] = tile_lhsT(inputs["L0_attn_wo"].T)

    shared["g_idxT"] = np.ascontiguousarray(inputs["L0_ffn_gate_idx"].T).astype(bf16)
    shared["u_idxT"] = np.ascontiguousarray(inputs["L0_ffn_up_idx"].T).astype(bf16)
    shared["d_idxT"] = np.ascontiguousarray(inputs["L0_ffn_down_idx"].T).astype(bf16)
    shared["lut"] = np.ascontiguousarray(inputs["L0_ffn_lut"].reshape(1, 16)).astype(f32)

    shared["Wk1"] = tile_lhsT(inputs["L1_attn_Wk"])
    shared["Wv1"] = tile_lhsT(inputs["L1_attn_Wv"])
    shared["Wg1"] = tile_lhsT(inputs["L1_attn_Wg"])
    shared["Wp1"] = tile_lhsT(inputs["L1_attn_Wp"])
    lq = np.asarray(inputs["L1_attn_latent_queries"])[0]
    shared["latq"] = np.ascontiguousarray(lq.transpose(1, 2, 0)).reshape(H, DH, NL).astype(f32)

    def unpack_T(packed):
        w = packed.astype(np.uint32)
        nib = np.zeros((w.shape[0], w.shape[1] * 8), np.uint8)
        for i in range(8):
            nib[:, i::8] = ((w >> np.uint32(4 * i)) & np.uint32(15)).astype(np.uint8)
        return np.ascontiguousarray(nib.T).astype(np.float32).astype(bf16)

    shared["gu_codesT"] = unpack_T(inputs["L1_ffn_gate_up_packed"])
    shared["dn_codesT"] = unpack_T(inputs["L1_ffn_down_packed"])
    shared["gu_scalesT"] = np.ascontiguousarray(
        np.repeat(np.asarray(inputs["L1_ffn_gate_up_scales"]), 128, axis=0))
    shared["dn_scalesT"] = np.ascontiguousarray(
        np.repeat(np.asarray(inputs["L1_ffn_down_scales"]), 128, axis=0))

    inv = 1.0 / 10000.0 ** (np.arange(0, DH, 2, dtype=np.float64) / DH)
    fr = np.arange(T, dtype=np.float64)[None, :] * inv[:, None]
    shared["cos0"] = np.concatenate([np.cos(fr), np.cos(fr)], 0).astype(f32)
    shared["sin0"] = np.concatenate([np.sin(fr), np.sin(fr)], 0).astype(f32)
    P0 = np.zeros((DH, DH), np.float32)
    for i in range(64):
        P0[i, i + 64] = -1.0
        P0[i + 64, i] = 1.0
    shared["permT0"] = np.ascontiguousarray(P0.T).astype(bf16)
    inv1 = 1.0 / 10000.0 ** (np.arange(0, 64, dtype=np.float64) * 2.0 / DH)
    ang = np.arange(128, dtype=np.float64)[None, :] * inv1[:, None]
    c1 = np.zeros((DH, 128), np.float64); s1_ = np.zeros((DH, 128), np.float64)
    c1[0::2] = np.cos(ang); c1[1::2] = np.cos(ang)
    s1_[0::2] = np.sin(ang); s1_[1::2] = np.sin(ang)
    shared["cos1"] = c1.astype(f32)
    shared["sin1"] = s1_.astype(f32)
    P1 = np.zeros((DH, DH), np.float32)
    for i in range(64):
        P1[2 * i, 2 * i + 1] = -1.0
        P1[2 * i + 1, 2 * i] = 1.0
    shared["permT1"] = np.ascontiguousarray(P1.T).astype(bf16)
    lam = np.exp2(-8.0 * np.arange(1, H + 1, dtype=np.float64) / H)
    r = np.arange(CH, dtype=np.float64)
    diagT = np.zeros((CH, H * CH), np.float32)
    qdec = np.zeros((DH, H * CH), np.float32)
    kdec = np.zeros((DH, H * CH), np.float32)
    for h in range(H):
        dd = r[None, :] - r[:, None]
        diagT[:, h * CH:(h + 1) * CH] = np.where(dd >= 0, np.exp(-lam[h] * dd), 0.0)
        qdec[:, h * CH:(h + 1) * CH] = np.exp(-lam[h] * (r + 1.0))[None, :]
        kdec[:, h * CH:(h + 1) * CH] = np.exp(-lam[h] * (CH - 1.0 - r))[None, :]
    shared["diagT"] = diagT
    shared["qdec"] = qdec
    shared["kdec"] = kdec
    inv0 = 1.0 / 10000.0 ** (np.arange(0, DH, 2, dtype=np.float64) / DH)
    cc, jj, ii = np.meshgrid(np.arange(T // CH), np.arange(CH), inv0,
                             indexing="ij")
    ang0 = (cc * CH + jj) * ii
    c0t = np.cos(ang0).transpose(1, 0, 2)
    s0t = np.sin(ang0).transpose(1, 0, 2)
    shared["cos0tm"] = np.ascontiguousarray(
        np.concatenate([c0t, c0t], -1).reshape(CH, -1)).astype(f32)
    shared["sin0tm"] = np.ascontiguousarray(
        np.concatenate([s0t, s0t], -1).reshape(CH, -1)).astype(f32)
    kdecT = np.zeros((CH, H), np.float64)
    for h in range(H):
        kdecT[:, h] = np.exp(-lam[h] * (CH - 1.0 - r))
    shared["kdecT"] = kdecT.astype(f32)
    shared["kdecTM"] = np.ascontiguousarray(
        np.repeat(kdecT.T[:, :, None], DH, axis=2).transpose(1, 0, 2)
        .reshape(CH, H * DH)).astype(f32)
    shared["blk_dec"] = np.exp(-lam * CH).astype(np.float64)
    lmask = np.where(np.arange(128)[None, :] <= np.arange(NL)[:, None], 0.0, NEG)
    shared["latmask"] = lmask.astype(f32)

    x = np.asarray(inputs["L0_attn_x"])
    def padrows(a, rows):
        out = np.zeros((rows, a.shape[1]), a.dtype)
        out[: a.shape[0]] = a
        return out
    d_pad = padrows(shared["d_idxT"], 6144)
    dn_pad = padrows(shared["dn_codesT"], 6144)
    dns_pad = padrows(shared["dn_scalesT"], 6144)
    per_core = []
    bpc = x.shape[0] // n_cores
    for c in range(n_cores):
        xc = x[c * bpc:(c + 1) * bpc]
        xfm = np.ascontiguousarray(xc.transpose(2, 0, 1).reshape(D, -1))
        pc = {"x0": xfm}
        pc["g_idxT_s"] = np.ascontiguousarray(shared["g_idxT"][c * 256:(c + 1) * 256])
        pc["u_idxT_s"] = np.ascontiguousarray(shared["u_idxT"][c * 256:(c + 1) * 256])
        pc["d_idxT_s"] = np.ascontiguousarray(d_pad[c * 768:(c + 1) * 768])
        pc["gu_codesT_s"] = np.ascontiguousarray(shared["gu_codesT"][c * 256:(c + 1) * 256])
        pc["gu_scalesT_s"] = np.ascontiguousarray(shared["gu_scalesT"][c * 256:(c + 1) * 256])
        pc["dn_codesT_s"] = np.ascontiguousarray(dn_pad[c * 768:(c + 1) * 768])
        pc["dn_scalesT_s"] = np.ascontiguousarray(dns_pad[c * 768:(c + 1) * 768])
        per_core.append(pc)
    return shared, per_core


def gather_out(results):
    outs = []
    for r in results:
        xfm = np.asarray(r["xout"])
        outs.append(xfm.reshape(D, NB, T).transpose(1, 2, 0))
    return np.concatenate(outs, 0).astype(np.float32)


from contextlib import ExitStack, contextmanager


class Prog:

    def __init__(self, nc, stack):
        self.nc = nc
        self.es = stack
        self.stack = stack
        self.t = {}
        self.scount = {}
        self.chan_engine = {}
        self._sems = {}

    def sem(self, key):
        if key not in self._sems:
            nm = key if isinstance(key, str) else f"c_{key[1]}"
            self._sems[key] = self.es.enter_context(
                self.nc.semaphore(str(nm)[:60]))
        return self._sems[key]

    @contextmanager
    def scope(self):
        old = self.stack
        with ExitStack() as s:
            self.stack = s
            try:
                yield
            finally:
                self.stack = old

    input_names = None

    def dram_in(self, name, shape, dt):
        self.t[name] = self.nc.dram_tensor(name, shape, dt, kind="ExternalInput")
        if Prog.input_names is not None:
            Prog.input_names.append(name)
        return self.t[name]

    def dram_out(self, name, shape, dt):
        self.t[name] = self.nc.dram_tensor(name, shape, dt, kind="ExternalOutput")
        return self.t[name]

    def dram_tmp(self, name, shape, dt, shared=False):
        kw = {"addr_space": "Shared"} if shared else {}
        self.t[name] = self.nc.dram_tensor(name, shape, dt, **kw)
        return self.t[name]

    def sbuf(self, name, shape, dt, persistent=False):
        st = self.es if persistent else self.stack
        h = st.enter_context(self.nc.sbuf_tensor(name, shape, dt))
        self.t[name] = h
        return h

    def psum(self, name, shape, dt=F32):
        h = self.stack.enter_context(self.nc.psum_tensor(name, shape, dt))
        self.t[name] = h
        return h


def emit_dequant_ghost(pr, ph, src, dst, nrows, ncols, lutB, cdiff, tag):
    nc = pr.nc
    CC = 1376 if ncols % 1376 == 0 else 1024
    idx = [pr.sbuf(f"dq_{tag}_idx{i}", [128, CC], BF16) for i in range(2)]
    tmp = pr.sbuf(f"dq_{tag}_tmp", [128, CC], BF16)
    p = [pr.sbuf(f"dq_{tag}_p{i}", [128, CC], BF16) for i in range(4)]
    s01 = pr.sbuf(f"dq_{tag}_s01", [128, CC], F32)
    s23 = pr.sbuf(f"dq_{tag}_s23", [128, CC], F32)
    w = [pr.sbuf(f"dq_{tag}_w{i}", [128, CC], BF16) for i in range(2)]

    n_rt = nrows // 128
    chunks = []
    c0 = 0
    while c0 < ncols:
        cw = min(CC, ncols - c0)
        chunks.append((c0, cw))
        c0 += cw

    for rt in range(n_rt):
        for ci, (c0, cw) in enumerate(chunks):
            ib = idx[(rt * len(chunks) + ci) % 2]
            wb = w[(rt * len(chunks) + ci) % 2]
            ph.op("sync", lambda e, ib=ib, rt=rt, c0=c0, cw=cw: e.dma_start(
                out=ib[:, 0:cw], in_=src[rt * 128:(rt + 1) * 128, c0:c0 + cw]),
                reads=[], writes=[ib], dma=True)
            for g in range(4):
                j0 = 4 * g
                ph.op("vector", lambda e, pg=p[g], ib=ib, j0=j0, cw=cw: e.tensor_scalar(
                    pg[:, 0:cw], ib[:, 0:cw], float(j0), cdiff[:, j0:j0 + 1],
                    OP.is_ge, OP.mult), reads=[ib, cdiff], writes=[p[g]])
                for j in (j0 + 1, j0 + 2, j0 + 3):
                    ph.op("vector", lambda e, ib=ib, j=j, cw=cw: e.tensor_scalar(
                        tmp[:, 0:cw], ib[:, 0:cw], float(j), cdiff[:, j:j + 1],
                        OP.is_ge, OP.mult), reads=[ib, cdiff], writes=[tmp])
                    ph.op("vector", lambda e, pg=p[g], cw=cw: e.tensor_add(
                        pg[:, 0:cw], pg[:, 0:cw], tmp[:, 0:cw]),
                        reads=[tmp, p[g]], writes=[p[g]])
            ph.op("vector", lambda e, cw=cw: e.tensor_add(
                s01[:, 0:cw], p[0][:, 0:cw], p[1][:, 0:cw]),
                reads=[p[0], p[1]], writes=[s01])
            ph.op("vector", lambda e, cw=cw: e.tensor_add(
                s23[:, 0:cw], p[2][:, 0:cw], p[3][:, 0:cw]),
                reads=[p[2], p[3]], writes=[s23])
            ph.op("vector", lambda e, wb=wb, cw=cw: e.tensor_add(
                wb[:, 0:cw], s01[:, 0:cw], s23[:, 0:cw]),
                reads=[s01, s23], writes=[wb])
            ph.op("gpsimd", lambda e, wb=wb, rt=rt, c0=c0, cw=cw: e.dma_start(
                out=dst[rt * 128:(rt + 1) * 128, c0:c0 + cw], in_=wb[:, 0:cw]),
                reads=[wb], writes=[(dst, (rt, c0))], dma=True,
                chan=f"dqs_{tag}_{(rt * len(chunks) + ci) % 2}")


_FP4_VALS = [0.0, 0.5, 1.0, 1.5, 2.0, 3.0, 4.0, 6.0,
             -0.0, -0.5, -1.0, -1.5, -2.0, -3.0, -4.0, -6.0]


def emit_dequant_fp4(pr, ph, src, scales, dst, nrows, ncols, tag):
    nc = pr.nc
    CC = 1024
    idx = [pr.sbuf(f"dq_{tag}_idx{i}", [128, CC], BF16) for i in range(2)]
    sc = [pr.sbuf(f"dq_{tag}_sc{i}", [128, CC], F16) for i in range(2)]
    tmp = pr.sbuf(f"dq_{tag}_tmp", [128, CC], BF16)
    acc = pr.sbuf(f"dq_{tag}_acc", [128, CC], BF16)
    w = [pr.sbuf(f"dq_{tag}_w{i}", [128, CC], BF16) for i in range(2)]
    diffs = [_FP4_VALS[j] - _FP4_VALS[j - 1] for j in range(1, 16)]

    n_rt = nrows // 128
    chunks = []
    c0 = 0
    while c0 < ncols:
        cw = min(CC, ncols - c0)
        chunks.append((c0, cw))
        c0 += cw

    for rt in range(n_rt):
        for ci, (c0, cw) in enumerate(chunks):
            par = (rt * len(chunks) + ci) % 2
            ib, sb, wb = idx[par], sc[par], w[par]
            ph.op("sync", lambda e, ib=ib, rt=rt, c0=c0, cw=cw: e.dma_start(
                out=ib[:, 0:cw], in_=src[rt * 128:(rt + 1) * 128, c0:c0 + cw]),
                writes=[ib], dma=True)
            ph.op("sync", lambda e, sb=sb, rt=rt, c0=c0, cw=cw: e.dma_start(
                out=sb[:, 0:cw], in_=scales[rt * 128:(rt + 1) * 128, c0:c0 + cw]),
                writes=[sb], dma=True)
            ph.op("vector", lambda e, ib=ib, cw=cw: e.tensor_scalar(
                acc[:, 0:cw], ib[:, 0:cw], 1.0, float(diffs[0]), OP.is_ge, OP.mult),
                reads=[ib], writes=[acc])
            for j in range(2, 16):
                dj = float(diffs[j - 1])
                ph.op("vector", lambda e, ib=ib, j=j, dj=dj, cw=cw: e.tensor_scalar(
                    tmp[:, 0:cw], ib[:, 0:cw], float(j), dj, OP.is_ge, OP.mult),
                    reads=[ib], writes=[tmp])
                ph.op("vector", lambda e, cw=cw: e.tensor_add(
                    acc[:, 0:cw], acc[:, 0:cw], tmp[:, 0:cw]),
                    reads=[tmp, acc], writes=[acc])
            ph.op("vector", lambda e, wb=wb, sb=sb, cw=cw: e.tensor_mul(
                wb[:, 0:cw], acc[:, 0:cw], sb[:, 0:cw]),
                reads=[acc, sb], writes=[wb])
            ph.op("gpsimd", lambda e, wb=wb, rt=rt, c0=c0, cw=cw: e.dma_start(
                out=dst[rt * 128:(rt + 1) * 128, c0:c0 + cw], in_=wb[:, 0:cw]),
                reads=[wb], writes=[(dst, (rt, c0))], dma=True,
                chan=f"dqs_{tag}_{par}")

def emit_ffn(pr, ph, xin, xout, g_scr, g_off, u_scr, u_off, dn_scr, inter,
             rms, tag, out_toks=None):
    nc = pr.nc
    n_oi = inter // 128
    ones_bf = pr.t["ones_bf"]
    ones_f32 = pr.t["ones_f32"]

    xb2 = [[pr.sbuf(f"ffn_{tag}_xb{p}_{i}", [128, RT], BF16) for i in range(DT)]
           for p in range(2)]
    harr = [pr.sbuf(f"ffn_{tag}_h{i}", [128, RT], BF16) for i in range(n_oi)]
    z = [pr.sbuf(f"ffn_{tag}_z{i}", [128, RT], BF16) for i in range(DT)]
    nq = 2 if rms else DT
    zsq = [pr.sbuf(f"ffn_{tag}_q{i}", [128, RT], BF16) for i in range(nq)]
    sg = [pr.sbuf(f"ffn_{tag}_sg{i}", [128, RT], BF16) for i in range(4)]
    wg = [pr.sbuf(f"ffn_{tag}_wg{i}", [128, DT * RT], BF16) for i in range(2)]
    wu = [pr.sbuf(f"ffn_{tag}_wu{i}", [128, DT * RT], BF16) for i in range(2)]
    wdn = [pr.sbuf(f"ffn_{tag}_wd{i}", [128, 4 * RT], BF16) for i in range(2)]
    if rms:
        yarr = [pr.sbuf(f"ffn_{tag}_y{i}", [128, RT], BF16) for i in range(DT)]
    utmp = [pr.sbuf(f"ffn_{tag}_u{i}", [128, RT], F32) for i in range(3)]
    ost = [pr.sbuf(f"ffn_{tag}_o{i}", [128, RT], F32) for i in range(3)]
    rows = pr.sbuf(f"ffn_{tag}_rows", [1, 3 * RT], F32)

    work = [pr.psum(f"ffn_{tag}_pw{i}", [128, RT]) for i in range(4)]
    bc1 = pr.psum(f"ffn_{tag}_bc1", [128, RT])
    bc2 = pr.psum(f"ffn_{tag}_bc2", [128, RT])
    st1 = pr.psum(f"ffn_{tag}_st1", [1, RT])
    st2 = pr.psum(f"ffn_{tag}_st2", [1, RT])

    groups = [list(range(g, min(g + 4, n_oi))) for g in range(0, n_oi, 4)]

    def stat_mm(bank, rhs, first, last):
        def fn(e, bank=bank, rhs=rhs, first=first, last=last):
            return e.matmul(bank[:], ones_bf[:, 0:1], rhs[:],
                            start=first, stop=last)
        return fn

    def load_x(r):
        if r >= NR:
            return
        t0 = r * RT
        xb = xb2[r % 2]
        for ot in range(DT):
            ph.op("gpsimd", lambda e, ot=ot, t0=t0, xb=xb: e.dma_start(
                out=xb[ot][:], in_=xin[ot * 128:(ot + 1) * 128, t0:t0 + RT]),
                writes=[xb[ot]], dma=True, chan=f"ffn_{tag}_xbg{r % 2}")

    load_x(0)
    for r in range(NR):
        t0 = r * RT
        xb = xb2[r % 2]

        def wload(dst, scr, off, g0, ncols):
            rs = scr[:].ap[0][0]
            src_ap = bass.AP(scr, off + g0 * 128,
                             [[rs, 128], [128 * rs, DT], [1, ncols]])
            dst_ap = bass.AP(dst, 0, [[dst[:].ap[0][0], 128], [RT, DT], [1, ncols]])
            return lambda e: e.dma_start(out=dst_ap, in_=src_ap)

        for gi, G in enumerate(groups):
            g0 = G[0]
            ncols = len(G) * 128
            wgb = wg[gi % 2]
            wub = wu[gi % 2]
            ph.op("sync", wload(wgb, g_scr, g_off, g0, ncols),
                  writes=[wgb], dma=True)
            ph.op("sync", wload(wub, u_scr, u_off, g0, ncols),
                  writes=[wub], dma=True)
            for j, oi in enumerate(G):
                def gmm(e, j=j, wgb=wgb, xb=xb):
                    ins = None
                    for di in range(DT):
                        ins = e.matmul(work[j][:],
                                       wgb[:, di * RT + j * 128:di * RT + (j + 1) * 128],
                                       xb[di][:], start=(di == 0), stop=(di == DT - 1))
                    return ins
                ph.op("tensor", gmm,
                      reads=[wgb] + [xb[di] for di in range(DT)],
                      writes=[work[j]])
                ph.op("scalar", lambda e, j=j: e.activation(
                    sg[j][:], work[j][:], AF.Silu), reads=[work[j]], writes=[sg[j]])
            for j, oi in enumerate(G):
                def umm(e, j=j, wub=wub, xb=xb):
                    ins = None
                    for di in range(DT):
                        ins = e.matmul(work[j][:],
                                       wub[:, di * RT + j * 128:di * RT + (j + 1) * 128],
                                       xb[di][:], start=(di == 0), stop=(di == DT - 1))
                    return ins
                ph.op("tensor", umm,
                      reads=[wub] + [xb[di] for di in range(DT)],
                      writes=[work[j]])
                ph.op("vector", lambda e, j=j, oi=oi: e.tensor_mul(
                    harr[oi][:], sg[j][:], work[j][:]),
                    reads=[sg[j], work[j]], writes=[harr[oi]])

        kquads = [list(range(k, min(k + 4, n_oi))) for k in range(0, n_oi, 4)]
        for go in range(4):
            ots = [4 * go + i for i in range(4)]
            for qi, KQ in enumerate(kquads):
                wdb = wdn[qi % 2]
                nk = len(KQ)
                k0 = KQ[0]
                def dload(e, wdb=wdb, k0=k0, nk=nk, go=go):
                    rs = dn_scr[:].ap[0][0]
                    src_ap = bass.AP(dn_scr, k0 * 128 * rs + go * RT,
                                     [[rs, 128], [128 * rs, nk], [1, RT]])
                    dst_ap = bass.AP(wdb, 0, [[wdb[:].ap[0][0], 128], [RT, nk], [1, RT]])
                    return e.dma_start(out=dst_ap, in_=src_ap)
                ph.op("sync", dload, writes=[wdb], dma=True)
                def dmm(e, wdb=wdb, KQ=KQ, ots=ots):
                    ins = None
                    for kk, ki in enumerate(KQ):
                        for j, ot in enumerate(ots):
                            ins = e.matmul(work[j][:],
                                           wdb[:, kk * RT + j * 128:kk * RT + (j + 1) * 128],
                                           harr[ki][:], start=(ki == 0),
                                           stop=(ki == n_oi - 1))
                    return ins
                ph.op("tensor", dmm,
                      reads=[wdb] + [harr[ki] for ki in KQ],
                      writes=[work[j] for j in range(4)])
            for j, ot in enumerate(ots):
                q = zsq[ot % nq]
                if rms:
                    ph.op("vector", lambda e, j=j, ot=ot: e.tensor_copy(
                        yarr[ot][:], work[j][:]), reads=[work[j]], writes=[yarr[ot]])
                    ph.op("scalar", lambda e, ot=ot, q=q: e.activation(
                        q[:], yarr[ot][:], AF.Square),
                        reads=[yarr[ot]], writes=[q])
                    ph.op("tensor", stat_mm(st2, q, ot == 0, ot == DT - 1),
                          reads=[q, ones_bf], writes=[(st2, "acc")])
                else:
                    ph.op("vector", lambda e, j=j, ot=ot, xb=xb: e.scalar_tensor_tensor(
                        z[ot][:], work[j][:], 1.0, xb[ot][:], OP.mult, OP.add),
                        reads=[work[j], xb[ot]], writes=[z[ot]])
                    ph.op("scalar", lambda e, ot=ot, q=q: e.activation(
                        q[:], z[ot][:], AF.Square),
                        reads=[z[ot]], writes=[q])
                    ph.op("tensor", stat_mm(st1, z[ot], ot == 0, ot == DT - 1),
                          reads=[z[ot], ones_bf], writes=[(st1, "acc")])
                    ph.op("tensor", stat_mm(st2, q, ot == 0, ot == DT - 1),
                          reads=[q, ones_bf], writes=[(st2, "acc")])

        R = rows
        if rms:
            ph.op("vector", lambda e: e.tensor_scalar(
                R[:, 0:RT], st2[:], 1.0 / D, RMSEPS, OP.mult, OP.add),
                reads=[(st2, "acc")], writes=[(rows, "A")])
            ph.op("vector", lambda e: e.reciprocal(R[:, RT:2 * RT], R[:, 0:RT]),
                reads=[(rows, "A")], writes=[(rows, "B")])
            ph.op("scalar", lambda e: e.activation(
                R[:, 2 * RT:3 * RT], R[:, RT:2 * RT], AF.Sqrt),
                reads=[(rows, "B")], writes=[(rows, "C")])
            ph.op("tensor", lambda e: e.matmul(
                bc1[:], ones_f32[0:1, :], R[:, 2 * RT:3 * RT],
                start=True, stop=True), reads=[(rows, "C"), ones_f32], writes=[bc1])
            for ot in range(DT):
                q = zsq[ot % nq]
                ph.op("vector", lambda e, ot=ot: e.tensor_mul(
                    utmp[ot % 2][:], yarr[ot][:], bc1[:]),
                    reads=[yarr[ot], bc1], writes=[utmp[ot % 2]])
                ph.op("vector", lambda e, ot=ot, xb=xb: e.tensor_add(
                    z[ot][:], utmp[ot % 2][:], xb[ot][:]),
                    reads=[utmp[ot % 2], xb[ot]], writes=[z[ot]])
                ph.op("scalar", lambda e, ot=ot, q=q: e.activation(
                    q[:], z[ot][:], AF.Square),
                    reads=[z[ot]], writes=[q])
                ph.op("tensor", stat_mm(st1, z[ot], ot == 0, ot == DT - 1),
                      reads=[z[ot], ones_bf], writes=[(st1, "acc")])
                ph.op("tensor", stat_mm(st2, q, ot == 0, ot == DT - 1),
                      reads=[q, ones_bf], writes=[(st2, "acc")])

        load_x(r + 1)
        emit_ln_tail(pr, ph, tag, z, zsq, st1, st2, bc1, bc2, rows,
                     utmp, ost, xout, r, out_toks)


def emit_lut_prep(pr, ph):
    nc = pr.nc
    lut_s = pr.sbuf("lut_s", [1, 16], F32, persistent=True)
    lutB = pr.sbuf("lutB", [128, 16], F32, persistent=True)
    cdiff = pr.sbuf("cdiff", [128, 16], F32, persistent=True)
    pb = pr.psum("lut_pb", [128, 16])
    ph.op("sync", lambda e: e.dma_start(out=lut_s[:], in_=pr.t["lut"][:, :]),
          writes=[lut_s], dma=True)
    ph.op("tensor", lambda e: e.matmul(
        pb[:], pr.t["ones_f32"][0:1, :], lut_s[:], start=True, stop=True),
        reads=[lut_s, pr.t["ones_f32"]], writes=[pb])
    ph.op("vector", lambda e: e.tensor_copy(lutB[:], pb[:]), reads=[pb], writes=[lutB])
    ph.op("vector", lambda e: e.tensor_copy(cdiff[:, 0:1], lutB[:, 0:1]),
          reads=[lutB], writes=[(cdiff, 0)])
    ph.op("vector", lambda e: e.tensor_sub(cdiff[:, 1:16], lutB[:, 1:16], lutB[:, 0:15]),
          reads=[lutB], writes=[(cdiff, 1)])
    return lutB, cdiff


def emit_ones(pr, ph):
    ones_bf = pr.sbuf("ones_bf", [128, 1], BF16, persistent=True)
    ones_f32 = pr.sbuf("ones_f32", [1, 128], F32, persistent=True)
    ph.op("gpsimd", lambda e: e.memset(ones_bf[:], 1.0), writes=[ones_bf])
    ph.op("gpsimd", lambda e: e.memset(ones_f32[:], 1.0), writes=[ones_f32])


def build_l0_ffn_program():
    nc = bass.Bass("TRN2", target_bir_lowering=False)
    with ExitStack() as es:
        pr = Prog(nc, es)
        xin = pr.dram_in("xin", [D, NT], F32)
        xout = pr.dram_out("xout", [D, NT], F32)
        g_idxT = pr.dram_in("g_idxT", [D, I0], BF16)
        u_idxT = pr.dram_in("u_idxT", [D, I0], BF16)
        d_idxT = pr.dram_in("d_idxT", [I0, D], BF16)
        pr.dram_in("lut", [1, 16], F32)
        gw = pr.dram_tmp("gw", [D, I0], BF16, shared=True)
        uw = pr.dram_tmp("uw", [D, I0], BF16, shared=True)
        dw = pr.dram_tmp("dw", [I0, D], BF16)

        with pr.scope():
            ph0 = Ph(nc, pr=pr, name="prep")
            emit_ones(pr, ph0)
            emit_lut_prep(pr, ph0)
            ph0.emit(es)
        lutB, cdiff = pr.t["lutB"], pr.t["cdiff"]
        for tag, src, dst, nr_, nc_ in (
            ("g", g_idxT, gw, D, I0),
            ("u", u_idxT, uw, D, I0),
            ("d", d_idxT, dw, I0, D),
        ):
            with pr.scope():
                ph = Ph(nc, pr=pr, name="dq_" + tag)
                emit_dequant_ghost(pr, ph, src, dst, nr_, nc_, lutB, cdiff, tag)
                ph.emit(es)
        with pr.scope():
            ph = Ph(nc, pr=pr, name="ffn0")
            emit_ffn(pr, ph, xin, xout, gw, 0, uw, 0, dw, I0, False, "f0")
            ph.emit(es)
    return nc


def build_l1_ffn_program(debug_w=False):
    nc = bass.Bass("TRN2", target_bir_lowering=False)
    with ExitStack() as es:
        pr = Prog(nc, es)
        xin = pr.dram_in("xin", [D, NT], F32)
        xout = pr.dram_out("xout", [D, NT], F32)
        gu_codesT = pr.dram_in("gu_codesT", [D, GU], BF16)
        dn_codesT = pr.dram_in("dn_codesT", [I1, D], BF16)
        gu_scalesT = pr.dram_in("gu_scalesT", [D, GU], F16)
        dn_scalesT = pr.dram_in("dn_scalesT", [I1, D], F16)
        mk_w = pr.dram_out if debug_w else pr.dram_tmp
        guw = mk_w("guw", [D, GU], BF16)
        dnw = mk_w("dnw", [I1, D], BF16)

        with pr.scope():
            ph0 = Ph(nc, pr=pr, name="prep")
            emit_ones(pr, ph0)
            ph0.emit(es)
        with pr.scope():
            ph = Ph(nc, pr=pr, name="dq_gu")
            emit_dequant_fp4(pr, ph, gu_codesT, gu_scalesT, guw, D, GU, "gu")
            ph.emit(es)
        with pr.scope():
            ph = Ph(nc, pr=pr, name="dq_dn")
            emit_dequant_fp4(pr, ph, dn_codesT, dn_scalesT, dnw, I1, D, "dn")
            ph.emit(es)
        with pr.scope():
            ph = Ph(nc, pr=pr, name="ffn1")
            emit_ffn(pr, ph, xin, xout, guw, 0, guw, I1, dnw, I1, True, "f1")
            ph.emit(es)
    return nc

def emit_identity(pr, ph):
    ident = pr.sbuf("ident", [128, 128], BF16, persistent=True)
    ph.op("gpsimd", lambda e: e.iota(ident[:], [[1, 128]], base=0,
          channel_multiplier=-1, allow_small_or_imprecise_dtypes=True),
          writes=[ident])
    ph.op("vector", lambda e: e.tensor_scalar(ident[:], ident[:], 0.0, 1.0,
          OP.is_equal, OP.mult), reads=[ident], writes=[ident])
    return ident


def emit_ln_tail(pr, ph, tag, z, zsq_pair, st1, st2, bc1, bc2, rows, utmp, ost,
                 xout, r, out_toks):
    ones_f32 = pr.t["ones_f32"]
    A = rows[:, 0:RT]; B = rows[:, RT:2 * RT]; C = rows[:, 2 * RT:3 * RT]
    kA, kB, kC = (rows, "A"), (rows, "B"), (rows, "C")
    ph.op("vector", lambda e: e.tensor_scalar_mul(A, st1[:], 1.0 / D),
          reads=[(st1, "acc")], writes=[kA])
    ph.op("vector", lambda e: e.tensor_scalar_mul(B, st2[:], 1.0 / D),
          reads=[(st2, "acc")], writes=[kB])
    ph.op("vector", lambda e: e.tensor_mul(C, A, A), reads=[kA], writes=[kC])
    ph.op("vector", lambda e: e.tensor_sub(B, B, C), reads=[kB, kC], writes=[kB])
    ph.op("vector", lambda e: e.tensor_scalar_add(B, B, LNEPS), reads=[kB], writes=[kB])
    ph.op("vector", lambda e: e.reciprocal(C, B), reads=[kB], writes=[kC])
    ph.op("scalar", lambda e: e.activation(B, C, AF.Sqrt), reads=[kC], writes=[kB])
    ph.op("tensor", lambda e: e.matmul(
        bc1[:], ones_f32[0:1, :], A, start=True, stop=True),
        reads=[kA, ones_f32], writes=[bc1])
    ph.op("tensor", lambda e: e.matmul(
        bc2[:], ones_f32[0:1, :], B, start=True, stop=True),
        reads=[kB, ones_f32], writes=[bc2])
    t0 = r * RT
    nu, no = len(utmp), len(ost)
    for ot in range(DT):
        u = utmp[ot % nu]
        o = ost[ot % no]
        ph.op("vector", lambda e, ot=ot, u=u: e.scalar_tensor_tensor(
            u[:], z[ot][:], 1.0, bc1[:], OP.mult, OP.subtract),
            reads=[z[ot], bc1], writes=[u])
        ph.op("vector", lambda e, ot=ot, u=u, o=o: e.tensor_mul(
            o[:], u[:], bc2[:]),
            reads=[u, bc2], writes=[o])
        tok = ph.op("gpsimd", lambda e, ot=ot, t0=t0, o=o: e.dma_start(
            out=xout[ot * 128:(ot + 1) * 128, t0:t0 + RT], in_=o[:]),
            reads=[o], writes=[(xout, (ot, r))], dma=True,
            chan=f"xo_{tag}_{ot % no}")
        if out_toks is not None:
            out_toks.append(tok)


def emit_l1_attn(pr, es, xin, xout, out_toks=None):
    nc = pr.nc
    Wk1, Wv1, Wg1, Wp1 = pr.t["Wk1"], pr.t["Wv1"], pr.t["Wg1"], pr.t["Wp1"]
    wpb1 = pr.dram_tmp("wpb1", [DT, 128, D], BF16)
    latq, cos1d, sin1d, permT1d, latmaskd = (
        pr.t["latq"], pr.t["cos1"], pr.t["sin1"], pr.t["permT1"], pr.t["latmask"])

    attnb = pr.sbuf("a1_attnb", [128, NB * H * 128], BF16)
    latqb = pr.sbuf("a1_latq", [128, H * NL], BF16)
    cos1 = pr.sbuf("a1_cos1", [128, 128], F32)
    sin1 = pr.sbuf("a1_sin1", [128, 128], F32)
    permT1 = pr.sbuf("a1_permT1", [128, 128], BF16)
    latmask = pr.sbuf("a1_mask", [64, 128], F32)

    with pr.scope():
        ph = Ph(nc, pr=pr, name="a1kv")
        ident = pr.t["ident"]
        karr = [pr.sbuf(f"a1_k{b}", [128, H * 128], BF16) for b in range(NB)]
        varr = [pr.sbuf(f"a1_v{b}", [128, H * 128], BF16) for b in range(NB)]
        stage = [pr.sbuf(f"a1kv_st{i}", [128, 2048], F32) for i in range(2)]
        wkv = [pr.sbuf(f"a1kv_w{i}", [128, 2048], BF16) for i in range(2)]
        xk = [pr.sbuf(f"a1kv_x{b}", [128, DT * 128], BF16) for b in range(NB)]
        xst = [pr.sbuf(f"a1kv_xs{i}", [128, 128], F32) for i in range(2)]
        kev = pr.sbuf("a1kv_kev", [128, 128], BF16)
        tc = pr.sbuf("a1kv_tc", [128, 128], BF16)
        ts = pr.sbuf("a1kv_ts", [128, 128], BF16)
        sm = pr.sbuf("a1kv_sm", [64, 128], F32)
        ngm = pr.sbuf("a1kv_ngm", [64, 1], F32)
        wexp = pr.sbuf("a1kv_wexp", [64, 128], BF16)
        ssum = pr.sbuf("a1kv_ssum", [64, 1], F32)
        rs = pr.sbuf("a1kv_rs", [64, 1], F32)
        wn = pr.sbuf("a1kv_wn", [64, 128], BF16)
        wT = pr.sbuf("a1kv_wT", [128, 64], BF16)
        vT = pr.sbuf("a1kv_vT", [128, 128], BF16)
        pk = pr.psum("a1kv_pk", [128, 128])
        pr_ = pr.psum("a1kv_pr", [128, 128])
        psc = pr.psum("a1kv_psc", [64, 128])
        ptw = pr.psum("a1kv_ptw", [128, 64], BF16)
        ptv = pr.psum("a1kv_ptv", [128, 128], BF16)
        pat = pr.psum("a1kv_pat", [128, 128])

        ph.op("sync", lambda e: e.dma_start(out=cos1[:], in_=cos1d[:, :]),
              writes=[cos1], dma=True)
        ph.op("sync", lambda e: e.dma_start(out=sin1[:], in_=sin1d[:, :]),
              writes=[sin1], dma=True)
        ph.op("sync", lambda e: e.dma_start(out=permT1[:], in_=permT1d[:, :]),
              writes=[permT1], dma=True)
        ph.op("sync", lambda e: e.dma_start(out=latmask[:], in_=latmaskd[:, :]),
              writes=[latmask], dma=True)
        lqs = stage[0]
        ph.op("sync", lambda e: e.dma_start(
            out=lqs[:, 0:H * NL],
            in_=bass.AP(latq, 0, [[NL, 128], [128 * NL, H], [1, NL]])),
            writes=[stage[0]], dma=True)
        ph.op("vector", lambda e: e.tensor_copy(latqb[:], lqs[:, 0:H * NL]),
              reads=[stage[0]], writes=[latqb])

        for b in range(NB):
            for di in range(DT):
                xs = xst[di % 2]
                ph.op("gpsimd", lambda e, xs=xs, b=b, di=di: e.dma_start(
                    out=xs[:], in_=xin[di * 128:(di + 1) * 128,
                                       b * T:b * T + 128]),
                    writes=[xs], dma=True)
                ph.op("vector", lambda e, xs=xs, b=b, di=di: e.tensor_copy(
                    xk[b][:, di * 128:(di + 1) * 128], xs[:]),
                    reads=[xs], writes=[(xk[b], di)])

        wpc = pr.sbuf("a1kv_wpc", [128, D], BF16)
        for mt in range(DT):
            st = stage[mt % 2]
            ph.op("sync", lambda e, st=st, mt=mt: e.dma_start(
                out=st[:], in_=Wp1[mt, :, :]), writes=[st], dma=True)
            ph.op("vector", lambda e, st=st: e.tensor_copy(wpc[:], st[:]),
                  reads=[st], writes=[wpc])
            ph.op("gpsimd", lambda e, mt=mt: e.dma_start(
                out=wpb1[mt, :, :], in_=wpc[:]), reads=[wpc],
                writes=[(wpb1, mt)], dma=True)
        for mat, dest, dorope in ((Wk1, karr, True), (Wv1, varr, False)):
            for et in range(H):
                st = stage[et % 2]
                wb = wkv[et % 2]
                ph.op("sync", lambda e, st=st, et=et, mat=mat: e.dma_start(
                    out=st[:], in_=mat[et, :, :]), writes=[st], dma=True)
                ph.op("vector", lambda e, st=st, wb=wb: e.tensor_copy(wb[:], st[:]),
                      reads=[st], writes=[wb])
                for b in range(NB):
                    def kmm(e, wb=wb, b=b):
                        ins = None
                        for di in range(DT):
                            ins = e.matmul(pk[:], wb[:, di * 128:(di + 1) * 128],
                                           xk[b][:, di * 128:(di + 1) * 128],
                                           start=(di == 0), stop=(di == DT - 1))
                        return ins
                    ph.op("tensor", kmm,
                          reads=[wb] + [(xk[b], di) for di in range(DT)],
                          writes=[pk])
                    if dorope:
                        ph.op("vector", lambda e: e.tensor_copy(kev[:], pk[:]),
                              reads=[pk], writes=[kev])
                        ph.op("tensor", lambda e: e.matmul(
                            pr_[:], permT1[:], kev[:], start=True, stop=True),
                            reads=[kev, permT1], writes=[pr_])
                        ph.op("vector", lambda e: e.tensor_mul(tc[:], kev[:], cos1[:]),
                              reads=[kev, cos1], writes=[tc])
                        ph.op("vector", lambda e: e.tensor_mul(ts[:], pr_[:], sin1[:]),
                              reads=[pr_, sin1], writes=[ts])
                        ph.op("vector", lambda e, dest=dest, b=b, et=et: e.tensor_add(
                            dest[b][:, et * 128:(et + 1) * 128], tc[:], ts[:]),
                            reads=[tc, ts], writes=[(dest[b], et)])
                    else:
                        ph.op("vector", lambda e, dest=dest, b=b, et=et: e.tensor_copy(
                            dest[b][:, et * 128:(et + 1) * 128], pk[:]),
                            reads=[pk], writes=[(dest[b], et)])

        for b in range(NB):
            for h in range(H):
                ph.op("tensor", lambda e, b=b, h=h: e.matmul(
                    psc[:], latqb[:, h * NL:(h + 1) * NL],
                    karr[b][:, h * 128:(h + 1) * 128], start=True, stop=True),
                    reads=[latqb, (karr[b], h)], writes=[psc])
                ph.op("vector", lambda e: e.scalar_tensor_tensor(
                    sm[:], psc[:], SCALE1, latmask[:], OP.mult, OP.add),
                    reads=[psc, latmask], writes=[sm])
                ph.op("vector", lambda e: e.reduce_max(
                    ngm[:], sm[:], axis=AX.X, negate=True),
                    reads=[sm], writes=[ngm])
                ph.op("scalar", lambda e: e.activation(
                    wexp[:], sm[:], AF.Exp, bias=ngm[:, 0:1]),
                    reads=[sm, ngm], writes=[wexp])
                ph.op("vector", lambda e: e.reduce_sum(ssum[:], wexp[:], axis=AX.X),
                      reads=[wexp], writes=[ssum])
                ph.op("vector", lambda e: e.reciprocal(rs[:], ssum[:]),
                      reads=[ssum], writes=[rs])
                ph.op("vector", lambda e: e.tensor_scalar_mul(
                    wn[:], wexp[:], rs[:, 0:1]), reads=[wexp, rs], writes=[wn])
                ph.op("tensor", lambda e: e.transpose(ptw[:], wn[:], ident[0:64, 0:64]),
                      reads=[wn, ident], writes=[ptw])
                ph.op("vector", lambda e: e.tensor_copy(wT[:], ptw[:]),
                      reads=[ptw], writes=[wT])
                ph.op("tensor", lambda e, b=b, h=h: e.transpose(
                    ptv[:], varr[b][:, h * 128:(h + 1) * 128], ident[:]),
                    reads=[(varr[b], h), ident], writes=[ptv])
                ph.op("vector", lambda e: e.tensor_copy(vT[:], ptv[:]),
                      reads=[ptv], writes=[vT])
                r0 = (h % 2) * 64
                ph.op("tensor", lambda e, r0=r0: e.matmul(
                    pat[r0:r0 + 64, :], wT[:], vT[:], start=True, stop=True),
                    reads=[wT, vT], writes=[pat])
                ph.op("vector", lambda e, b=b, h=h, r0=r0: e.tensor_copy(
                    attnb[r0:r0 + 64, (b * H + h) * 128:(b * H + h + 1) * 128],
                    pat[r0:r0 + 64, :]),
                    reads=[pat], writes=[(attnb, (b, h))])
        ph.emit(es)

    with pr.scope():
        ph = Ph(nc, pr=pr, name="a1main")
        NG = 1024 // 128
        wgres = pr.sbuf("a1m_wg", [128, NG * 2048], BF16)
        wpst = [pr.sbuf(f"a1m_wp{i}", [128, D], BF16) for i in range(2)]
        xb2 = [[pr.sbuf(f"a1m_xb{p}_{i}", [128, RT], BF16) for i in range(DT)]
               for p in range(2)]
        gexp = [pr.sbuf(f"a1m_ge{i}", [128, RT], BF16) for i in range(2)]
        gr = [pr.sbuf(f"a1m_gr{i}", [1, RT], F32) for i in range(2)]
        warr = [pr.sbuf(f"a1m_wa{i}", [128, RT], BF16) for i in range(H)]
        z = [pr.sbuf(f"a1m_z{i}", [128, RT], BF16) for i in range(DT)]
        zsq = [pr.sbuf(f"a1m_q{i}", [128, RT], BF16) for i in range(DT)]
        utmp = [pr.sbuf(f"a1m_u{i}", [128, RT], F32) for i in range(4)]
        ost = [pr.sbuf(f"a1m_o{i}", [128, RT], F32) for i in range(4)]
        rows = pr.sbuf("a1m_rows", [1, 3 * RT], F32)
        work = [pr.psum(f"a1m_pw{i}", [128, RT]) for i in range(3)]
        gs = pr.psum("a1m_gs", [1, RT])
        st1 = pr.psum("a1m_st1", [1, RT])
        st2 = pr.psum("a1m_st2", [1, RT])
        bc1 = pr.psum("a1m_bc1", [128, RT])
        bc2 = pr.psum("a1m_bc2", [128, RT])
        ones_bf = pr.t["ones_bf"]

        for mt in range(NG):
            ph.op("gpsimd", lambda e, mt=mt: e.dma_start(
                out=wgres[:, mt * 2048:(mt + 1) * 2048], in_=Wg1[mt, :, :]),
                writes=[(wgres, "all")], dma=True)


        def load_x(r):
            if r >= NR:
                return
            t0 = r * RT
            xb = xb2[r % 2]
            for ot in range(DT):
                ph.op("gpsimd", lambda e, ot=ot, t0=t0, xb=xb: e.dma_start(
                    out=xb[ot][:], in_=xin[ot * 128:(ot + 1) * 128, t0:t0 + RT]),
                    writes=[xb[ot]], dma=True, chan=f"a1m_xbg{r % 2}")

        load_x(0)
        for r in range(NR):
            t0 = r * RT
            b = r // (T // RT)
            xb = xb2[r % 2]
            ones_f32 = pr.t["ones_f32"]
            gsb = [gs[:], st1[:]]
            gskey = ["a1m_gs", "a1m_st1"]
            for og in range(NG):
                gp = og % 2
                w = work[gp]
                ge = gexp[gp]
                def gmm(e, og=og, w=w, xb=xb):
                    ins = None
                    for di in range(DT):
                        c0 = og * 2048 + di * 128
                        ins = e.matmul(w[:], wgres[:, c0:c0 + 128],
                                       xb[di][:], start=(di == 0), stop=(di == DT - 1))
                    return ins
                ph.op("tensor", gmm,
                      reads=[(wgres, "all")] + [xb[di] for di in range(DT)],
                      writes=[w])
                ph.op("scalar", lambda e, w=w, ge=ge: e.activation(
                    ge[:], w[:], AF.Exp, scale=SCALE1),
                    reads=[w], writes=[ge])
                for half in (0, 1):
                    r0 = half * 64
                    ph.op("tensor", lambda e, ge=ge, r0=r0, half=half: e.matmul(
                        gsb[half], ones_bf[r0:r0 + 64, 0:1], ge[r0:r0 + 64, :],
                        start=True, stop=True),
                        reads=[ge, ones_bf], writes=[(pr.t[gskey[half]], "g")])
                    g = gr[half]
                    ph.op("vector", lambda e, g=g, half=half: e.reciprocal(
                        g[:], gsb[half]),
                        reads=[(pr.t[gskey[half]], "g")], writes=[g])
                    ph.op("tensor", lambda e, g=g, r0=r0: e.matmul(
                        bc1[r0:r0 + 64, :], ones_f32[0:1, 0:64], g[:],
                        start=True, stop=True),
                        reads=[g, ones_f32], writes=[(pr.t["a1m_bc1"], "g")])
                ph.op("vector", lambda e, ge=ge: e.tensor_mul(
                    ge[:], ge[:], bc1[:]),
                    reads=[ge, (pr.t["a1m_bc1"], "g")], writes=[ge])
                for half in (0, 1):
                    h = 2 * og + half
                    r0 = half * 64
                    wv = work[2]
                    ph.op("tensor", lambda e, b=b, h=h, r0=r0, wv=wv, ge=ge: e.matmul(
                        wv[:], attnb[r0:r0 + 64, (b * H + h) * 128:(b * H + h + 1) * 128],
                        ge[r0:r0 + 64, :], start=True, stop=True),
                        reads=[(attnb, (b, h)), ge], writes=[wv])
                    ph.op("scalar", lambda e, h=h, wv=wv: e.copy(
                        warr[h][:], wv[:]), reads=[wv], writes=[warr[h]])
            for ot in range(DT):
                w = work[1 + ot % 2]
                wp = wpst[ot % 2]
                ph.op("sync", lambda e, ot=ot, wp=wp: e.dma_start(
                    out=wp[:], in_=wpb1[ot, :, :]), writes=[wp], dma=True)
                def pmm(e, ot=ot, w=w, wp=wp):
                    ins = None
                    for et in range(H):
                        ins = e.matmul(w[:],
                                       wp[:, et * 128:(et + 1) * 128],
                                       warr[et][:], start=(et == 0), stop=(et == H - 1))
                    return ins
                ph.op("tensor", pmm,
                      reads=[wp] + [warr[et] for et in range(H)],
                      writes=[w])
                q = zsq[ot]
                ph.op("vector", lambda e, ot=ot, w=w, xb=xb: e.scalar_tensor_tensor(
                    z[ot][:], w[:], 1.0, xb[ot][:], OP.mult, OP.add),
                    reads=[w, xb[ot]], writes=[z[ot]])
                ph.op("scalar", lambda e, ot=ot, q=q: e.activation(
                    q[:], z[ot][:], AF.Square), reads=[z[ot]], writes=[q])
                ph.op("tensor", lambda e, ot=ot: e.matmul(
                    st1[:], ones_bf[:, 0:1], z[ot][:],
                    start=(ot == 0), stop=(ot == DT - 1)),
                    reads=[z[ot], ones_bf], writes=[(st1, "acc"), (st1, "g")])
                ph.op("tensor", lambda e, ot=ot, q=q: e.matmul(
                    st2[:], ones_bf[:, 0:1], q[:],
                    start=(ot == 0), stop=(ot == DT - 1)),
                    reads=[q, ones_bf], writes=[(st2, "acc")])
            load_x(r + 1)
            emit_ln_tail(pr, ph, "a1", z, zsq, st1, st2, bc1, bc2, rows,
                         utmp, ost, xout, r, out_toks)
        ph.emit(es)


def build_l1_attn_program():
    nc = bass.Bass("TRN2", target_bir_lowering=False)
    with ExitStack() as es:
        pr = Prog(nc, es)
        xin = pr.dram_in("xin", [D, NT], F32)
        xout = pr.dram_out("xout", [D, NT], F32)
        pr.dram_in("Wk1", [H, 128, D], F32)
        pr.dram_in("Wv1", [H, 128, D], F32)
        pr.dram_in("Wg1", [8, 128, D], F32)
        pr.dram_in("Wp1", [DT, 128, D], F32)
        pr.dram_in("latq", [H, 128, NL], F32)
        pr.dram_in("cos1", [128, 128], F32)
        pr.dram_in("sin1", [128, 128], F32)
        pr.dram_in("permT1", [128, 128], BF16)
        pr.dram_in("latmask", [64, 128], F32)
        with pr.scope():
            ph0 = Ph(nc, pr=pr, name="prep")
            emit_ones(pr, ph0)
            emit_identity(pr, ph0)
            ph0.emit(es)
        with pr.scope():
            emit_l1_attn(pr, es, xin, xout)
    return nc

A0_DEBUG = False
A0_DEBUG2 = False


def emit_l0_attn(pr, es, xin, xout, out_toks=None, pre_ops=None):
    nc = pr.nc
    wqT, wkT, wvT, woT = pr.t["wqT"], pr.t["wkT"], pr.t["wvT"], pr.t["woT"]
    SC = 512
    NSC = T // SC
    NCH = SC // CH
    lam = np.exp2(-8.0 * np.arange(1, H + 1, dtype=np.float64) / H)
    blk_dec = [float(np.exp(-lam[h] * CH)) for h in range(H)]

    wqb = pr.dram_tmp("wqb", [H, 128, D], BF16)
    wkb = pr.dram_tmp("wkb", [HKV, 128, D], BF16)
    wob = pr.dram_tmp("wob", [DT, 128, D], BF16)
    wk_rhs = pr.dram_tmp("wk_rhs", [DT, 128, HKV * 128], BF16)
    wv_rhs = pr.dram_tmp("wv_rhs", [DT, 128, HKV * 128], BF16)

    cos0 = pr.sbuf("a0_cos0", [128, T], BF16)
    sin0 = pr.sbuf("a0_sin0", [128, T], BF16)
    cos0tm = pr.sbuf("a0_cos0tm", [CH, (T // CH) * 128], BF16)
    sin0tm = pr.sbuf("a0_sin0tm", [CH, (T // CH) * 128], BF16)
    permT0 = pr.sbuf("a0_permT0", [128, 128], BF16)
    diagT = pr.sbuf("a0_diagT", [64, H * CH], BF16)
    qdec = pr.sbuf("a0_qdec", [128, H * CH], BF16)
    kdecT = pr.sbuf("a0_kdecT", [CH, H], F32)
    kdecTM = pr.sbuf("a0_kdecTM", [CH, H * DH], BF16)

    with pr.scope():
        ph = pr.t.get("_shared_prep_ph") or Ph(nc, pr=pr, name="a0prep")
        for dst, srcn in ((permT0, "permT0"), (kdecT, "kdecT")):
            ph.op("sync", lambda e, dst=dst, srcn=srcn: e.dma_start(
                out=dst[:], in_=pr.t[srcn][:, :]), writes=[dst], dma=True)
        for dst, srcn in ((cos0, "cos0"), (sin0, "sin0"),
                          (cos0tm, "cos0tm"), (sin0tm, "sin0tm"),
                          (diagT, "diagT"), (qdec, "qdec"),
                          (kdecTM, "kdecTM")):
            ph.op("gpsimd", lambda e, dst=dst, srcn=srcn: e.dma_start(
                out=dst[:], in_=pr.t[srcn][:, :]), writes=[dst], dma=True)
        stage = [pr.sbuf(f"a0p_st{i}", [128, D], F32) for i in range(2)]
        wcast = [pr.sbuf(f"a0p_w{i}", [128, D], BF16) for i in range(2)]
        jobs = []
        for src_, dst, n, rhs in ((wqT, wqb, H, None), (wkT, wkb, HKV, wk_rhs),
                                  (wvT, None, HKV, wv_rhs), (woT, wob, DT, None)):
            for mt in range(n):
                jobs.append((src_, dst, mt, rhs))
        for i, (src_, dst, mt, rhs) in enumerate(jobs):
            st, wc = stage[i % 2], wcast[i % 2]
            ph.op("sync", lambda e, st=st, src_=src_, mt=mt: e.dma_start(
                out=st[:], in_=src_[mt, :, :]), writes=[st], dma=True)
            ph.op("vector", lambda e, st=st, wc=wc: e.tensor_copy(wc[:], st[:]),
                  reads=[st], writes=[wc])
            if dst is not None:
                ph.op("gpsimd", lambda e, wc=wc, dst=dst, mt=mt: e.dma_start(
                    out=dst[mt, :, :], in_=wc[:]), reads=[wc],
                    writes=[(dst, mt)], dma=True)
            if rhs is not None:
                for kt in range(DT):
                    ph.op("gpsimd", lambda e, wc=wc, rhs=rhs, mt=mt, kt=kt: e.dma_start(
                        out=rhs[kt, :, mt * 128:(mt + 1) * 128],
                        in_=wc[:, kt * 128:(kt + 1) * 128]), reads=[wc],
                        writes=[(rhs, (mt, kt))], dma=True,
                        chan=f"rhs_{kt % 4}")
        if pr.t.get("_shared_prep_ph") is None:
            ph.emit(es)
        else:
            ph.emit(es)
            pr.t["_shared_prep_ph"] = None

    with pr.scope():
        ph = Ph(nc, pr=pr, name="a0main")
        if pre_ops is not None:
            pre_ops(ph)
        ones_bf = pr.t["ones_bf"]
        ones_f32 = pr.t["ones_f32"]
        xb2 = [[pr.sbuf(f"a0m_xb{p}_{i}", [128, SC], BF16) for i in range(DT)]
               for p in range(2)]
        wbuf = [pr.sbuf(f"a0m_w{i}", [128, D], BF16) for i in range(2)]
        wobuf = wbuf
        kvres = pr.sbuf("a0m_kvres", [128, DT * 512], BF16)
        qarr = pr.sbuf("a0m_q", [128, H * SC], BF16)
        karr = pr.sbuf("a0m_k", [128, HKV * SC], BF16)
        ktm_raw = pr.sbuf("a0m_ktr", [CH, NCH * 512], BF16)
        ktm = ktm_raw
        vtm = pr.sbuf("a0m_vtm", [CH, NCH * 512], BF16)
        obuf = pr.sbuf("a0m_o", [128, H * SC], BF16)
        kv_sb = [[pr.sbuf(f"a0m_kv{h}_{p}", [128, 128], BF16) for p in range(2)]
                 for h in range(H)]
        qev = pr.sbuf("a0m_qev", [128, SC], BF16)
        tc = pr.sbuf("a0m_tc", [128, SC], BF16)
        ts2 = pr.sbuf("a0m_ts", [128, SC], BF16)
        qd = pr.sbuf("a0m_qd", [128, H * CH], BF16)
        kdT = [pr.sbuf(f"a0m_kdT{i}", [64, 128], BF16) for i in range(2)]
        z = [pr.sbuf(f"a0m_z{i}", [128, SC], BF16) for i in range(DT)]
        zsq = [pr.sbuf(f"a0m_q{i}", [128, SC], BF16) for i in range(DT)]
        utmp = [pr.sbuf(f"a0m_u{i}", [128, SC], F32) for i in range(2)]
        ost = [pr.sbuf(f"a0m_os{i}", [128, SC], F32) for i in range(2)]
        rows = pr.sbuf("a0m_rows", [1, 3 * SC], F32)

        qp = pr.psum("a0m_qp", [128, SC])
        qr = pr.psum("a0m_qr", [128, SC])
        pj = pr.psum("a0m_pj", [128, SC])
        bc1 = pr.psum("a0m_bc1", [128, SC])
        bc2 = pr.psum("a0m_bc2", [128, SC])
        pkv0 = pr.psum("a0m_pkv0", [128, 128])
        pkv1 = pr.psum("a0m_pkv1", [128, 128])
        stx = pr.psum("a0m_st", [1, SC])
        po = [qr[:, 0:CH], bc2[:, 0:CH]]
        pkv = [pkv0[:, :], pkv1[:, :]]
        pokey = ["qr", "bc2"]
        pkvkey = ["pkv0", "pkv1"]
        psh = [pj, bc1]
        pshkey = ["pj", "bc1"]
        smaskb = pr.sbuf("a0m_smb", [64, H * CH], BF16)


        def load_x(b, S):
            if S >= NSC:
                b, S = b + 1, 0
            if b >= NB:
                return
            t0 = b * T + S * SC
            xb = xb2[(b * NSC + S) % 2]
            for di in range(DT):
                ph.op("gpsimd", lambda e, di=di, t0=t0, xb=xb: e.dma_start(
                    out=xb[di][:], in_=xin[di * 128:(di + 1) * 128, t0:t0 + SC]),
                    writes=[xb[di]], dma=True, chan=f"a0m_xbg{(b * NSC + S) % 2}")

        load_x(0, 0)
        for b in range(NB):
            for h in range(H):
                ph.op("gpsimd", lambda e, h=h, b=b: e.memset(
                    kv_sb[h][0][:], 0.0), writes=[(kv_sb[h][0], "kv")])
            for S in range(NSC):
                t0 = b * T + S * SC
                p0 = S * SC
                r = (b * T + S * SC) // RT
                xb = xb2[(b * NSC + S) % 2]
                for (wsrc, n_et, dest) in ((wqb, H, qarr), (wkb, HKV, karr)):
                    for et in range(n_et):
                        wb = wbuf[et % 2]
                        ph.op("sync", lambda e, wb=wb, wsrc=wsrc, et=et: e.dma_start(
                            out=wb[:], in_=wsrc[et, :, :]), writes=[wb], dma=True)
                        def qmm(e, wb=wb, xb=xb):
                            ins = None
                            for di in range(DT):
                                ins = e.matmul(qp[:], wb[:, di * 128:(di + 1) * 128],
                                               xb[di][:], start=(di == 0),
                                               stop=(di == DT - 1))
                            return ins
                        ph.op("tensor", qmm,
                              reads=[wb] + [xb[di] for di in range(DT)], writes=[qp])
                        ph.op("vector", lambda e: e.tensor_copy(qev[:], qp[:]),
                              reads=[qp], writes=[qev])
                        ph.op("tensor", lambda e: e.matmul(
                            qr[:], permT0[:], qev[:], start=True, stop=True),
                            reads=[qev, permT0], writes=[(qr, "qr")])
                        ph.op("vector", lambda e, p0=p0: e.tensor_mul(
                            tc[:], qev[:], cos0[:, p0:p0 + SC]),
                            reads=[qev, cos0], writes=[tc])
                        ph.op("vector", lambda e, p0=p0: e.tensor_mul(
                            ts2[:], qr[:], sin0[:, p0:p0 + SC]),
                            reads=[(qr, "qr"), sin0], writes=[ts2])
                        ph.op("vector", lambda e, dest=dest, et=et: e.tensor_add(
                            dest[:, et * SC:(et + 1) * SC], tc[:], ts2[:]),
                            reads=[tc, ts2], writes=[(dest, et)])
                for (wsc, dest, dkey) in ((wk_rhs, ktm_raw, "ktr"), (wv_rhs, vtm, "vtm")):
                    for kt in range(DT):
                        ph.op("sync", lambda e, kt=kt, wsc=wsc: e.dma_start(
                            out=kvres[:, kt * 512:(kt + 1) * 512], in_=wsc[kt, :, :]),
                            writes=[kvres], dma=True, chan="kvres")
                    for c in range(NCH):
                        def tmm(e, c=c, xb=xb):
                            ins = None
                            for di in range(DT):
                                ins = e.matmul(
                                    qp[0:64, :],
                                    xb[di][:, c * CH:(c + 1) * CH],
                                    kvres[:, di * 512:(di + 1) * 512],
                                    start=(di == 0), stop=(di == DT - 1))
                            return ins
                        ph.op("tensor", tmm,
                              reads=[kvres] + [xb[di] for di in range(DT)],
                              writes=[qp])
                        ph.op("vector", lambda e, dest=dest, c=c: e.tensor_copy(
                            dest[:, c * 512:(c + 1) * 512], qp[0:64, :]),
                            reads=[qp], writes=[(dest, c)])
                cA = S * NCH * 64
                kraw_reads = [(ktm_raw, cc2) for cc2 in range(NCH)]
                pairs = [(c, g) for c in range(NCH) for g in range(HKV)]
                if A0_DEBUG2 and b == 0 and S == 0:
                    pairs = pairs[:1]
                for c, g in pairs:
                    o0 = c * 512 + g * 128
                    tco = (S * NCH + c) * 128
                    ph.op("vector", lambda e, o0=o0, tco=tco: e.tensor_mul(
                        tc[0:CH, 0:128], ktm_raw[:, o0:o0 + 128],
                        cos0tm[:, tco:tco + 128]),
                        reads=kraw_reads + [cos0tm], writes=[(tc, "r")])
                    ph.op("vector", lambda e, o0=o0, tco=tco: e.tensor_mul(
                        ts2[0:CH, 0:128], ktm_raw[:, o0:o0 + 128],
                        sin0tm[:, tco:tco + 128]),
                        reads=kraw_reads + [sin0tm], writes=[(ts2, "r")])
                    ph.op("vector", lambda e, o0=o0: e.tensor_sub(
                        ktm_raw[:, o0:o0 + 64],
                        tc[0:CH, 0:64], ts2[0:CH, 64:128]),
                        reads=[(tc, "r"), (ts2, "r")] + kraw_reads,
                        writes=[(ktm_raw, c)])
                    ph.op("vector", lambda e, o0=o0: e.tensor_add(
                        ktm_raw[:, o0 + 64:o0 + 128],
                        tc[0:CH, 64:128], ts2[0:CH, 0:64]),
                        reads=[(tc, "r"), (ts2, "r")] + kraw_reads,
                        writes=[(ktm_raw, c)])
                for c in range(NCH):
                    cg = S * NCH + c
                    par = cg % 2
                    j0 = c * CH
                    ph.op("vector", lambda e, j0=j0: e.tensor_mul(
                        qd[:], bass.AP(qarr, j0, [[qarr[:].ap[0][0], 128], [SC, H], [1, CH]]),
                        qdec[:]), reads=[(qarr, et) for et in range(H)] + [qdec],
                        writes=[qd])
                    for hb in range(2):
                        bank = psh[hb]
                        bkey = (pr.t[f"a0m_{pshkey[hb]}"], pshkey[hb])
                        def stmm(e, hb=hb, j0=j0, bank=bank):
                            ins = None
                            for hh in range(8):
                                h = hb * 8 + hh
                                g = h // 4
                                ins = e.matmul(
                                    bank[0:64, hh * CH:(hh + 1) * CH],
                                    karr[:, g * SC + j0:g * SC + j0 + CH],
                                    qarr[:, h * SC + j0:h * SC + j0 + CH],
                                    start=True, stop=True)
                            return ins
                        ph.op("tensor", stmm,
                              reads=[(karr, g2) for g2 in range(HKV)]
                                    + [(qarr, h2) for h2 in range(hb * 8, hb * 8 + 8)],
                              writes=[bkey])
                        ph.op("vector", lambda e, hb=hb, bank=bank: e.tensor_mul(
                            smaskb[:, hb * 8 * CH:(hb + 1) * 8 * CH],
                            bank[0:64, 0:8 * CH],
                            diagT[:, hb * 8 * CH:(hb + 1) * 8 * CH]),
                            reads=[bkey, diagT], writes=[(smaskb, hb)])
                    for h in range(H):
                        g = h // 4
                        hp = h % 2
                        kt_ = kdT[hp]
                        ph.op("vector", lambda e, h=h, g=g, c=c, kt_=kt_: e.tensor_mul(
                            kt_[:], ktm[:, c * 512 + g * 128:c * 512 + (g + 1) * 128],
                            kdecTM[:, h * DH:(h + 1) * DH]),
                            reads=[(ktm, c), kdecTM], writes=[kt_])
                        def omm(e, h=h, g=g, c=c, par=par, hp=hp):
                            e.matmul(po[hp], vtm[:, c * 512 + g * 128:c * 512 + (g + 1) * 128],
                                     smaskb[0:64, h * CH:(h + 1) * CH],
                                     start=True, stop=False)
                            return e.matmul(po[hp], kv_sb[h][par][:],
                                            qd[:, h * CH:(h + 1) * CH],
                                            start=False, stop=True)
                        ph.op("tensor", omm,
                              reads=[(vtm, c), (smaskb, h // 8), (kv_sb[h][par], "kv"), qd],
                              writes=[(pr.t[f"a0m_{pokey[hp]}"], pokey[hp])])
                        ph.op("scalar", lambda e, h=h, j0=j0, hp=hp: e.copy(
                            obuf[:, h * SC + j0:h * SC + j0 + CH], po[hp]),
                            reads=[(pr.t[f"a0m_{pokey[hp]}"], pokey[hp])],
                            writes=[(obuf, h)])
                        ph.op("tensor", lambda e, g=g, c=c, hp=hp, kt_=kt_: e.matmul(
                            pkv[hp], kt_[:], vtm[:, c * 512 + g * 128:c * 512 + (g + 1) * 128],
                            start=True, stop=True),
                            reads=[kt_, (vtm, c)],
                            writes=[(pr.t[f"a0m_{pkvkey[hp]}"], pkvkey[hp])])
                        ph.op("vector", lambda e, h=h, par=par, hp=hp, bd=blk_dec[h]: e.scalar_tensor_tensor(
                            kv_sb[h][1 - par][:], kv_sb[h][par][:], bd, pkv[hp],
                            OP.mult, OP.add),
                            reads=[(kv_sb[h][par], "kv"),
                                   (pr.t[f"a0m_{pkvkey[hp]}"], pkvkey[hp])],
                            writes=[(kv_sb[h][1 - par], "kv")])
                for ot in range(DT):
                    wb = wobuf[ot % 2]
                    ph.op("sync", lambda e, wb=wb, ot=ot: e.dma_start(
                        out=wb[:], in_=wob[ot, :, :]), writes=[wb], dma=True)
                    def pmm(e, wb=wb):
                        ins = None
                        for et in range(H):
                            ins = e.matmul(pj[:], wb[:, et * 128:(et + 1) * 128],
                                           obuf[:, et * SC:(et + 1) * SC],
                                           start=(et == 0), stop=(et == H - 1))
                        return ins
                    ph.op("tensor", pmm,
                          reads=[wb] + [(obuf, et) for et in range(H)],
                          writes=[(pr.t["a0m_pj"], "pj")])
                    ph.op("vector", lambda e, ot=ot, xb=xb: e.scalar_tensor_tensor(
                        z[ot][:], pj[:], 1.0, xb[ot][:], OP.mult, OP.add),
                        reads=[(pr.t["a0m_pj"], "pj"), xb[ot]], writes=[z[ot]])
                    ph.op("scalar", lambda e, ot=ot: e.activation(
                        zsq[ot][:], z[ot][:], AF.Square),
                        reads=[z[ot]], writes=[zsq[ot]])
                load_x(b, S + 1)
                R = rows
                A = R[:, 0:SC]; B2 = R[:, SC:2 * SC]; C2 = R[:, 2 * SC:3 * SC]
                kA, kB, kC = (rows, "A"), (rows, "B"), (rows, "C")
                def s1mm(e):
                    ins = None
                    for ot in range(DT):
                        ins = e.matmul(stx[:], ones_bf[:, 0:1], z[ot][:],
                                       start=(ot == 0), stop=(ot == DT - 1))
                    return ins
                ph.op("tensor", s1mm, reads=[z[ot] for ot in range(DT)] + [ones_bf],
                      writes=[(stx, "acc")])
                ph.op("vector", lambda e: e.tensor_scalar_mul(A, stx[:], 1.0 / D),
                      reads=[(stx, "acc")], writes=[kA])
                def s2mm(e):
                    ins = None
                    for ot in range(DT):
                        ins = e.matmul(stx[:], ones_bf[:, 0:1], zsq[ot][:],
                                       start=(ot == 0), stop=(ot == DT - 1))
                    return ins
                ph.op("tensor", s2mm, reads=[zsq[ot] for ot in range(DT)] + [ones_bf],
                      writes=[(stx, "acc")])
                ph.op("vector", lambda e: e.tensor_scalar_mul(B2, stx[:], 1.0 / D),
                      reads=[(stx, "acc")], writes=[kB])
                ph.op("vector", lambda e: e.tensor_mul(C2, A, A), reads=[kA], writes=[kC])
                ph.op("vector", lambda e: e.tensor_sub(B2, B2, C2), reads=[kB, kC], writes=[kB])
                ph.op("vector", lambda e: e.tensor_scalar_add(B2, B2, LNEPS),
                      reads=[kB], writes=[kB])
                ph.op("vector", lambda e: e.reciprocal(C2, B2), reads=[kB], writes=[kC])
                ph.op("scalar", lambda e: e.activation(B2, C2, AF.Sqrt),
                      reads=[kC], writes=[kB])
                ph.op("tensor", lambda e: e.matmul(
                    bc1[:], ones_f32[0:1, :], A, start=True, stop=True),
                    reads=[kA, ones_f32], writes=[(pr.t["a0m_bc1"], "bc1")])
                ph.op("tensor", lambda e: e.matmul(
                    bc2[:], ones_f32[0:1, :], B2, start=True, stop=True),
                    reads=[kB, ones_f32], writes=[(pr.t["a0m_bc2"], "bc2")])
                for ot in range(DT):
                    u = utmp[ot % 2]
                    o = ost[ot % 2]
                    ph.op("vector", lambda e, ot=ot, u=u: e.scalar_tensor_tensor(
                        u[:], z[ot][:], 1.0, bc1[:], OP.mult, OP.subtract),
                        reads=[z[ot], (pr.t["a0m_bc1"], "bc1")], writes=[u])
                    ph.op("vector", lambda e, u=u, o=o: e.tensor_mul(
                        o[:], u[:], bc2[:]),
                        reads=[u, (pr.t["a0m_bc2"], "bc2")], writes=[o])
                    tok = ph.op("gpsimd", lambda e, ot=ot, t0=t0, o=o: e.dma_start(
                        out=xout[ot * 128:(ot + 1) * 128, t0:t0 + SC], in_=o[:]),
                        reads=[o], writes=[(xout, (ot, r))], dma=True,
                        chan=f"xo_a0_{ot % 2}")
                    if out_toks is not None:
                        out_toks.append(tok)
        ph.emit(es)


def build_l0_attn_program():
    nc = bass.Bass("TRN2", target_bir_lowering=False)
    with ExitStack() as es:
        pr = Prog(nc, es)
        xin = pr.dram_in("xin", [D, NT], F32)
        xout = pr.dram_out("xout", [D, NT], F32)
        pr.dram_in("wqT", [H, 128, D], F32)
        pr.dram_in("wkT", [HKV, 128, D], F32)
        pr.dram_in("wvT", [HKV, 128, D], F32)
        pr.dram_in("woT", [DT, 128, D], F32)
        pr.dram_in("cos0", [128, T], F32)
        pr.dram_in("sin0", [128, T], F32)
        pr.dram_in("permT0", [128, 128], BF16)
        pr.dram_in("diagT", [64, H * CH], F32)
        pr.dram_in("qdec", [128, H * CH], F32)
        pr.dram_in("cos0tm", [CH, (T // CH) * 128], F32)
        pr.dram_in("sin0tm", [CH, (T // CH) * 128], F32)
        pr.dram_in("kdecT", [CH, H], F32)
        pr.dram_in("kdecTM", [CH, H * DH], F32)
        if A0_DEBUG:
            pr.dram_out("dbg_q", [128, H * 512], BF16)
            pr.dram_out("dbg_k", [128, HKV * 512], BF16)
            pr.dram_out("dbg_ktm", [CH, 8 * 512], BF16)
            pr.dram_out("dbg_vtm", [CH, 8 * 512], BF16)
            pr.dram_out("dbg_ktr", [CH, 8 * 512], BF16)
            pr.dram_out("dbg_cos", [CH, (T // CH) * 64], BF16)
            pr.dram_out("dbg_sin", [CH, (T // CH) * 64], BF16)
        if A0_DEBUG2:
            pr.dram_out("dbg_tc", [64, 512], BF16)
            pr.dram_out("dbg_ts", [64, 512], BF16)
        with pr.scope():
            ph0 = Ph(nc, pr=pr, name="prep")
            emit_ones(pr, ph0)
            emit_identity(pr, ph0)
            ph0.emit(es)
        with pr.scope():
            emit_l0_attn(pr, es, xin, xout)
    return nc


def kernel(**inputs):
    return kernel_fused(**inputs)


def kernel_multilaunch(**inputs):
    n_cores = 8
    shared, per_core = host_prep(inputs, n_cores)
    cores = list(range(n_cores))

    def launch(nc, names, xin_list):
        in_maps = []
        for c in range(n_cores):
            im = {k: shared[k] for k in names}
            im["xin"] = xin_list[c]
            in_maps.append(im)
        res = run_bass_kernel_spmd(nc, in_maps, core_ids=cores)
        return [np.asarray(r["xout"]) for r in res.results]

    x = [pc["x0"] for pc in per_core]
    nc = build_l0_attn_program()
    x = launch(nc, ("wqT", "wkT", "wvT", "woT", "cos0", "sin0", "permT0",
                    "diagT", "qdec", "cos0tm", "sin0tm", "kdecT", "kdecTM"), x)
    nc = build_l0_ffn_program_x()
    x = launch(nc, ("g_idxT", "u_idxT", "d_idxT", "lut"), x)
    nc = build_l1_attn_program()
    x = launch(nc, ("Wk1", "Wv1", "Wg1", "Wp1", "latq", "cos1", "sin1",
                    "permT1", "latmask"), x)
    nc = build_l1_ffn_program_x()
    x = launch(nc, ("gu_codesT", "dn_codesT", "gu_scalesT", "dn_scalesT"), x)

    out = np.empty((16, T, D), np.float32)
    bpc = 16 // n_cores
    for c in range(n_cores):
        out[c * bpc:(c + 1) * bpc] = x[c].reshape(D, NB, T).transpose(1, 2, 0)
    return out


def build_l0_ffn_program_x():
    return build_l0_ffn_program()


def build_l1_ffn_program_x():
    return build_l1_ffn_program()


_FUSED_INPUTS = None


def build_full_program():
    global _FUSED_INPUTS
    nc = bass.Bass("TRN2", target_bir_lowering=False)
    Prog.input_names = []
    with ExitStack() as es:
        pr = Prog(nc, es)
        x0 = pr.dram_in("xin", [D, NT], F32)
        xout = pr.dram_out("xout", [D, NT], F32)
        xA = pr.dram_tmp("xA", [D, NT], F32)
        xB = pr.dram_tmp("xB", [D, NT], F32)
        pr.dram_in("wqT", [H, 128, D], F32)
        pr.dram_in("wkT", [HKV, 128, D], F32)
        pr.dram_in("wvT", [HKV, 128, D], F32)
        pr.dram_in("woT", [DT, 128, D], F32)
        pr.dram_in("cos0", [128, T], F32)
        pr.dram_in("sin0", [128, T], F32)
        pr.dram_in("permT0", [128, 128], BF16)
        pr.dram_in("diagT", [64, H * CH], F32)
        pr.dram_in("qdec", [128, H * CH], F32)
        pr.dram_in("cos0tm", [CH, (T // CH) * 128], F32)
        pr.dram_in("sin0tm", [CH, (T // CH) * 128], F32)
        pr.dram_in("kdecT", [CH, H], F32)
        pr.dram_in("kdecTM", [CH, H * DH], F32)
        g_idxT = pr.dram_in("g_idxT_s", [256, I0], BF16)
        u_idxT = pr.dram_in("u_idxT_s", [256, I0], BF16)
        d_idxT = pr.dram_in("d_idxT_s", [768, D], BF16)
        pr.dram_in("lut", [1, 16], F32)
        gw = pr.dram_tmp("gw", [D, I0], BF16, shared=True)
        uw = pr.dram_tmp("uw", [D, I0], BF16, shared=True)
        dw = pr.dram_tmp("dw", [6144, D], BF16, shared=True)
        gw_s = pr.dram_tmp("gw_s", [256, I0], BF16)
        uw_s = pr.dram_tmp("uw_s", [256, I0], BF16)
        dw_s = pr.dram_tmp("dw_s", [768, D], BF16)
        pr.dram_in("Wk1", [H, 128, D], F32)
        pr.dram_in("Wv1", [H, 128, D], F32)
        pr.dram_in("Wg1", [8, 128, D], F32)
        pr.dram_in("Wp1", [DT, 128, D], F32)
        pr.dram_in("latq", [H, 128, NL], F32)
        pr.dram_in("cos1", [128, 128], F32)
        pr.dram_in("sin1", [128, 128], F32)
        pr.dram_in("permT1", [128, 128], BF16)
        pr.dram_in("latmask", [64, 128], F32)
        gu_codesT = pr.dram_in("gu_codesT_s", [256, GU], BF16)
        dn_codesT = pr.dram_in("dn_codesT_s", [768, D], BF16)
        gu_scalesT = pr.dram_in("gu_scalesT_s", [256, GU], F16)
        dn_scalesT = pr.dram_in("dn_scalesT_s", [768, D], F16)
        guw = pr.dram_tmp("guw", [D, GU], BF16, shared=True)
        dnw = pr.dram_tmp("dnw", [6144, D], BF16, shared=True)
        guw_s = pr.dram_tmp("guw_s", [256, GU], BF16)
        dnw_s = pr.dram_tmp("dnw_s", [768, D], BF16)

        with pr.scope():
            ph0 = Ph(nc, pr=pr, name="prep")
            emit_ones(pr, ph0)
            emit_identity(pr, ph0)
            emit_lut_prep(pr, ph0)
            ph0.emit(es)
        lutB, cdiff = pr.t["lutB"], pr.t["cdiff"]
        with pr.scope():
            ph = Ph(nc, pr=pr, name="dq")
            pr.t["_shared_prep_ph"] = ph
            for tag, src, dst, nr_, nc_ in (
                    ("g", g_idxT, gw_s, 256, I0), ("u", u_idxT, uw_s, 256, I0),
                    ("d", d_idxT, dw_s, 768, D)):
                emit_dequant_ghost(pr, ph, src, dst, nr_, nc_, lutB, cdiff, tag)
            emit_dequant_fp4(pr, ph, gu_codesT, gu_scalesT, guw_s, 256, GU, "gu")
            emit_dequant_fp4(pr, ph, dn_codesT, dn_scalesT, dnw_s, 768, D, "dn")

        def gathers(ph):
            for shard, full in ((gw_s, gw), (uw_s, uw), (dw_s, dw),
                                (guw_s, guw), (dnw_s, dnw)):
                ph.op("gpsimd", lambda e, shard=shard, full=full: e.collective_compute(
                    "AllGather", OP.bypass, replica_groups=[list(range(8))],
                    ins=[shard.ap().opt()], outs=[full.ap().opt()]),
                    reads=[shard], writes=[full], inc=1)

        with pr.scope():
            emit_l0_attn(pr, es, x0, xA, pre_ops=gathers)
        pr.t["_shared_prep_ph"] = None
        with pr.scope():
            ph = Ph(nc, pr=pr, name="ffn0")
            emit_ffn(pr, ph, xA, xB, gw, 0, uw, 0, dw, I0, False, "f0")
            ph.emit(es)
        with pr.scope():
            emit_l1_attn(pr, es, xB, xA)
        with pr.scope():
            ph = Ph(nc, pr=pr, name="ffn1")
            emit_ffn(pr, ph, xA, xout, guw, 0, guw, I1, dnw, I1, True, "f1")
            ph.emit(es)
    _FUSED_INPUTS = set(Prog.input_names)
    Prog.input_names = None
    return nc


def kernel_fused(**inputs):
    n_cores = 8
    shared, per_core = host_prep(inputs, n_cores)
    nc = build_full_program()

    in_maps = []
    for c in range(n_cores):
        im = dict(shared)
        for k, v in per_core[c].items():
            im[k] = v
        im["xin"] = per_core[c]["x0"]
        im = {k: v for k, v in im.items() if _FUSED_INPUTS is None or k in _FUSED_INPUTS}
        in_maps.append(im)
    res = run_bass_kernel_spmd(nc, in_maps, core_ids=list(range(n_cores)))
    out = np.empty((16, T, D), np.float32)
    bpc = 16 // n_cores
    for c in range(n_cores):
        xf = np.asarray(res.results[c]["xout"])
        out[c * bpc:(c + 1) * bpc] = xf.reshape(D, NB, T).transpose(1, 2, 0)
    return out
